# version 23
# baseline (speedup 1.0000x reference)
import numpy as np
import ml_dtypes
from contextlib import ExitStack
import concourse.bass as bass
import concourse.mybir as mybir
from concourse.bass_utils import run_bass_kernel_spmd

F32 = mybir.dt.float32
BF16 = mybir.dt.bfloat16
I8 = mybir.dt.int8
AF = mybir.ActivationFunctionType
ALU = mybir.AluOpType
AX = mybir.AxisListType

NTOK = 8192
OWN = 2048
NT_ALL = 64
NT_OWN = 16
D = 1024
HID = 4096
NH = 16
ISCALE2 = float(2.0 ** -24)

ENGS = ("sy", "gp", "sc", "ve", "te")

PHASE_LIMIT = 6
B2SUB = 2
SERIAL = False

ARENA_BYTES = 65536
CARVES = {
    "wstage":  (1, 2, 0,     16384),
    "xnb":     (2, 2, 16384, 4096),
    "xbb":     (1, 1, 20480, 32768),
    "wdump":   (1, 1, 53248, 8192),
    "fbuf":    (2, 2, 20480, 4096),
    "fbuf4":   (3, 3, 16384, 8192),
    "mrow_f":  (2, 4, 32768, 4096),
    "mrow_b":  (2, 4, 36864, 2048),
    "lngrow":  (0, 0, 20480, 2048),
    "lnbrow":  (0, 0, 22528, 2048),
    "nw2row":  (0, 0, 38912, 16384),
    "nw1row":  (0, 0, 55296, 4096),
    "borow":   (0, 0, 59392, 4096),
    "bqrow":   (0, 0, 63488, 2048),
    "t1s":     (2, 2, 38912, 4096),
    "e1s":     (2, 2, 43008, 4096),
    "xnT":     (2, 2, 51200, 2048),
    "q0s":     (2, 2, 53248, 1024),
    "sAs":     (2, 2, 54272, 2048),
    "sBs":     (2, 2, 56320, 2048),
    "adjs":    (2, 2, 58368, 2048),
    "xq8":     (2, 2, 60416, 1024),
    "qtok":    (2, 2, 61440, 2048),
    "qtile":   (2, 2, 63488, 2048),
    "wsB":     (2, 2, 24576, 4096),
    "wtrn":    (2, 2, 28672, 2048),
    "wkv":     (4, 4, 0,     16384),
    "t_bf":    (4, 4, 16384, 16384),
    "states":  (4, 4, 38912, 8192),
    "wrow_f":  (4, 4, 47104, 4096),
    "wrow_b":  (4, 4, 51200, 2048),
    "bkbv":    (4, 4, 53248, 8192),
    "bvrow":   (4, 4, 61440, 4096),
    "t2f":     (5, 5, 0,     16384),
    "u0":      (5, 5, 16384, 16384),
    "e2h":     (5, 5, 32768, 8192),
    "sA2":     (5, 5, 40960, 4096),
    "sB2":     (5, 5, 45056, 2048),
    "sB2b":    (5, 5, 63488, 2048),
    "q0b":     (5, 5, 47104, 4096),
    "x2b":     (5, 5, 51200, 4096),
    "xqb2":    (5, 5, 55296, 4096),
    "xqT":     (5, 5, 59392, 4096),
    "x2bf2":   (6, 6, 0,     16384),
    "x2T":     (6, 6, 16384, 8192),
    "qtb":     (6, 6, 24576, 4096),
    "ys":      (6, 6, 28672, 4096),
    "th":      (6, 6, 32768, 4096),
    "gh":      (6, 6, 36864, 4096),
    "g_t":     (6, 6, 40960, 4096),
    "outa":    (6, 6, 45056, 4096),
    "outn":    (6, 6, 49152, 2048),
    "outnT":   (6, 6, 51200, 2048),
    "s_s":     (6, 6, 53248, 4096),
    "sg":      (6, 6, 57344, 4096),
    "res":     (6, 6, 61440, 4096),
}


def _check_carves():
    items = list(CARVES.items())
    for i in range(len(items)):
        n1, (l1, h1, o1, b1) = items[i]
        assert o1 + b1 <= ARENA_BYTES, n1
        for j in range(i + 1, len(items)):
            n2, (l2, h2, o2, b2) = items[j]
            if max(l1, l2) <= min(h1, h2):
                assert o1 + b1 <= o2 or o2 + b2 <= o1, f"arena overlap {n1} {n2}"


_check_carves()


class Emitter:

    def __init__(self):
        self.mode = "count"
        self.cnt = {k: 0 for k in ENGS}
        self.marks = {}
        self.cur_eng = None
        self.handles = None
        self.sems = None
        self.auto = 0
        self.serial = False
        self.auto_s = 0
        self.prev_mark = None

    def op(self, eng, fn=None, mark=None, waits=(), dma=False, sem=None):
        key = sem if sem is not None else eng
        if dma and mark is None:
            mark = f"_dma{self.auto}"
            self.auto += 1
        if self.serial and fn is not None:
            if mark is None:
                mark = f"_ser{self.auto_s}"
                self.auto_s += 1
            if self.prev_mark is not None:
                waits = tuple(waits) + (self.prev_mark,)
            self.prev_mark = mark
        if self.mode == "count":
            if mark is not None:
                assert mark not in self.marks, f"dup mark {mark}"
                self.cnt[key] = self.cnt.get(key, 0) + (16 if dma else 1)
                self.marks[mark] = (key, self.cnt[key])
            return
        if eng != self.cur_eng:
            return
        e = self.handles
        last = None
        for w in waits:
            sem_key, val = self.marks[w]
            if last != (sem_key, val):
                e.wait_ge(self.sems[sem_key], val)
            last = (sem_key, val)
        if fn is not None:
            inst = fn(e)
            if mark is not None:
                inst.then_inc(self.sems[key], 16 if dma else 1)


def build(debug=False, detect_races=False):
    nc = bass.Bass("TRN2", target_bir_lowering=False,
                   detect_race_conditions=detect_races)

    x_d = nc.dram_tensor("x", [NTOK, D], F32, kind="ExternalInput").ap()
    wq_d = nc.dram_tensor("wq", [D, D], F32, kind="ExternalInput").ap()
    wk_d = nc.dram_tensor("wk", [2, D, D], F32, kind="ExternalInput").ap()
    wv_d = nc.dram_tensor("wv", [2, D, D], F32, kind="ExternalInput").ap()
    wo_d = nc.dram_tensor("wo", [D, D], F32, kind="ExternalInput").ap()
    bq_d = nc.dram_tensor("bq", [1, D], F32, kind="ExternalInput").ap()
    bk_d = nc.dram_tensor("bk", [2, D], F32, kind="ExternalInput").ap()
    bv_d = nc.dram_tensor("bv", [2, D], F32, kind="ExternalInput").ap()
    bo_d = nc.dram_tensor("bo", [1, D], F32, kind="ExternalInput").ap()
    lng_d = nc.dram_tensor("lng", [1, D], F32, kind="ExternalInput").ap()
    lnb_d = nc.dram_tensor("lnb", [1, D], F32, kind="ExternalInput").ap()
    nw1_d = nc.dram_tensor("nw1", [1, D], F32, kind="ExternalInput").ap()
    nw2_d = nc.dram_tensor("nw2", [1, HID], F32, kind="ExternalInput").ap()
    acts_d = nc.dram_tensor("acts", [1, 1], F32, kind="ExternalInput").ap()
    actb_d = nc.dram_tensor("actb", [1, 1], F32, kind="ExternalInput").ap()
    w1t_d = nc.dram_tensor("w1t", [D, HID], F32, kind="ExternalInput").ap()
    w2t_d = nc.dram_tensor("w2t", [HID, D], F32, kind="ExternalInput").ap()
    out_d = nc.dram_tensor("out", [OWN, D], F32, kind="ExternalOutput").ap()

    ikind = "ExternalOutput" if debug else "Internal"
    xnbf_d = nc.dram_tensor("xnbf", [NTOK, D], BF16, kind=ikind).ap()
    xq_d = nc.dram_tensor("xqd", [OWN, D], I8, kind=ikind).ap()
    qt_d = nc.dram_tensor("qtd", [128, 8 * OWN], BF16, kind=ikind).ap()
    x2_d = nc.dram_tensor("x2d", [OWN, HID], I8, kind=ikind).ap()
    w1q_d = nc.dram_tensor("w1qd", [D, HID], I8, kind=ikind).ap()
    w2q_d = nc.dram_tensor("w2qd", [HID, D], I8, kind=ikind).ap()
    mrow_d = nc.dram_tensor("mrowd", [1, D], BF16, kind=ikind).ap()

    dbg = {}
    if debug:
        for name, shape, dt in [
            ("dbg_G", [128, 8 * D], BF16), ("dbg_m", [1, D], F32),
            ("dbg_stp", [64, D], F32),
            ("dbg_r", [NT_OWN * 128, HID], F32),
            ("dbg_y", [NT_OWN * 128, D], F32),
            ("dbg_oa", [NT_OWN * 128, D], F32),
            ("dbg_g", [NT_OWN * 128, D], F32),
            ("dbg_st2", [128, 1024], BF16),
        ]:
            dbg[name] = nc.dram_tensor(name, shape, dt, kind="ExternalOutput").ap()

    id_dram = nc.inline_tensor(np.eye(128, dtype=ml_dtypes.bfloat16), name="identbf")
    ones_dram = nc.inline_tensor(np.ones((1, 128), dtype=ml_dtypes.bfloat16), name="onesbf")
    onesf_dram = nc.inline_tensor(np.ones((1, 128), dtype=np.float32), name="onesf")
    onesbc_dram = nc.inline_tensor(np.ones((128, 1), dtype=ml_dtypes.bfloat16), name="onesbc")
    onesfc_dram = nc.inline_tensor(np.ones((128, 1), dtype=np.float32), name="onesfc")

    (C_A, C_B, C_BOA, C_EPSLN, C_EPSRMS, C_TH1, C_TH2, C_HALF, C_NHALF,
     C_S1W, C_S2W, C_NTH1, C_NTH2, C_EPSQ, C_A2, C_AB2, C_EB, C_S2F,
     C_S1F, C_T0) = range(20)

    em = Emitter()
    es = ExitStack()
    with es:
        def T(name, shape, dtype):
            return es.enter_context(nc.sbuf_tensor(name, shape, dtype))

        id_sb = T("id_sb", [128, 128], BF16)
        ones_sb = T("ones_sb", [1, 128], BF16)
        onesf_sb = T("onesf_sb", [1, 128], F32)
        onesbc_sb = T("onesbc_sb", [128, 1], BF16)
        onesfc_sb = T("onesfc_sb", [128, 1], F32)
        scal_f = T("scal_f", [1, 16], F32)
        cols_f = T("cols_f", [128, 20], F32)
        rq1_cols = T("rq1_cols", [128, NT_OWN], F32)
        scol_cols = T("scol_cols", [128, NT_OWN], F32)
        c1_cols = T("c1_cols", [128, NT_OWN], F32)
        ysc_cols = T("ysc_cols", [128, NT_OWN], F32)
        lncols = T("lncols", [128, 12], F32)
        qcols = T("qcols", [128, 8], F32)
        wstat = T("wstat", [128, 32], F32)
        stat_s = T("stat_s", [128, NT_ALL], F32)
        stat_r = T("stat_r", [128, NT_ALL], F32)
        mean_all = T("mean_all", [128, NT_ALL], F32)
        var_all = T("var_all", [128, NT_ALL], F32)
        rstd_all = T("rstd_all", [128, NT_ALL], F32)
        negm_all = T("negm_all", [128, NT_ALL], F32)
        scr8 = T("scr8", [128, 8], F32)
        rsq16 = T("rsq16", [128, NT_OWN], F32)
        nw1b_tile = T("nw1b", [128, D], BF16)
        mcol_b = T("mcol_b", [128, 8], BF16)
        statet_b = T("statet_b", [64, D], BF16)
        statet2 = T("statet2", [128, 1024], BF16)
        bq_tile = T("bq_tile", [128, D], BF16)
        lng_tile = T("lng_tile", [128, D], BF16)
        lnb_tile = T("lnb_tile", [128, D], BF16)
        bo_tile = T("bo_tile", [128, D], F32)
        nw1_tile = T("nw1_tile", [128, D], F32)
        anw2_tile = T("anw2_tile", [128, HID], F32)
        xbuf = T("xbuf", [128, 4 * D], F32)
        sqscr = T("sqscr", [128, 2048], F32)
        wq_sb = T("wq_sb", [128, 8 * D], BF16)
        w1q_sb = T("w1q_sb", [128, 8 * HID], BF16)
        arena = T("arena", [128, ARENA_BYTES // 2], BF16)

        STREAMS = ("sy", "gp", "sc", "ve", "te", "syp", "syo", "gpl",
                   "gpt", "gpm", "gpd", "gpp",
                   "syx0", "syx1", "syf0", "syf1", "syw0", "syw1",
                   "sy20", "sy21", "syq0", "syq1", "gpn0", "gpn1",
                   "gpw0", "gpw1", "gpxq0", "gpxq1", "gpqt0", "gpqt1",
                   "gpx20", "gpx21", "gdy", "gdo", "gdg", "gdr",
                   "gpb0", "gpb1", "syx2", "syx3", "syf2", "syf3")
        _sems = {k: es.enter_context(nc.semaphore("sem_" + k)) for k in STREAMS}
        PS = es.enter_context(nc.psum_tensor("PS", [128, 4096], F32))
        blk = es.enter_context(nc.Block())

        def carve(name, rows, cols, dtype):
            lo, hi, off, nbytes = CARVES[name]
            esz = {F32: 4, BF16: 2, I8: 1}[dtype]
            assert cols * esz <= nbytes, (name, cols, esz, nbytes)
            v = arena[:, off // 2:(off + cols * esz) // 2]
            v = v.bitcast(dtype)
            return v[0:rows, :]

        wstage = carve("wstage", 128, 4096, F32)
        xnb = carve("xnb", 128, 2 * D, BF16)
        xbb = carve("xbb", 128, 16 * D, BF16)
        wdump = carve("wdump", 128, 2048, F32)
        fbuf = carve("fbuf", 128, 2 * D, BF16)
        fbuf4 = carve("fbuf4", 128, 4 * D, BF16)
        mrow_f = carve("mrow_f", 1, D, F32)
        mrow_b = carve("mrow_b", 1, D, BF16)
        lngrow = carve("lngrow", 1, D, BF16)
        lnbrow = carve("lnbrow", 1, D, BF16)
        nw2row = carve("nw2row", 1, HID, F32)
        nw1row = carve("nw1row", 1, D, F32)
        borow = carve("borow", 1, D, F32)
        bqrow = carve("bqrow", 1, D, BF16)
        t1b = carve("t1s", 128, D, BF16)
        e1s = carve("e1s", 128, D, F32)
        xnT = carve("xnT", 128, D, BF16)
        q0s = carve("q0s", 128, D, I8)
        sAs = carve("sAs", 128, D, BF16)
        sBs = carve("sBs", 128, D, BF16)
        adjs = carve("adjs", 128, D, BF16)
        xq8 = carve("xq8", 128, D, I8)
        qtok = carve("qtok", 128, D, BF16)
        qtile = carve("qtile", 128, D, BF16)
        wsB = carve("wsB", 128, 2048, BF16)
        wtrn = carve("wtrn", 128, 2048, I8)
        wkv_sb = carve("wkv", 128, 8 * D, BF16)
        t_bf = carve("t_bf", 128, 8 * D, BF16)
        states_sb = carve("states", 64, 2 * D, F32)
        wrow_f = carve("wrow_f", 1, D, F32)
        wrow_b = carve("wrow_b", 1, D, BF16)
        bkbv_b = carve("bkbv", 1, 4 * D, BF16)
        bvrow = carve("bvrow", 1, D, F32)
        t2f_sb = carve("t2f", 128, HID, F32)
        u0_sb = carve("u0", 128, HID, F32)
        e2h_sb = carve("e2h", 128, 2048, F32)
        sA2_sb = carve("sA2", 128, 2 * D, BF16)
        sB2_sb = carve("sB2", 128, D, BF16)
        sB2b_sb = carve("sB2b", 128, D, BF16)
        q0b_sb = carve("q0b", 128, HID, I8)
        x2b_sb = carve("x2b", 128, HID, I8)
        xqb2 = carve("xqb2", 128, 2 * D, BF16)
        xqT_sb = carve("xqT", 128, 2 * D, BF16)
        x2bf2 = carve("x2bf2", 128, 2 * HID, BF16)
        x2T_sb = carve("x2T", 128, HID, BF16)
        qtb = carve("qtb", 128, 2 * D, BF16)
        y_sb = carve("ys", 128, D, F32)
        th_sb = carve("th", 128, D, F32)
        gh_sb = carve("gh", 128, D, F32)
        g_t = carve("g_t", 128, D, F32)
        outa_sb = carve("outa", 128, D, F32)
        outn_sb = carve("outn", 128, D, BF16)
        outnT_sb = carve("outnT", 128, D, BF16)
        s_sb = carve("s_s", 128, D, F32)
        sg_sb = carve("sg", 128, D, F32)
        res_f = carve("res", 128, D, F32)

        wsA = sqscr[:, :].bitcast(BF16)[:, 0:2048]
        g_bf = w1q_sb[:, 0:8 * D]
        r_sb = wq_sb[:, 0:8 * D].bitcast(F32)
        w2q_sb = w1q_sb[:, 0:8 * HID]
        wo_sb = wq_sb[:, 0:8 * D]
        PT = PS[:, 3584:4096].bitcast(BF16)
        PT2 = PS[:, 3072:4096].bitcast(BF16)
        PTo = PS[:, 1024:2048].bitcast(BF16)

        def prog():
            E = em.op

            def de(*ws):
                return ws if detect_races else ()

            p0loads = [
                (id_sb[:, :], id_dram.ap()[:, :]),
                (ones_sb[:, :], ones_dram.ap()[:, :]),
                (onesf_sb[:, :], onesf_dram.ap()[:, :]),
                (onesbc_sb[:, :], onesbc_dram.ap()[:, :]),
                (onesfc_sb[:, :], onesfc_dram.ap()[:, :]),
                (nw1row[:, :], nw1_d[:, :]),
                (borow[:, :], bo_d[:, :]),
                (nw2row[:, :], nw2_d[:, :]),
                (scal_f[0:1, 0:1], acts_d[:, :]),
                (scal_f[0:1, 1:2], actb_d[:, :]),
            ]
            for i, (dst, src) in enumerate(p0loads):
                E("sy", lambda e, d=dst, s=src: e.dma_start(out=d, in_=s),
                  mark="p0ld" if i == len(p0loads) - 1 else None, dma=True,
                  sem="syp")
            p0casts = [
                (bqrow[:, :], bq_d[:, :]),
                (lngrow[:, :], lng_d[:, :]),
                (lnbrow[:, :], lnb_d[:, :]),
            ]
            for i, (dst, src) in enumerate(p0casts):
                E("gp", lambda e, d=dst, s=src: e.dma_start(out=d, in_=s),
                  mark="p0g" if i == len(p0casts) - 1 else None, dma=True,
                  sem="gpl")
            wq_view = wq_d.rearrange("(c p) j -> p c j", p=128)
            E("gp", lambda e: e.dma_start(
                out=wq_sb[:, :].rearrange("p (c j) -> p c j", c=8), in_=wq_view),
              mark="wqld", waits=("p0g",), dma=True, sem="gpl")

            def k1mm(e, outsl, lhsT, rhs, st):
                return e.matmul(outsl, lhsT, rhs, start=st, stop=True)

            for i, (row, dst) in enumerate([(bqrow, bq_tile), (lngrow, lng_tile),
                                            (lnbrow, lnb_tile)]):
                for h in range(2):
                    E("te", lambda e, i=i, h=h, row=row: k1mm(
                        e, PS[:, i * 1024 + h * 512:i * 1024 + (h + 1) * 512],
                        ones_sb[0:1, :], row[0:1, h * 512:(h + 1) * 512], st=True),
                      mark=f"bcmm{i}_{h}",
                      waits=("p0ld", "p0g") if (i == 0 and h == 0) else ())
                E("ve", lambda e, i=i, dst=dst: e.tensor_copy(
                    dst[:, :], PS[:, i * 1024:(i + 1) * 1024]),
                  mark=f"bcev{i}", waits=(f"bcmm{i}_0", f"bcmm{i}_1"))
            E("te", lambda e: k1mm(e, PS[:, 3072:3584], onesf_sb[0:1, :],
                                   nw1row[0:1, 0:512], True),
              mark="nw1a", waits=("p0ld",))
            E("te", lambda e: k1mm(e, PS[:, 0:512], onesf_sb[0:1, :],
                                   nw1row[0:1, 512:1024], True),
              mark="nw1b", waits=("bcev0",))
            E("ve", lambda e: e.tensor_copy(nw1_tile[:, 0:512], PS[:, 3072:3584]),
              mark="nw1ev1", waits=("nw1a",))
            E("ve", lambda e: e.tensor_copy(nw1_tile[:, 512:1024], PS[:, 0:512]),
              mark="nw1ev2", waits=("nw1b",))
            E("ve", lambda e: e.tensor_copy(nw1b_tile[:, :], nw1_tile[:, :]),
              mark="nw1bf", waits=("nw1ev2",))
            E("te", lambda e: k1mm(e, PS[:, 512:1024], onesf_sb[0:1, :],
                                   borow[0:1, 0:512], True),
              mark="boa", waits=("bcev0",))
            E("te", lambda e: k1mm(e, PS[:, 1024:1536], onesf_sb[0:1, :],
                                   borow[0:1, 512:1024], True),
              mark="bob", waits=("bcev1",))
            E("ve", lambda e: e.tensor_copy(bo_tile[:, 0:512], PS[:, 512:1024]),
              mark="boev1", waits=("boa",))
            E("ve", lambda e: e.tensor_copy(bo_tile[:, 512:1024], PS[:, 1024:1536]),
              mark="boev2", waits=("bob",))
            E("te", lambda e: k1mm(e, PS[:, 2560:2561], onesf_sb[0:1, :],
                                   scal_f[0:1, 0:1], True),
              mark="acol", waits=("bcev2", "p0ld"))
            E("te", lambda e: k1mm(e, PS[:, 2561:2562], onesf_sb[0:1, :],
                                   scal_f[0:1, 1:2], True), mark="bcol")
            E("ve", lambda e: e.tensor_copy(cols_f[:, C_A:C_B + 1],
                                            PS[:, 2560:2562]),
              mark="abev", waits=("acol", "bcol"))
            E("ve", lambda e: e.reciprocal(cols_f[:, C_BOA:C_BOA + 1],
                                           cols_f[:, C_A:C_A + 1]),
              mark="rcp0", waits=("abev",))
            E("ve", lambda e: e.tensor_tensor(cols_f[:, C_BOA:C_BOA + 1],
                                              cols_f[:, C_BOA:C_BOA + 1],
                                              cols_f[:, C_B:C_B + 1], ALU.mult),
              mark="boacol", waits=("rcp0",))
            E("ve", lambda e: e.tensor_tensor(
                cols_f[:, C_A2:C_A2 + 1], cols_f[:, C_A:C_A + 1],
                cols_f[:, C_A:C_A + 1], ALU.mult),
              mark="a2c", waits=("abev",))
            E("ve", lambda e: e.tensor_tensor(
                cols_f[:, C_AB2:C_AB2 + 1], cols_f[:, C_A:C_A + 1],
                cols_f[:, C_B:C_B + 1], ALU.mult),
              mark="ab2a")
            E("ve", lambda e: e.tensor_scalar(
                cols_f[:, C_AB2:C_AB2 + 1], cols_f[:, C_AB2:C_AB2 + 1],
                2.0, None, ALU.mult),
              mark="ab2c", waits=("ab2a",))
            E("ve", lambda e: e.tensor_tensor(
                cols_f[:, C_EB:C_EB + 1], cols_f[:, C_B:C_B + 1],
                cols_f[:, C_B:C_B + 1], ALU.mult),
              mark="eba")
            E("ve", lambda e: e.tensor_scalar(
                cols_f[:, C_EB:C_EB + 1], cols_f[:, C_EB:C_EB + 1],
                1e-8, None, ALU.add),
              mark="ebc", waits=("eba",))
            for cslot, val in [(C_EPSLN, 1e-5), (C_EPSRMS, 1e-8),
                               (C_HALF, 0.5), (C_NHALF, -0.5), (C_EPSQ, 1e-5)]:
                E("ve", lambda e, c=cslot, v=val: e.memset(cols_f[:, c:c + 1], v),
                  mark=f"ms{cslot}")
            for c in range(8):
                E("ve", lambda e, c=c: e.tensor_tensor(
                    wq_sb[:, c * 1024 + c * 128:c * 1024 + (c + 1) * 128],
                    wq_sb[:, c * 1024 + c * 128:c * 1024 + (c + 1) * 128],
                    id_sb[:, :], ALU.add),
                  waits=("wqld", "p0ld") if c == 0 else (),
                  mark="wqf" if c == 7 else None)
            for h in range(8):
                E("te", lambda e, h=h: k1mm(
                    e, PS[:, 3072:3584], onesf_sb[0:1, :],
                    nw2row[0:1, h * 512:(h + 1) * 512], True),
                  mark=f"nw2mm{h}",
                  waits=("nw1ev1",) if h == 0 else (f"nw2ev{h-1}",))
                E("ve", lambda e, h=h: e.tensor_scalar(
                    anw2_tile[:, h * 512:(h + 1) * 512], PS[:, 3072:3584],
                    cols_f[:, C_A:C_A + 1], None, ALU.mult),
                  mark=f"nw2ev{h}", waits=(f"nw2mm{h}", "abev"))

            if PHASE_LIMIT < 1:
                E("ve", fn=None, waits=("nw2ev7", "boacol"))
                return
            NG = NT_ALL // 8
            for g in range(NG):
                for j in range(8):
                    t = g * 8 + j
                    b = t % 4
                    w_ld = [f"sq{t-4}", f"cst{t-4}"] if t >= 4 else ["p0ld"]
                    E("sy" if t % 2 == 0 else "sc",
                      lambda e, t=t, b=b: e.dma_start(
                        out=xbuf[:, b * D:(b + 1) * D],
                        in_=x_d[t * 128:(t + 1) * 128, :]),
                      mark=f"xld{t}", waits=w_ld, dma=True, sem=f"syx{t%4}")
                    E("ve", lambda e, t=t, b=b: e.tensor_tensor(
                        sqscr[:, (b % 2) * D:(b % 2 + 1) * D],
                        xbuf[:, b * D:(b + 1) * D],
                        xbuf[:, b * D:(b + 1) * D], ALU.mult),
                      mark=f"sqm{t}",
                      waits=(f"xld{t}",) +
                      (de(f"sq{t-2}") if t >= 2 else ()))
                    E("ve", lambda e, t=t, b=b: e.reduce_sum(
                        stat_r[:, t:t + 1], xbuf[:, b * D:(b + 1) * D], AX.X),
                      mark=f"rd{t}", waits=(f"xld{t}",))
                    E("ve", lambda e, t=t, b=b: e.reduce_sum(
                        stat_s[:, t:t + 1],
                        sqscr[:, (b % 2) * D:(b % 2 + 1) * D], AX.X),
                      mark=f"sq{t}", waits=(f"sqm{t}",))
                    E("ve", lambda e, t=t, b=b: e.tensor_copy(
                        xbb[:, (t % 16) * D:(t % 16 + 1) * D],
                        xbuf[:, b * D:(b + 1) * D]),
                      mark=f"cst{t}",
                      waits=(f"rd{t}",) +
                      (("nw2mm7",) if t < 2 else ()) +
                      ((f"gram{t-16}", f"xnst{t-16}") if t >= 16 else ()))
                for i in range(4 * g, 4 * g + 4):
                    if i < 16:
                        wsrc = w1t_d[(i // 2) * 128:(i // 2 + 1) * 128,
                                     (i % 2) * 2048:(i % 2 + 1) * 2048]
                        wdst0 = wstage[:, (i % 2) * 2048:(i % 2 + 1) * 2048]
                    else:
                        wsrc = w2t_d.rearrange("(a p) d -> p a d", p=128)[
                            :, (i - 16) * 2:(i - 16) * 2 + 2, :]
                        wdst0 = wstage[:, (i % 2) * 2048:(i % 2 + 1) * 2048
                                       ].rearrange("p (a d) -> p a d", a=2)
                    E("gp", lambda e, w=wsrc, d0=wdst0: e.dma_start(
                        out=d0, in_=w),
                      mark=f"wld{i}",
                      waits=(f"wstat{i-2}",) if i >= 2 else (),
                      dma=True, sem=f"gpw{i%2}")
                    E("sc", lambda e, i=i: e.activation(
                        wdump[:, :],
                        wstage[:, (i % 2) * 2048:(i % 2 + 1) * 2048],
                        AF.Abs, accum_out=wstat[:, i:i + 1]),
                      mark=f"wstat{i}",
                      waits=(f"wld{i}",) +
                      (de(f"wstat{i-1}") if i >= 1 else ("nw2mm7",)))
                sl = slice(g * 8, g * 8 + 8)
                E("ve", lambda e, sl=sl: e.tensor_scalar(
                    mean_all[:, sl], stat_r[:, sl], 1.0 / D, None, ALU.mult),
                  mark=f"ga{g}", waits=(f"rd{g*8+7}",))
                E("ve", lambda e, sl=sl: e.tensor_scalar(
                    var_all[:, sl], stat_s[:, sl], 1.0 / D, None, ALU.mult),
                  mark=f"gb{g}", waits=(f"sq{g*8+7}",))
                E("ve", lambda e, sl=sl: e.tensor_tensor(
                    scr8[:, :], mean_all[:, sl], mean_all[:, sl], ALU.mult),
                  mark=f"gc{g}", waits=(f"ga{g}",))
                E("ve", lambda e, sl=sl: e.tensor_tensor(
                    var_all[:, sl], var_all[:, sl], scr8[:, :], ALU.subtract),
                  mark=f"gd{g}", waits=(f"gb{g}", f"gc{g}"))
                E("sc", lambda e, sl=sl: e.activation(
                    rstd_all[:, sl], var_all[:, sl], AF.Ln,
                    bias=cols_f[:, C_EPSLN:C_EPSLN + 1]),
                  mark=f"ge{g}", waits=(f"gd{g}",))
                E("sc", lambda e, sl=sl: e.activation(
                    rstd_all[:, sl], rstd_all[:, sl], AF.Exp, scale=-0.5),
                  mark=f"gf{g}", waits=de(f"ge{g}"))
                E("ve", lambda e, sl=sl: e.scalar_tensor_tensor(
                    negm_all[:, sl], mean_all[:, sl], -1.0, rstd_all[:, sl],
                    ALU.mult, ALU.mult),
                  mark=f"cmg{g}", waits=(f"gf{g}",))
                for j in range(8):
                    t = g * 8 + j
                    s16 = (t % 16) * D
                    E("ve", lambda e, t=t, s16=s16: e.tensor_scalar(
                        xbb[:, s16:s16 + D], xbb[:, s16:s16 + D],
                        rstd_all[:, t:t + 1], negm_all[:, t:t + 1],
                        ALU.mult, ALU.add),
                      mark=f"nx{t}", waits=(f"cst{t}", f"cmg{g}"))
                    for c in range(4):
                        for h in range(2):
                            E("te", lambda e, t=t, s16=s16, c=c, h=h: e.matmul(
                                PS[:, (2 * c + h) * 512:(2 * c + h + 1) * 512],
                                xbb[:, s16 + c * 128:s16 + (c + 1) * 128],
                                xbb[:, s16 + h * 512:s16 + (h + 1) * 512],
                                start=(t == 0), stop=(t == NT_ALL - 1)),
                              mark=f"gram{t}" if (c == 3 and h == 1) else None,
                              waits=((f"nx{t}",) +
                                     (("nw2ev7",) if t == 0 else ()))
                              if (c == 0 and h == 0) else ())
                    E("sy" if t % 2 == 0 else "sc",
                      lambda e, t=t, s16=s16: e.dma_start(
                        out=xnbf_d[t * 128:(t + 1) * 128, :],
                        in_=xbb[:, s16:s16 + D]),
                      mark=f"xnst{t}",
                      waits=(f"nx{t}",) +
                      ((f"xnst{t-2}",) if t >= 2 else ()),
                      dma=True, sem=f"gpn{t%2}")

            for c in range(4):
                E("ve", lambda e, c=c: e.tensor_copy(
                    g_bf[:, c * 1024:(c + 1) * 1024],
                    PS[:, c * 1024:(c + 1) * 1024]),
                  mark=f"gh1ev{c}", waits=(f"gram{NT_ALL-1}",) if c == 0 else ())

            E("ve", lambda e: e.reduce_sum(qcols[:, 0:1], wstat[:, 0:16], AX.X),
              waits=("wstat15",), mark="w1red")
            E("ve", lambda e: e.reduce_sum(qcols[:, 1:2], wstat[:, 16:32], AX.X),
              waits=("wstat31",), mark="w2red")
            E("te", lambda e: e.matmul(PS[0:1, 2048:2049], onesfc_sb[:, 0:1],
                                       qcols[:, 0:1], start=True, stop=True),
              waits=("w1red", "gh1ev3"), mark="w1s")
            E("te", lambda e: e.matmul(PS[0:1, 2049:2050], onesfc_sb[:, 0:1],
                                       qcols[:, 1:2], start=True, stop=True),
              waits=("w2red",), mark="w2s")
            E("ve", lambda e: e.tensor_copy(scal_f[0:1, 2:4], PS[0:1, 2048:2050]),
              waits=("w1s", "w2s"), mark="wsev")
            E("ve", lambda e: e.tensor_scalar(scal_f[0:1, 2:4], scal_f[0:1, 2:4],
                                              1.0 / (D * HID), 1e-5, ALU.mult,
                                              ALU.max), mark="wclip",
              waits=("wsev",))
            E("te", lambda e: e.matmul(PS[:, 2048:2049], onesf_sb[0:1, :],
                                       scal_f[0:1, 2:3], start=True, stop=True),
              waits=("wsev", "wclip"), mark="s1b")
            E("te", lambda e: e.matmul(PS[:, 2049:2050], onesf_sb[0:1, :],
                                       scal_f[0:1, 3:4], start=True, stop=True),
              mark="s2b")
            E("ve", lambda e: e.tensor_copy(cols_f[:, C_S1W:C_S2W + 1],
                                            PS[:, 2048:2050]),
              waits=("s1b", "s2b"), mark="swcols")
            E("ve", lambda e: e.tensor_scalar(cols_f[:, C_S2F:C_S2F + 1],
                                              cols_f[:, C_S2W:C_S2W + 1],
                                              0.5 / 127.0, None, ALU.mult),
              mark="s2fc", waits=("swcols",))
            E("ve", lambda e: e.tensor_scalar(cols_f[:, C_S1F:C_S1F + 1],
                                              cols_f[:, C_S1W:C_S1W + 1],
                                              0.5, None, ALU.mult),
              mark="s1fc")
            E("ve", lambda e: e.tensor_scalar(cols_f[:, C_TH1:C_TH1 + 1],
                                              cols_f[:, C_S1W:C_S1W + 1],
                                              0.5, None, ALU.mult),
              mark="th1c", waits=("swcols",))
            E("ve", lambda e: e.tensor_scalar(cols_f[:, C_TH2:C_TH2 + 1],
                                              cols_f[:, C_S2W:C_S2W + 1],
                                              0.5, None, ALU.mult),
              mark="th2c")
            E("ve", lambda e: e.tensor_scalar(cols_f[:, C_NTH1:C_NTH1 + 1],
                                              cols_f[:, C_TH1:C_TH1 + 1],
                                              -1.0, None, ALU.mult),
              waits=("th1c",))
            E("ve", lambda e: e.tensor_scalar(cols_f[:, C_NTH2:C_NTH2 + 1],
                                              cols_f[:, C_TH2:C_TH2 + 1],
                                              -1.0, None, ALU.mult),
              mark="thcols", waits=("th2c",))

            if PHASE_LIMIT < 2:
                E("ve", fn=None, waits=("thcols", f"gram{NT_ALL-1}",
                                        f"xnst{NT_ALL-2}", f"xnst{NT_ALL-1}",
                                        "wstat31"))
                return
            E("ve", lambda e: e.tensor_scalar(
                mean_all[:, 0:NT_OWN], var_all[:, 0:NT_OWN], 1e-5, None,
                ALU.add),
              mark="v2a", waits=("cmg1",))
            E("ve", lambda e: e.reciprocal(
                negm_all[:, 0:NT_OWN], mean_all[:, 0:NT_OWN]),
              mark="v2b", waits=("v2a", f"nx{NT_OWN-1}"))
            E("ve", lambda e: e.tensor_tensor(
                mean_all[:, 0:NT_OWN], var_all[:, 0:NT_OWN],
                negm_all[:, 0:NT_OWN], ALU.mult),
              mark="v2c", waits=("v2b",))
            E("sc", lambda e: e.activation(
                rsq16[:, :], mean_all[:, 0:NT_OWN], AF.Ln,
                bias=cols_f[:, C_EPSRMS:C_EPSRMS + 1]),
              mark="v2d", waits=("v2c",))
            E("sc", lambda e: e.activation(
                rsq16[:, :], rsq16[:, :], AF.Exp, scale=-0.5),
              mark="v2e", waits=de("v2d"))
            def tern_chunk(i):
                nth = C_NTH1 if i < 16 else C_NTH2
                pth = C_TH1 if i < 16 else C_TH2
                if i < 16:
                    wsrc = w1t_d[(i // 2) * 128:(i // 2 + 1) * 128,
                                 (i % 2) * 2048:(i % 2 + 1) * 2048]
                    wdst = w1q_d[(i // 2) * 128:(i // 2 + 1) * 128,
                                 (i % 2) * 2048:(i % 2 + 1) * 2048]
                    wsg0 = wstage[:, (i % 2) * 2048:(i % 2 + 1) * 2048]
                    wtr0 = wtrn[:, :]
                else:
                    wsrc = w2t_d.rearrange("(a p) d -> p a d", p=128)[
                        :, (i - 16) * 2:(i - 16) * 2 + 2, :]
                    wdst = w2q_d.rearrange("(a p) d -> p a d", p=128)[
                        :, (i - 16) * 2:(i - 16) * 2 + 2, :]
                    wsg0 = wstage[:, (i % 2) * 2048:(i % 2 + 1) * 2048
                                  ].rearrange("p (a d) -> p a d", a=2)
                    wtr0 = wtrn[:, :].rearrange("p (a d) -> p a d", a=2)
                E("sc" if i % 2 == 0 else "sy",
                  lambda e, w=wsrc, d0=wsg0: e.dma_start(out=d0, in_=w),
                  waits=(f"wtrn{i-2}",) if i >= 2 else ("thcols",),
                  mark=f"vwld{i}", dma=True, sem=f"syw{i%2}")
                E("sc", lambda e, i=i, nth=nth: e.sign(
                    wsA[:, :], wstage[:, (i % 2) * 2048:(i % 2 + 1) * 2048],
                    bias=cols_f[:, nth:nth + 1]),
                  mark=f"wsga{i}",
                  waits=(f"vwld{i}",) + ((f"wtrn{i-1}",) if i >= 1 else ()))
                E("sc", lambda e, i=i, pth=pth: e.sign(
                    wsB[:, :], wstage[:, (i % 2) * 2048:(i % 2 + 1) * 2048],
                    bias=cols_f[:, pth:pth + 1]),
                  mark=f"wsgn{i}")
                E("ve", lambda e: e.tensor_tensor(
                    wtrn[:, :], wsA[:, :], wsB[:, :], ALU.add),
                  waits=(f"wsga{i}", f"wsgn{i}") +
                  ((f"wtst{i-1}",) if i >= 1 else ()),
                  mark=f"wtrn{i}")
                E("gp", lambda e, w=wdst, s0=wtr0: e.dma_start(out=w, in_=s0),
                  waits=(f"wtrn{i}",), mark=f"wtst{i}", dma=True, sem="gpt")

            mseq = [0]

            def m_mms(src_ap, first, wmarks, last=False):
                for h in range(2):
                    E("te", lambda e, s=src_ap, h=h, f=first, l=last: e.matmul(
                        PS[0:1, 1024 + h * 512:1024 + (h + 1) * 512],
                        onesbc_sb[:, 0:1], s[:, h * 512:(h + 1) * 512],
                        start=f, stop=l),
                      mark=f"mmm{mseq[0]}" if h == 1 else None,
                      waits=wmarks if h == 0 else ())
                mseq[0] += 1

            for t in range(NT_OWN):
                b = t % 2
                w_ld = ([f"mmm{4*(t-2)}", f"t1m{t-2}"] if t >= 2
                        else [f"xnst{NT_ALL-2}", f"xnst{NT_ALL-1}",
                              f"gram{NT_ALL-1}"])
                E("sy", lambda e, t=t, b=b: e.dma_start(
                    out=xnb[:, b * D:(b + 1) * D],
                    in_=xnbf_d[t * 128:(t + 1) * 128, :]),
                  mark=f"jxld{t}", waits=w_ld, dma=True, sem=f"syx{t%2}")
                E("ve", lambda e, t=t, b=b: e.tensor_tensor(
                    t1b[:, :], xnb[:, b * D:(b + 1) * D], nw1b_tile[:, :],
                    ALU.mult),
                  waits=(f"jxld{t}",) +
                  (("nw1bf", f"gram{NT_ALL-1}") if t == 0
                   else (f"q0m{t-1}", f"em{t-1}")),
                  mark=f"t1m{t}")
                E("ve", lambda e: e.reduce_max(
                    qcols[:, 4:5], t1b[:, :], AX.X, apply_absolute_value=True),
                  mark=f"am1{t}", waits=(f"t1m{t}",))
                E("ve", lambda e, t=t: e.tensor_tensor(
                    qcols[:, 5:6], qcols[:, 4:5], rsq16[:, t:t + 1], ALU.mult),
                  mark=f"q5a{t}", waits=("v2e", f"am1{t}"))
                E("ve", lambda e: e.tensor_scalar(
                    qcols[:, 5:6], qcols[:, 5:6], 1e-5, None, ALU.max),
                  mark=f"q5b{t}", waits=(f"q5a{t}",))
                E("ve", lambda e, t=t: e.tensor_scalar(
                    rq1_cols[:, t:t + 1], qcols[:, 5:6], 1.0 / 127.0, None,
                    ALU.mult),
                  waits=(f"q5b{t}",))
                E("ve", lambda e: e.reciprocal(qcols[:, 6:7], qcols[:, 5:6]),
                  mark=f"rc1{t}")
                E("ve", lambda e: e.tensor_scalar(
                    qcols[:, 6:7], qcols[:, 6:7], 127.0, None, ALU.mult),
                  mark=f"q6b{t}", waits=(f"rc1{t}",))
                E("ve", lambda e, t=t: e.tensor_tensor(
                    c1_cols[:, t:t + 1], rsq16[:, t:t + 1], qcols[:, 6:7],
                    ALU.mult),
                  mark=f"c1m{t}", waits=(f"q6b{t}",))
                E("sc", lambda e, t=t: e.activation(
                    q0s[:, :], t1b[:, :], AF.Copy, scale=c1_cols[:, t:t + 1]),
                  waits=(f"c1m{t}",) +
                  ((f"xqm{t-1}",) if t >= 1 else de("wstat31")),
                  mark=f"q0m{t}")
                E("ve", lambda e, t=t: e.scalar_tensor_tensor(
                    e1s[:, :], t1b[:, :], c1_cols[:, t:t + 1], q0s[:, :],
                    ALU.mult, ALU.subtract),
                  waits=(f"q0m{t}", f"c1m{t}") +
                  ((f"sgn1{t-1}",) if t >= 1 else ()),
                  mark=f"em{t}")
                E("sc", lambda e: e.sign(sAs[:, :], e1s[:, :],
                                         bias=cols_f[:, C_NHALF:C_NHALF + 1]),
                  mark=f"sga{t}", waits=(f"em{t}",))
                E("sc", lambda e: e.sign(sBs[:, :], e1s[:, :],
                                         bias=cols_f[:, C_HALF:C_HALF + 1]),
                  mark=f"sgn1{t}")
                E("ve", lambda e: e.tensor_tensor(
                    adjs[:, :], sAs[:, :], sBs[:, :], ALU.add),
                  mark=f"adm{t}",
                  waits=(f"sga{t}", f"sgn1{t}") +
                  ((f"xqst{t-1}",) if t >= 1 else ()))
                E("ve", lambda e: e.scalar_tensor_tensor(
                    xq8[:, :], adjs[:, :], 0.5, q0s[:, :], ALU.mult, ALU.add),
                  mark=f"xqm{t}", waits=(f"adm{t}",))
                E("gp", lambda e, t=t: e.dma_start(
                    out=xq_d[t * 128:(t + 1) * 128, :], in_=xq8[:, :]),
                  waits=(f"xqm{t}",), mark=f"xqst{t}", dma=True, sem=f"gpxq{t%2}")
                for i in range(8):
                    E("te", lambda e, b=b, i=i: e.transpose(
                        PT[:, i * 128:(i + 1) * 128],
                        xnb[:, b * D + i * 128:b * D + (i + 1) * 128],
                        id_sb[:, :]),
                      waits=((f"jxld{t}",) + ((f"qTev{t-1}",) if t >= 1
                                              else ("gh1ev3",)))
                      if i == 0 else (),
                      mark=f"xnTmm{t}" if i == 7 else None)
                E("ve", lambda e: e.tensor_copy(xnT[:, :], PT[:, 0:1024]),
                  waits=(f"xnTmm{t}",), mark=f"xnTev{t}")
                for c in range(8):
                    for h in range(2):
                        E("te", lambda e, c=c, h=h: e.matmul(
                            PS[:, h * 512:(h + 1) * 512],
                            xnT[:, c * 128:(c + 1) * 128],
                            wq_sb[:, c * 1024 + h * 512:c * 1024 + (h + 1) * 512],
                            start=(c == 0), stop=(c == 7)),
                          waits=((f"xnTev{t}", "wqf") +
                                 ((f"qev{t-1}",) if t >= 1 else ()))
                          if (c == 0 and h == 0) else (),
                          mark=f"qmm{t}" if (c == 7 and h == 1) else None)
                E("ve", lambda e: e.tensor_tensor(
                    qtok[:, :], PS[:, 0:1024], bq_tile[:, :], ALU.add),
                  waits=(f"qmm{t}",), mark=f"qev{t}")
                for i in range(8):
                    E("te", lambda e, i=i: e.transpose(
                        PT[:, i * 128:(i + 1) * 128],
                        qtok[:, i * 128:(i + 1) * 128], id_sb[:, :]),
                      waits=(f"qev{t}", f"xnTev{t}") if i == 0 else (),
                      mark=f"qTmm{t}" if i == 7 else None)
                E("ve", lambda e: e.tensor_copy(qtile[:, :], PT[:, 0:1024]),
                  waits=(f"qTmm{t}",) + ((f"qtst{t-1}",) if t >= 1 else ()),
                  mark=f"qTev{t}")
                E("gp", lambda e, t=t: e.dma_start(
                    out=qt_d.rearrange("p (c u) -> p c u", c=8)[
                        :, :, t * 128:(t + 1) * 128],
                    in_=qtile[:, :].rearrange("p (c u) -> p c u", c=8)),
                  waits=(f"qTev{t}",), mark=f"qtst{t}", dma=True, sem=f"gpqt{t%2}")
                if t < 8:
                    tern_chunk(t)
                m_mms(xnb[:, b * D:(b + 1) * D], first=(t == 0),
                      wmarks=(f"jxld{t}",))
                for j in range(3):
                    u = NT_OWN + t * 3 + j
                    fb = u % 2
                    E("sy", lambda e, u=u, fb=fb: e.dma_start(
                        out=fbuf[:, fb * D:(fb + 1) * D],
                        in_=xnbf_d[u * 128:(u + 1) * 128, :]),
                      waits=(f"mmm{mseq[0]-2}",) if mseq[0] >= 2 else (),
                      mark=f"fld{u}", dma=True, sem=f"syf{u%2}")
                    m_mms(fbuf[:, fb * D:(fb + 1) * D], first=False,
                          wmarks=(f"fld{u}",),
                          last=(t == NT_OWN - 1 and j == 2))

            E("ve", lambda e: e.tensor_copy(mrow_f[0:1, :], PS[0:1, 1024:2048]),
              waits=(f"mmm{mseq[0]-1}",), mark="mev")
            E("ve", lambda e: e.tensor_copy(mrow_b[0:1, :], mrow_f[0:1, :]),
              mark="mbf", waits=("mev",))
            E("gp", lambda e: e.dma_start(out=mrow_d[:, :], in_=mrow_b[0:1, :]),
              waits=("mbf",), mark="mst", dma=True, sem="gpm")
            for c in range(8):
                E("gp", lambda e, c=c: e.dma_start(
                    out=mcol_b[:, c:c + 1],
                    in_=mrow_d[0:1, c * 128:(c + 1) * 128]),
                  waits=("mst",) if c == 0 else (), mark=f"mcol{c}", dma=True,
                  sem="gpm")
            if debug:
                E("gp", lambda e: e.dma_start(out=dbg["dbg_m"][:, :],
                                              in_=mrow_f[0:1, :]),
                  waits=("mev",), mark="dbgm", dma=True, sem="gpd")

            if PHASE_LIMIT < 3:
                E("ve", fn=None, waits=("mbf", f"qtst{NT_OWN-1}",
                                        f"xqst{NT_OWN-1}", f"mcol7"))
                return
            for r in range(2):
                for i in range(8):
                    k = r * 8 + i
                    f = 4 + k // 4
                    cb = k % 4
                    E("te", lambda e, i=i, f=f, cb=cb: e.transpose(
                        PT[:, i * 128:(i + 1) * 128],
                        g_bf[:, cb * 1024 + f * 128:cb * 1024 + (f + 1) * 128],
                        id_sb[:, :]),
                      waits=(("gh1ev3", f"qTev{NT_OWN-1}") if (r == 0 and i == 0)
                             else (("symev0",) if (r == 1 and i == 0) else ())),
                      mark=f"symt{r}" if i == 7 else None)
                fo = 4 + r * 2
                E("ve", lambda e, fo=fo: e.tensor_copy(
                    g_bf[:, fo * 1024:fo * 1024 + 512], PT[:, 0:512]),
                  waits=(f"symt{r}",))
                E("ve", lambda e, fo=fo: e.tensor_copy(
                    g_bf[:, (fo + 1) * 1024:(fo + 1) * 1024 + 512],
                    PT[:, 512:1024]),
                  mark=f"symev{r}")
            for u in range(NT_ALL):
                fb = u % 4
                E("sc" if u % 2 == 0 else "sy",
                  lambda e, u=u, fb=fb: e.dma_start(
                    out=fbuf4[:, fb * 512:(fb + 1) * 512],
                    in_=xnbf_d[u * 128:(u + 1) * 128, 512:1024]),
                  waits=(f"g2{u-4}",) if u >= 4 else (f"mmm{mseq[0]-1}",),
                  mark=f"f2ld{u}", dma=True, sem=f"syf{u%4}")
                for c in range(4):
                    E("te", lambda e, u=u, fb=fb, c=c: e.matmul(
                        PS[:, c * 512:(c + 1) * 512],
                        fbuf4[:, fb * 512 + c * 128:fb * 512 + (c + 1) * 128],
                        fbuf4[:, fb * 512:(fb + 1) * 512],
                        start=(u == 0), stop=(u == NT_ALL - 1)),
                      waits=((f"f2ld{u}", "mev", f"qTev{NT_OWN-1}")
                             if (c == 0 and u == 0) else
                             ((f"f2ld{u}",) if c == 0 else ())),
                      mark=f"g2{u}" if c == 3 else None)
                if u % 8 < 5 and u % 2 == 0:
                    tern_chunk(8 + (u // 8) * 3 + (u % 8) // 2)
            for c in range(4, 8):
                E("ve", lambda e, c=c: e.tensor_copy(
                    g_bf[:, c * 1024 + 512:(c + 1) * 1024],
                    PS[:, (c - 4) * 512:(c - 3) * 512]),
                  mark=f"gh2ev{c}",
                  waits=(f"g2{NT_ALL-1}", "symev1") if c == 4 else ())
            if debug:
                E("gp", lambda e: e.dma_start(out=dbg["dbg_G"][:, :],
                                              in_=g_bf[:, :]),
                  waits=("gh2ev7",), mark="dbgg", dma=True, sem="gpd")

            if PHASE_LIMIT < 4:
                E("ve", fn=None, waits=("gh2ev7", "wtst31"))
                return
            for idx, (dst, src) in enumerate([
                    (bkbv_b[0:1, 0:D], bk_d[0:1, :]),
                    (bkbv_b[0:1, D:2 * D], bk_d[1:2, :]),
                    (bkbv_b[0:1, 2 * D:3 * D], bv_d[0:1, :]),
                    (bkbv_b[0:1, 3 * D:4 * D], bv_d[1:2, :])]):
                E("gp", lambda e, d=dst, s=src: e.dma_start(out=d, in_=s),
                  waits=("wtst31", f"xqst{NT_OWN-1}") if idx == 0 else (),
                  mark=f"bkbvld{idx}", dma=True, sem="gpl")
            for k in range(2):
                E("sy", lambda e, k=k: e.dma_start(out=bvrow[0:1, :],
                                                   in_=bv_d[k:k + 1, :]),
                  waits=("vwld31", f"qtst{NT_OWN-1}") if k == 0
                  else (f"wradd{k-1}",),
                  mark=f"bvrowld{k}", dma=True, sem="syp")
                wvv = wv_d.rearrange("t (c p) j -> t p c j", p=128)[k]
                wkk = wk_d.rearrange("t (c p) j -> t p c j", p=128)[k]
                E("gp", lambda e, w=wvv: e.dma_start(
                    out=wkv_sb[:, :].rearrange("p (c j) -> p c j", c=8), in_=w),
                  waits=(f"smm{k-1}_{NH-1}",) if k >= 1 else ("bkbvld3",),
                  mark=f"wvld{k}", dma=True, sem="gpl")
                for c in range(8):
                    E("ve", lambda e, c=c: e.tensor_tensor(
                        wkv_sb[:, c * 1024 + c * 128:c * 1024 + (c + 1) * 128],
                        wkv_sb[:, c * 1024 + c * 128:c * 1024 + (c + 1) * 128],
                        id_sb[:, :], ALU.add),
                      waits=(f"wvld{k}", "gh2ev7") if c == 0 else (),
                      mark=f"wvf{k}" if c == 7 else None)
                for c in range(8):
                    for h in range(2):
                        E("te", lambda e, c=c, h=h: e.matmul(
                            PS[0:1, 2560 + h * 512:2560 + (h + 1) * 512],
                            mcol_b[:, c:c + 1],
                            wkv_sb[:, c * 1024 + h * 512:c * 1024 + (h + 1) * 512],
                            start=(c == 0), stop=(c == 7)),
                          waits=(f"wvf{k}", "mcol7", "swcols")
                          if (c == 0 and h == 0) else (),
                          mark=f"wrmm{k}" if (c == 7 and h == 1) else None)
                E("ve", lambda e, k=k: e.tensor_copy(wrow_f[0:1, :],
                                                PS[0:1, 2560:3584]),
                  waits=(f"wrmm{k}", f"bvrowld{k}"), mark=f"wrev{k}")
                E("ve", lambda e, k=k: e.scalar_tensor_tensor(
                    wrow_f[0:1, :], bvrow[0:1, :],
                    float(NTOK), wrow_f[0:1, :], ALU.mult, ALU.add),
                  mark=f"wradd{k}", waits=(f"wrev{k}",))
                E("ve", lambda e: e.tensor_copy(wrow_b[0:1, :], wrow_f[0:1, :]),
                  mark=f"wrbf{k}", waits=(f"wradd{k}",))
                for c in range(8):
                    pp = (c % 2) * 1024
                    prevt = (f"tev{k}_{c-2}",) if c >= 2 else \
                        ((f"tev{k-1}_{c+6}",) if k >= 1 else ())
                    for h in range(2):
                        E("te", lambda e, c=c, h=h, pp=pp, k=k: e.matmul(
                            PS[:, pp + h * 512:pp + (h + 1) * 512],
                            mrow_b[0:1, c * 128:(c + 1) * 128],
                            bkbv_b[0:1, (2 + k) * D + h * 512:
                                   (2 + k) * D + (h + 1) * 512],
                            start=True, stop=False),
                          waits=((f"wvf{k}", "mbf") + prevt) if h == 0 else ())
                    for f in range(8):
                        for h in range(2):
                            E("te", lambda e, c=c, f=f, h=h, pp=pp: e.matmul(
                                PS[:, pp + h * 512:pp + (h + 1) * 512],
                                g_bf[:, f * 1024 + c * 128:f * 1024 + (c + 1) * 128],
                                wkv_sb[:, f * 1024 + h * 512:
                                       f * 1024 + (h + 1) * 512],
                                start=False, stop=(f == 7)),
                              mark=f"tmm{k}_{c}" if (f == 7 and h == 1) else None)
                    E("ve", lambda e, c=c, pp=pp, k=k: e.tensor_copy(
                        t_bf[:, c * 1024:(c + 1) * 1024], PS[:, pp:pp + 1024]),
                      waits=(f"tmm{k}_{c}",), mark=f"tev{k}_{c}")
                E("gp", lambda e, w=wkk: e.dma_start(
                    out=wkv_sb[:, :].rearrange("p (c j) -> p c j", c=8), in_=w),
                  waits=(f"tmm{k}_7", f"wrmm{k}"),
                  mark=f"wkld{k}", dma=True, sem="gpl")
                for c in range(8):
                    E("ve", lambda e, c=c: e.tensor_tensor(
                        wkv_sb[:, c * 1024 + c * 128:c * 1024 + (c + 1) * 128],
                        wkv_sb[:, c * 1024 + c * 128:c * 1024 + (c + 1) * 128],
                        id_sb[:, :], ALU.add),
                      waits=(f"wkld{k}", f"tev{k}_7") if c == 0 else (),
                      mark=f"wkf{k}" if c == 7 else None)
                for h in range(NH):
                    E("te", lambda e, h=h, k=k: e.matmul(
                        PS[0:64, 2048 + h * 64:2048 + (h + 1) * 64],
                        bkbv_b[0:1, k * D + h * 64:k * D + (h + 1) * 64],
                        wrow_b[0:1, h * 64:(h + 1) * 64],
                        start=True, stop=False),
                      waits=(f"wrbf{k}", f"wkf{k}") if h == 0 else ())
                    for cc in range(8):
                        E("te", lambda e, h=h, cc=cc: e.matmul(
                            PS[0:64, 2048 + h * 64:2048 + (h + 1) * 64],
                            wkv_sb[:, cc * 1024 + h * 64:cc * 1024 + h * 64 + 64],
                            t_bf[:, cc * 1024 + h * 64:cc * 1024 + h * 64 + 64],
                            start=False, stop=(cc == 7)),
                          mark=f"smm{k}_{h}" if cc == 7 else None)
                E("ve", lambda e, k=k: e.tensor_copy(
                    states_sb[0:64, k * D:(k + 1) * D], PS[0:64, 2048:3072]),
                  waits=tuple(f"smm{k}_{h}" for h in range(NH)) +
                  (("wrev0",) if k == 0 else ()),
                  mark=f"stev{k}")
            E("ve", lambda e: e.tensor_tensor(
                states_sb[0:64, 0:D], states_sb[0:64, 0:D],
                states_sb[0:64, D:2 * D], ALU.mult),
              waits=("stev0", "stev1"), mark="stp1")
            E("ve", lambda e: e.tensor_scalar(
                statet_b[0:64, :], states_sb[0:64, 0:D], ISCALE2, None, ALU.mult),
              mark="stp", waits=("stp1",))
            E("ve", lambda e: e.memset(statet2[:, :], 0.0), mark="st2z")
            E("ve", lambda e: e.tensor_copy(
                statet2[0:64, 0:512].rearrange("p (j u) -> p j u", j=8),
                statet_b[0:64, :].rearrange("p (j x u) -> p j x u",
                                            j=8, x=2)[:, :, 0, :]),
              waits=("stp", "st2z"), mark="st2e")
            for j in range(8):
                E("te", lambda e, j=j: e.matmul(
                    PS[64:128, 2048 + j * 64:2048 + (j + 1) * 64],
                    id_sb[0:64, 0:64],
                    statet_b[0:64, j * 128 + 64:j * 128 + 128],
                    start=True, stop=True),
                  waits=("stp",) if j == 0 else (),
                  mark="st2mm" if j == 7 else None)
            E("ve", lambda e: e.tensor_copy(statet2[64:128, 512:1024],
                                            PS[64:128, 2048:2560]),
              waits=("st2mm",), mark="st2o")
            if debug:
                E("gp", lambda e: e.dma_start(out=dbg["dbg_stp"][:, :],
                                              in_=states_sb[0:64, 0:D]),
                  waits=("stp1",), mark="dbgstp", dma=True, sem="gpd")

            if PHASE_LIMIT < 5:
                E("ve", fn=None, waits=("st2o", "st2e"))
                return
            for c in range(8):
                E("gp", lambda e, c=c: e.dma_start(
                    out=w1q_sb[:, c * HID:(c + 1) * HID],
                    in_=w1q_d[c * 128:(c + 1) * 128, :]),
                  waits=(("stev1",) + (("dbgstp",) if debug else ()))
                  if c == 0 else (),
                  mark=f"w1qld{c}", dma=True, sem="gpl")
            E("ve", lambda e: e.tensor_scalar(
                scol_cols[:, :], rq1_cols[:, :],
                cols_f[:, C_S1F:C_S1F + 1], None, ALU.mult),
              mark="scolc", waits=("s1fc", f"c1m{NT_OWN-1}"))

            def b1Aload(t):
                b = t % 2
                E("gp", lambda e, t=t, b=b: e.dma_start(
                    out=xqb2[:, b * D:(b + 1) * D],
                    in_=xq_d[t * 128:(t + 1) * 128, :]),
                  waits=(f"xqst{t}",) + ((f"xqTmm{t-2}",) if t >= 2
                                         else ("stp",)),
                  mark=f"xqld{t}", dma=True, sem=f"gpb{t%2}")
                for i in range(8):
                    E("te", lambda e, b=b, i=i: e.transpose(
                        PT[:, i * 128:(i + 1) * 128],
                        xqb2[:, b * D + i * 128:b * D + (i + 1) * 128],
                        id_sb[:, :]),
                      waits=((f"xqld{t}",) +
                             ((f"xqTev{t-1}",) if t >= 1 else ("st2o",)) +
                             ((f"mm1_{t-2}_2",) if t >= 2 else ()))
                      if i == 0 else (),
                      mark=f"xqTmm{t}" if i == 7 else None)
                E("ve", lambda e, b=b: e.tensor_copy(
                    xqT_sb[:, b * D:(b + 1) * D], PT[:, 0:1024]),
                  waits=(f"xqTmm{t}",), mark=f"xqTev{t}")

            def b1Amm(t):
                b = t % 2
                groups = [(0, 3, 0), (3, 3, 1536), (6, 2, 0)]
                for g, (nq0, nn, off) in enumerate(groups):
                    for c in range(8):
                        for j in range(nn):
                            nq = nq0 + j
                            wl = []
                            if c == 0 and j == 0:
                                wl = [f"xqTev{t}", "w1qld7"]
                                if g == 2:
                                    wl.append(f"rev{t}_0")
                                elif t >= 1:
                                    wl.append(f"rev{t-1}_2" if g == 0
                                              else f"rev{t-1}_{g}")
                            E("te", lambda e, c=c, nq=nq, off=off, j=j,
                              b=b: e.matmul(
                                PS[:, off + j * 512:off + (j + 1) * 512],
                                xqT_sb[:, b * D + c * 128:
                                       b * D + (c + 1) * 128],
                                w1q_sb[:, c * HID + nq * 512:
                                       c * HID + (nq + 1) * 512],
                                start=(c == 0), stop=(c == 7)),
                              waits=wl,
                              mark=f"mm1_{t}_{g}" if (c == 7 and j == nn - 1)
                              else None)

            def b1S(t):
                b = t % 2
                gws = [(0, 1536, 0), (3, 1536, 1536), (6, 1024, 0)]
                for g, (nq0, gw, off) in enumerate(gws):
                    E("sc", lambda e, g=g, off=off, gw=gw, nq0=nq0,
                      t=t: e.activation(
                        r_sb[:, nq0 * 512:nq0 * 512 + gw], PS[:, off:off + gw],
                        AF.Relu, scale=scol_cols[:, t:t + 1]),
                      waits=(f"mm1_{t}_{g}",) +
                      (("scolc",) if (t == 0 and g == 0) else ()) +
                      ((f"u0m{t-1}_1",) if (t >= 1 and g == 0) else ()),
                      mark=f"rev{t}_{g}")
                if debug:
                    E("gp", lambda e, t=t: e.dma_start(
                        out=dbg["dbg_r"][t * 128:(t + 1) * 128, :],
                        in_=r_sb[:, 0:HID]),
                      waits=(f"rev{t}_2",), mark=f"dbgr{t}", dma=True,
                      sem="gdr")
                E("sc", lambda e: e.activation(
                    u0_sb[:, :], r_sb[:, :], AF.Square,
                    accum_out=qcols[:, 0:1]),
                  waits=(f"rev{t}_2",) +
                  ((f"u0p{t-1}", f"t2m{t-1}") if t >= 1 else ()),
                  mark=f"sqa{t}_0")
                E("sc", lambda e: e.activation(
                    r_sb[:, :], u0_sb[:, :], AF.Square,
                    accum_out=qcols[:, 2:3]),
                  mark=f"u0m{t}_1",
                  waits=de(f"sqa{t}_0") +
                  ((f"dbgr{t}",) if debug else ()))
            def b1V(t):
                b = t % 2
                E("ve", lambda e: e.tensor_tensor(
                    qcols[:, 2:3], qcols[:, 2:3], cols_f[:, C_A2:C_A2 + 1],
                    ALU.mult),
                  mark=f"k2{t}", waits=(f"u0m{t}_1",) +
                  (("a2c", "ebc") if t == 0 else ()))
                E("ve", lambda e: e.tensor_scalar(
                    u0_sb[:, :], u0_sb[:, :], cols_f[:, C_BOA:C_BOA + 1],
                    None, ALU.add),
                  mark=f"u0p{t}")
                E("ve", lambda e: e.scalar_tensor_tensor(
                    qcols[:, 0:1], qcols[:, 0:1], cols_f[:, C_AB2:C_AB2 + 1],
                    qcols[:, 2:3], ALU.mult, ALU.add),
                  mark=f"msm{t}", waits=de(f"k2{t}"))
                E("sc", lambda e: e.activation(
                    qcols[:, 4:5], qcols[:, 0:1], AF.Ln, scale=1.0 / HID,
                    bias=cols_f[:, C_EB:C_EB + 1]),
                  mark=f"ln2q{t}", waits=(f"msm{t}",))
                E("sc", lambda e: e.activation(
                    qcols[:, 4:5], qcols[:, 4:5], AF.Exp, scale=-0.5),
                  mark=f"rs2{t}", waits=de(f"ln2q{t}"))
                E("ve", lambda e: e.tensor_tensor(
                    t2f_sb[:, :], u0_sb[:, :], anw2_tile[:, :], ALU.mult),
                  waits=de(f"u0p{t}") + ((f"q0bm{t-1}",) if t >= 1 else ()),
                  mark=f"t2m{t}")
                E("ve", lambda e: e.memset(scr8[:, 0:1], 0.0))
                E("ve", lambda e: e.reduce_max(
                    qcols[:, 5:6], t2f_sb[:, :], AX.X, apply_absolute_value=True),
                  waits=de(f"t2m{t}"), mark=f"am2{t}")
                E("ve", lambda e: e.memset(scr8[:, 1:2], 0.0))
                E("ve", lambda e: e.tensor_scalar(
                    qcols[:, 5:6], qcols[:, 5:6], qcols[:, 4:5], 1e-5,
                    ALU.mult, ALU.max),
                  mark=f"yb{t}", waits=(f"rs2{t}",) + de(f"am2{t}"))
                E("ve", lambda e: e.reciprocal(qcols[:, 6:7], qcols[:, 5:6]),
                  mark=f"rc2{t}", waits=(f"yb{t}",))
                E("ve", lambda e, t=t: e.tensor_tensor(
                    ysc_cols[:, t:t + 1], qcols[:, 5:6],
                    cols_f[:, C_S2F:C_S2F + 1], ALU.mult),
                  mark=f"ysc{t}", waits=de(f"yb{t}") +
                  (("s2fc",) if t == 0 else ()))
                E("ve", lambda e: e.scalar_tensor_tensor(
                    qcols[:, 6:7], qcols[:, 6:7], 127.0, qcols[:, 4:5],
                    ALU.mult, ALU.mult),
                  mark=f"c2m{t}", waits=de(f"rc2{t}"))
                E("ve", lambda e: e.tensor_scalar(
                    q0b_sb[:, :], t2f_sb[:, :], qcols[:, 6:7], None, ALU.mult),
                  waits=(f"c2m{t}",) + ((f"x2m{t-1}_3",) if t >= 1 else ()),
                  mark=f"q0bm{t}")
            def b1Q(t):
                b = t % 2
                for hh in range(2):
                    E("ve", lambda e, hh=hh: e.scalar_tensor_tensor(
                        e2h_sb[:, :], t2f_sb[:, hh * 2048:(hh + 1) * 2048],
                        qcols[:, 6:7], q0b_sb[:, hh * 2048:(hh + 1) * 2048],
                        ALU.mult, ALU.subtract),
                      waits=(f"q0bm{t}",) +
                      ((f"sg2a{t}_1", f"sg2{t}_1") if hh >= 1 else ()),
                      mark=f"e2m{t}_{hh}")
                    for qq in range(2):
                        qi = hh * 2 + qq
                        p = qi % 2
                        sgb = sB2_sb if p == 0 else sB2b_sb
                        prevq = ((f"x2m{t}_{qi-2}",) if qi >= 2 else
                                 ((f"x2m{t-1}_{qi+2}",) if t >= 1 else ()))
                        E("sc", lambda e, qq=qq, p=p: e.sign(
                            sA2_sb[:, p * D:(p + 1) * D],
                            e2h_sb[:, qq * 1024:(qq + 1) * 1024],
                            bias=cols_f[:, C_NHALF:C_NHALF + 1]),
                          mark=f"sg2a{t}_{qi}",
                          waits=(f"e2m{t}_{hh}",) + prevq)
                        E("sc", lambda e, qq=qq, sgb=sgb: e.sign(
                            sgb[:, :], e2h_sb[:, qq * 1024:(qq + 1) * 1024],
                            bias=cols_f[:, C_HALF:C_HALF + 1]),
                          mark=f"sg2{t}_{qi}")
                        E("ve", lambda e, sgb=sgb, p=p: e.tensor_tensor(
                            sgb[:, :], sA2_sb[:, p * D:(p + 1) * D],
                            sgb[:, :], ALU.add),
                          mark=f"xma{t}_{qi}",
                          waits=(f"sg2a{t}_{qi}", f"sg2{t}_{qi}") +
                          ((f"x2st{t-1}",) if (qi == 0 and t >= 1) else ()))
                        E("ve", lambda e, qi=qi, sgb=sgb: e.scalar_tensor_tensor(
                            x2b_sb[:, qi * 1024:(qi + 1) * 1024], sgb[:, :],
                            0.5, q0b_sb[:, qi * 1024:(qi + 1) * 1024],
                            ALU.mult, ALU.add),
                          mark=f"x2m{t}_{qi}",
                          waits=(f"xma{t}_{qi}",))
                E("gp", lambda e, t=t: e.dma_start(
                    out=x2_d[t * 128:(t + 1) * 128, :], in_=x2b_sb[:, :]),
                  waits=(f"x2m{t}_3",), mark=f"x2st{t}", dma=True, sem=f"gpx2{t%2}")

            b1Aload(0)
            b1Aload(1)
            b1Amm(0)
            b1S(0)
            b1V(0)
            for t in range(NT_OWN):
                if t + 2 < NT_OWN:
                    b1Aload(t + 2)
                if t + 1 < NT_OWN:
                    b1Amm(t + 1)
                    b1S(t + 1)
                b1Q(t)
                if t + 1 < NT_OWN:
                    b1V(t + 1)

            if PHASE_LIMIT < 6:
                E("ve", fn=None, waits=(f"x2st{NT_OWN-1}",))
                return
            for c in range(4):
                E("gp", lambda e, c=c: e.dma_start(
                    out=w2q_sb[:, c * 8 * D:(c + 1) * 8 * D].rearrange(
                        "p (a j) -> p a j", a=8),
                    in_=w2q_d.rearrange("(a p) j -> p a j", p=128)[
                        :, c * 8:(c + 1) * 8, :]),
                  waits=(f"x2st{NT_OWN-1}",) if c == 0
                  else (f"w2qld{c-1}",),
                  mark=f"w2qld{c}", dma=True, sem="gpl")
            E("gp", lambda e: e.dma_start(
                out=wo_sb[:, :].rearrange("p (c j) -> p c j", c=8),
                in_=wo_d.rearrange("(c p) j -> p c j", p=128)),
              mark="wold", waits=("w2qld3",), dma=True, sem="gpl")
            def b2A(t):
                b = t % 2
                s8 = b * HID
                E("gp", lambda e, t=t, s8=s8: e.dma_start(
                    out=x2bf2[:, s8:s8 + HID],
                    in_=x2_d[t * 128:(t + 1) * 128, :]),
                  waits=(f"x2st{t}",) +
                  ((f"x2Tmm{t-2}_1",) if t >= 2 else
                   (f"e2m{NT_OWN-1}_1", f"t2m{NT_OWN-1}")),
                  mark=f"x2ld{t}", dma=True, sem=f"gpx2{t%2}")
                E("sy", lambda e, t=t, b=b: e.dma_start(
                    out=qtb[:, b * D:(b + 1) * D].rearrange(
                        "p (c u) -> p c u", c=8),
                    in_=qt_d.rearrange("p (c u) -> p c u", c=8)[
                        :, :, t * 128:(t + 1) * 128]),
                  waits=((f"omm{t-2}",) if (t >= 2 and B2SUB >= 2) else ()) +
                  ((f"t2m{NT_OWN-1}",) if t < 2 else ()),
                  mark=f"qtld{t}", dma=True, sem=f"syq{t%2}")
                E("sy", lambda e, t=t, b=b: e.dma_start(
                    out=xbuf[:, b * D:(b + 1) * D],
                    in_=x_d[t * 128:(t + 1) * 128, :]),
                  waits=(f"resm{t-2}",) if t >= 2 else (),
                  mark=f"bxld{t}", dma=True, sem=f"syx{t%2}")
                for bt in range(2):
                    for i in range(16):
                        w0 = ()
                        if i == 0:
                            w0 = (f"x2ld{t}",)
                            if bt == 1:
                                w0 += (f"x2Tev{t}_0",)
                            elif t >= 1:
                                w0 += (f"x2Tev{t-1}_1", f"mm2_{t-1}")
                            else:
                                w0 += ("st2o",)
                        E("te", lambda e, bt=bt, i=i, s8=s8: e.transpose(
                            PT2[:, i * 128:(i + 1) * 128],
                            x2bf2[:, s8 + bt * 2048 + i * 128:
                                  s8 + bt * 2048 + (i + 1) * 128],
                            id_sb[:, :]),
                          waits=w0,
                          mark=f"x2Tmm{t}_{bt}" if i == 15 else None)
                    E("ve", lambda e, bt=bt: e.tensor_copy(
                        x2T_sb[:, bt * 2048:(bt + 1) * 2048], PT2[:, 0:2048]),
                      waits=(f"x2Tmm{t}_{bt}",) +
                      ((f"x2m{NT_OWN-1}_3", f"t2m{NT_OWN-1}")
                       if (t == 0 and bt == 0) else ()),
                      mark=f"x2Tev{t}_{bt}")
                for cc in range(32):
                    wl = []
                    if t == 0 and cc % 8 == 0:
                        wl.append(f"w2qld{cc//8}")
                    if cc == 0:
                        wl += [f"x2Tev{t}_1"]
                        wl.append(f"yev{t-1}" if t >= 1 else f"rev{NT_OWN-1}_2")
                    E("te", lambda e, cc=cc: e.matmul(
                        PS[:, 0:512],
                        x2T_sb[:, cc * 128:(cc + 1) * 128],
                        w2q_sb[:, cc * 1024:cc * 1024 + 512],
                        start=(cc == 0), stop=(cc == 31)),
                      waits=wl)
                    E("te", lambda e, cc=cc: e.matmul(
                        PS[:, 512:1024],
                        x2T_sb[:, cc * 128:(cc + 1) * 128],
                        w2q_sb[:, cc * 1024 + 512:(cc + 1) * 1024],
                        start=(cc == 0), stop=(cc == 31)),
                      mark=f"mm2_{t}" if cc == 31 else None)

            def b2B(t):
                b = t % 2
                E("sc", lambda e, t=t: e.activation(
                    y_sb[:, :], PS[:, 0:1024], AF.Copy,
                    scale=ysc_cols[:, t:t + 1]),
                  waits=(f"mm2_{t}", f"ysc{t}") +
                  ((f"gm{t-1}",) if t >= 1
                   else (f"t2m{NT_OWN-1}", f"sg2{NT_OWN-1}_3")) +
                  ((f"dbgy{t-1}",) if (debug and t >= 1) else ()),
                  mark=f"yev{t}")
                E("sc", lambda e: e.activation(
                    th_sb[:, :], y_sb[:, :], AF.Tanh, scale=0.5),
                  mark=f"thm{t}", waits=de(f"yev{t}"))
                E("ve", lambda e: e.tensor_scalar(
                    gh_sb[:, :], th_sb[:, :], 0.5, 0.5, ALU.mult, ALU.add),
                  mark=f"ghm{t}", waits=(f"thm{t}",))
                E("ve", lambda e: e.tensor_tensor(
                    g_t[:, :], gh_sb[:, :], y_sb[:, :], ALU.mult),
                  mark=f"gm{t}",
                  waits=(f"ghm{t}",) +
                  ((f"dbggt{t-1}",) if (debug and t >= 1) else ()))
                for h in range(NH if B2SUB >= 2 else 0):
                    wl = []
                    if h == 0:
                        wl = [f"qtld{t}", "st2o"]
                        if t >= 1:
                            wl.append(f"onTev{t-1}")
                    E("te", lambda e, h=h, b=b: e.matmul(
                        PS[:, 1024 + h * 64:1024 + (h + 1) * 64],
                        qtb[0:128,
                            b * D + (h // 2) * 128:b * D + (h // 2) * 128 + 128],
                        statet2[0:128,
                                (h % 2) * 512 + (h // 2) * 64:
                                (h % 2) * 512 + (h // 2) * 64 + 64],
                        start=True, stop=True),
                      waits=wl, mark=f"omm{t}" if h == NH - 1 else None)
                if B2SUB < 2:
                    E("ve", lambda e, t=t, b=b: e.tensor_tensor(
                        res_f[:, :], g_t[:, :], xbuf[:, b * D:(b + 1) * D],
                        ALU.add),
                      waits=(f"bxld{t}", f"gm{t}") +
                      ((f"outst{t-1}",) if t >= 1 else ()),
                      mark=f"resm{t}")
                    E("sy", lambda e, t=t: e.dma_start(
                        out=out_d[t * 128:(t + 1) * 128, :], in_=res_f[:, :]),
                      waits=(f"resm{t}",), mark=f"outst{t}", dma=True, sem="syo")
                    return
                E("ve", lambda e: e.tensor_copy(
                    outa_sb[:, :], PS[:, 1024:2048]),
                  waits=(f"omm{t}",) +
                  ((f"dbgoa{t-1}",) if (debug and t >= 1)
                   else ((f"x2m{NT_OWN-1}_3",) if t == 0 else ())),
                  mark=f"oev{t}")
                E("ve", lambda e: e.reduce_sum(
                    lncols[:, 9:10], PS[:, 1024:2048], AX.X),
                  mark=f"orow{t}")
                E("ve", lambda e: e.tensor_tensor(
                    sqscr[:, 0:D], outa_sb[:, :], outa_sb[:, :], ALU.mult),
                  mark=f"osqm{t}", waits=(f"oev{t}",))
                E("ve", lambda e: e.tensor_scalar(
                    lncols[:, 4:5], lncols[:, 9:10], 1.0 / D, None, ALU.mult),
                  mark=f"oca{t}", waits=(f"orow{t}",))
                E("ve", lambda e: e.reduce_sum(
                    lncols[:, 10:11], sqscr[:, 0:D], AX.X),
                  mark=f"osq{t}", waits=(f"osqm{t}",))
                E("ve", lambda e: e.scalar_tensor_tensor(
                    lncols[:, 6:7], lncols[:, 4:5], 1.0, lncols[:, 4:5],
                    ALU.bypass, ALU.mult),
                  mark=f"occ{t}", waits=(f"oca{t}",))
                E("ve", lambda e: e.tensor_scalar(
                    lncols[:, 5:6], lncols[:, 10:11], 1.0 / D, None, ALU.mult),
                  mark=f"ocb{t}", waits=(f"osq{t}",))
                E("ve", lambda e: e.tensor_tensor(
                    lncols[:, 6:7], lncols[:, 5:6], lncols[:, 6:7], ALU.subtract),
                  mark=f"ocm{t}", waits=(f"ocb{t}", f"occ{t}"))
                E("sc", lambda e: e.activation(
                    lncols[:, 7:8], lncols[:, 6:7], AF.Ln,
                    bias=cols_f[:, C_EPSLN:C_EPSLN + 1]),
                  mark=f"oln{t}", waits=(f"ocm{t}",))
                E("sc", lambda e: e.activation(
                    lncols[:, 7:8], lncols[:, 7:8], AF.Exp, scale=-0.5),
                  mark=f"orstd{t}", waits=de(f"oln{t}"))
                E("ve", lambda e: e.scalar_tensor_tensor(
                    lncols[:, 8:9], lncols[:, 4:5], -1.0, lncols[:, 7:8],
                    ALU.mult, ALU.mult),
                  waits=(f"orstd{t}",), mark=f"onmr{t}")
                E("ve", lambda e: e.tensor_scalar(
                    outn_sb[:, :], outa_sb[:, :],
                    lncols[:, 7:8], lncols[:, 8:9], ALU.mult, ALU.add),
                  waits=(f"onmr{t}",) + ((f"onTmm{t-1}",) if t >= 1 else ()),
                  mark=f"onorm{t}")
                E("ve", lambda e: e.tensor_tensor(
                    outn_sb[:, :], outn_sb[:, :], lng_tile[:, :], ALU.mult),
                  mark=f"og1{t}", waits=(f"onorm{t}",))
                E("ve", lambda e: e.tensor_tensor(
                    outn_sb[:, :], outn_sb[:, :], lnb_tile[:, :], ALU.add),
                  mark=f"ognb{t}", waits=(f"og1{t}",))
                for i in range(8):
                    w0 = ()
                    if i == 0:
                        w0 = (f"ognb{t}", f"oev{t}")
                        if t >= 1:
                            w0 += (f"onTev{t-1}",)
                    E("te", lambda e, i=i: e.transpose(
                        PTo[:, i * 128:(i + 1) * 128],
                        outn_sb[:, i * 128:(i + 1) * 128], id_sb[:, :]),
                      waits=w0, mark=f"onTmm{t}" if i == 7 else None)
                E("ve", lambda e: e.tensor_copy(outnT_sb[:, :], PTo[:, 0:1024]),
                  waits=(f"onTmm{t}",), mark=f"onTev{t}")
                for c in range(8):
                    wl = []
                    if c == 0:
                        wl = [f"onTev{t}", "wold"]
                        if t >= 1:
                            wl.append(f"sev{t-1}")
                    E("te", lambda e, c=c: e.matmul(
                        PS[:, 2048:2560],
                        outnT_sb[:, c * 128:(c + 1) * 128],
                        wo_sb[:, c * 1024:c * 1024 + 512],
                        start=(c == 0), stop=(c == 7)),
                      waits=wl)
                    E("te", lambda e, c=c: e.matmul(
                        PS[:, 2560:3072],
                        outnT_sb[:, c * 128:(c + 1) * 128],
                        wo_sb[:, c * 1024 + 512:(c + 1) * 1024],
                        start=(c == 0), stop=(c == 7)),
                      mark=f"wmm{t}" if c == 7 else None)
                E("ve", lambda e: e.tensor_tensor(
                    s_sb[:, :], PS[:, 2048:3072], bo_tile[:, :], ALU.add),
                  waits=(f"wmm{t}",) +
                  ((f"x2st{NT_OWN-1}",) if t == 0 else ()),
                  mark=f"sev{t}")
                E("ve", lambda e: e.tensor_tensor(
                    sg_sb[:, :], s_sb[:, :], g_t[:, :], ALU.mult),
                  mark=f"sgm{t}", waits=(f"gm{t}", f"sev{t}"))
                E("ve", lambda e, t=t, b=b: e.tensor_tensor(
                    res_f[:, :], sg_sb[:, :], xbuf[:, b * D:(b + 1) * D],
                    ALU.add),
                  waits=(f"sgm{t}", f"bxld{t}") +
                  ((f"outst{t-1}",) if t >= 1 else ()),
                  mark=f"resm{t}")
                E("sy", lambda e, t=t: e.dma_start(
                    out=out_d[t * 128:(t + 1) * 128, :], in_=res_f[:, :]),
                  waits=(f"resm{t}",), mark=f"outst{t}", dma=True, sem="syo")
                if debug:
                    E("gp", lambda e, t=t: e.dma_start(
                        out=dbg["dbg_y"][t * 128:(t + 1) * 128, :],
                        in_=y_sb[:, :]),
                      waits=(f"gm{t}",), mark=f"dbgy{t}", dma=True, sem="gdy")
                    E("gp", lambda e, t=t: e.dma_start(
                        out=dbg["dbg_oa"][t * 128:(t + 1) * 128, :],
                        in_=outa_sb[:, :]),
                      waits=(f"osq{t}",), mark=f"dbgoa{t}", dma=True, sem="gdo")
                    E("gp", lambda e, t=t: e.dma_start(
                        out=dbg["dbg_g"][t * 128:(t + 1) * 128, :],
                        in_=g_t[:, :]),
                      waits=(f"resm{t}",), mark=f"dbggt{t}", dma=True,
                      sem="gdg")

            b2A(0)
            for t in range(NT_OWN):
                if t + 1 < NT_OWN:
                    b2A(t + 1)
                b2B(t)

            if debug:
                E("gp", lambda e: e.dma_start(out=dbg["dbg_st2"][:, :],
                                              in_=statet2[:, :]),
                  waits=tuple(f"outst{t}" for t in range(NT_OWN)),
                  mark="dbgst2", dma=True, sem="gpd")
                E("ve", fn=None, waits=("dbgst2",))
            E("ve", fn=None, waits=tuple(f"outst{t}" for t in range(NT_OWN)))
            E("ve", fn=None, waits=(f"qtst{NT_OWN-1}", "wtst31",
                                    f"x2st{NT_OWN-1}"))

        em.serial = SERIAL
        em.mode = "count"
        em.auto = 0
        em.auto_s = 0
        em.prev_mark = None
        prog()
        em.mode = "emit"
        em.sems = _sems

        def mk(eng):
            def body(e):
                em.cur_eng = eng
                em.handles = e
                em.auto = 0
                em.auto_s = 0
                em.prev_mark = None
                prog()
            return body
        blk.sync(mk("sy"))
        blk.gpsimd(mk("gp"))
        blk.scalar(mk("sc"))
        blk.vector(mk("ve"))
        blk.tensor(mk("te"))

    return nc


def kernel(L0_attn_x, L0_attn_Wq, L0_attn_bq, L0_attn_Wk, L0_attn_bk,
           L0_attn_Wv, L0_attn_bv, L0_attn_Wo, L0_attn_bo, L0_attn_ln_g,
           L0_attn_ln_b, L0_ffn_norm1_w, L0_ffn_w1, L0_ffn_act_scale,
           L0_ffn_act_bias, L0_ffn_norm2_w, L0_ffn_w2, _debug=False,
           _build_only=False, _detect_races=False):
    x = np.asarray(L0_attn_x, dtype=np.float32)
    assert x.shape == (2, NTOK, D)

    def f32c(a, shape=None):
        a = np.asarray(a, dtype=np.float32)
        if shape is not None:
            a = a.reshape(shape)
        return np.ascontiguousarray(a)

    common = {
        "wq": f32c(L0_attn_Wq), "wk": f32c(L0_attn_Wk), "wv": f32c(L0_attn_Wv),
        "wo": f32c(L0_attn_Wo), "bq": f32c(L0_attn_bq, (1, D)),
        "bk": f32c(L0_attn_bk), "bv": f32c(L0_attn_bv),
        "bo": f32c(L0_attn_bo, (1, D)), "lng": f32c(L0_attn_ln_g, (1, D)),
        "lnb": f32c(L0_attn_ln_b, (1, D)), "nw1": f32c(L0_ffn_norm1_w, (1, D)),
        "nw2": f32c(L0_ffn_norm2_w, (1, HID)),
        "acts": f32c(L0_ffn_act_scale, (1, 1)),
        "actb": f32c(L0_ffn_act_bias, (1, 1)),
        "w1t": np.ascontiguousarray(np.asarray(L0_ffn_w1, dtype=np.float32).T),
        "w2t": np.ascontiguousarray(np.asarray(L0_ffn_w2, dtype=np.float32).T),
    }
    in_maps = []
    for core in range(8):
        b, slot = divmod(core, 4)
        xb = x[b]
        xrot = np.ascontiguousarray(
            np.concatenate([xb[slot * OWN:(slot + 1) * OWN],
                            xb[:slot * OWN], xb[(slot + 1) * OWN:]], axis=0))
        m = dict(common)
        m["x"] = xrot
        in_maps.append(m)

    nc = build(debug=_debug, detect_races=_detect_races)
    if _build_only:
        return nc, in_maps
    res = run_bass_kernel_spmd(nc, in_maps, core_ids=list(range(8)))
    out = np.empty((2, NTOK, D), dtype=np.float32)
    for core in range(8):
        b, slot = divmod(core, 4)
        out[b, slot * OWN:(slot + 1) * OWN] = np.asarray(res.results[core]["out"])
    return out


# version 26
# speedup vs baseline: 1.1064x; 1.1064x over previous
import numpy as np
import ml_dtypes
from contextlib import ExitStack
import concourse.bass as bass
import concourse.mybir as mybir
from concourse.bass_utils import run_bass_kernel_spmd

F32 = mybir.dt.float32
BF16 = mybir.dt.bfloat16
I8 = mybir.dt.int8
AF = mybir.ActivationFunctionType
ALU = mybir.AluOpType
AX = mybir.AxisListType

NTOK = 8192
OWN = 2048
NT_ALL = 64
NT_OWN = 16
D = 1024
HID = 4096
NH = 16
ISCALE2 = float(2.0 ** -24)

ENGS = ("sy", "gp", "sc", "ve", "te")

PHASE_LIMIT = 6
B2SUB = 2
SERIAL = False

ARENA_BYTES = 65536
CARVES = {
    "wstage":  (1, 2, 0,     16384),
    "xnb":     (2, 2, 16384, 4096),
    "xbb":     (1, 1, 20480, 32768),
    "wdump":   (1, 1, 53248, 8192),
    "fbuf":    (2, 2, 20480, 4096),
    "fbuf4":   (3, 3, 16384, 8192),
    "mrow_f":  (2, 4, 32768, 4096),
    "mrow_b":  (2, 4, 36864, 2048),
    "lngrow":  (0, 0, 20480, 2048),
    "lnbrow":  (0, 0, 22528, 2048),
    "nw2row":  (0, 0, 38912, 16384),
    "nw1row":  (0, 0, 55296, 4096),
    "borow":   (0, 0, 59392, 4096),
    "bqrow":   (0, 0, 63488, 2048),
    "t1s":     (2, 2, 38912, 4096),
    "e1s":     (2, 2, 43008, 4096),
    "xnT":     (2, 2, 51200, 2048),
    "q0s":     (2, 2, 53248, 1024),
    "sAs":     (2, 2, 54272, 2048),
    "sBs":     (2, 2, 56320, 2048),
    "adjs":    (2, 2, 58368, 2048),
    "xq8":     (2, 2, 60416, 1024),
    "qtok":    (2, 2, 61440, 2048),
    "qtile":   (2, 2, 63488, 2048),
    "wsB":     (2, 2, 24576, 4096),
    "wtrn":    (2, 2, 28672, 2048),
    "wkv":     (4, 4, 0,     16384),
    "t_bf":    (4, 4, 16384, 16384),
    "states":  (4, 4, 38912, 8192),
    "wrow_f":  (4, 4, 47104, 4096),
    "wrow_b":  (4, 4, 51200, 2048),
    "bkbv":    (4, 4, 53248, 8192),
    "bvrow":   (4, 4, 61440, 4096),
    "t2f":     (5, 5, 0,     16384),
    "u0":      (5, 5, 16384, 16384),
    "e2h":     (5, 5, 32768, 8192),
    "sA2":     (5, 5, 40960, 4096),
    "sB2":     (5, 5, 45056, 2048),
    "sB2b":    (5, 5, 63488, 2048),
    "q0b":     (5, 5, 47104, 4096),
    "x2b":     (5, 5, 51200, 4096),
    "xqb2":    (5, 5, 55296, 4096),
    "xqT":     (5, 5, 59392, 4096),
    "x2bf2":   (6, 6, 0,     16384),
    "x2T":     (6, 6, 16384, 8192),
    "qtb":     (6, 6, 24576, 4096),
    "ys":      (6, 6, 28672, 4096),
    "th":      (6, 6, 32768, 4096),
    "gh":      (6, 6, 36864, 4096),
    "g_t":     (6, 6, 40960, 4096),
    "outa":    (6, 6, 45056, 4096),
    "outn":    (6, 6, 49152, 2048),
    "outnT":   (6, 6, 51200, 2048),
    "s_s":     (6, 6, 53248, 4096),
    "sg":      (6, 6, 57344, 4096),
    "res":     (6, 6, 61440, 4096),
}


def _check_carves():
    items = list(CARVES.items())
    for i in range(len(items)):
        n1, (l1, h1, o1, b1) = items[i]
        assert o1 + b1 <= ARENA_BYTES, n1
        for j in range(i + 1, len(items)):
            n2, (l2, h2, o2, b2) = items[j]
            if max(l1, l2) <= min(h1, h2):
                assert o1 + b1 <= o2 or o2 + b2 <= o1, f"arena overlap {n1} {n2}"


_check_carves()


class Emitter:

    def __init__(self):
        self.mode = "count"
        self.cnt = {k: 0 for k in ENGS}
        self.marks = {}
        self.cur_eng = None
        self.handles = None
        self.sems = None
        self.auto = 0
        self.serial = False
        self.auto_s = 0
        self.prev_mark = None

    def op(self, eng, fn=None, mark=None, waits=(), dma=False, sem=None):
        key = sem if sem is not None else eng
        if dma and mark is None:
            mark = f"_dma{self.auto}"
            self.auto += 1
        if self.serial and fn is not None:
            if mark is None:
                mark = f"_ser{self.auto_s}"
                self.auto_s += 1
            if self.prev_mark is not None:
                waits = tuple(waits) + (self.prev_mark,)
            self.prev_mark = mark
        if self.mode == "count":
            if mark is not None:
                assert mark not in self.marks, f"dup mark {mark}"
                self.cnt[key] = self.cnt.get(key, 0) + (16 if dma else 1)
                self.marks[mark] = (key, self.cnt[key])
            return
        if eng != self.cur_eng:
            return
        e = self.handles
        last = None
        for w in waits:
            sem_key, val = self.marks[w]
            if last != (sem_key, val):
                e.wait_ge(self.sems[sem_key], val)
            last = (sem_key, val)
        if fn is not None:
            inst = fn(e)
            if mark is not None:
                inst.then_inc(self.sems[key], 16 if dma else 1)


def build(debug=False, detect_races=False):
    nc = bass.Bass("TRN2", target_bir_lowering=False,
                   detect_race_conditions=detect_races)

    x_d = nc.dram_tensor("x", [NTOK, D], F32, kind="ExternalInput").ap()
    wq_d = nc.dram_tensor("wq", [D, D], F32, kind="ExternalInput").ap()
    wk_d = nc.dram_tensor("wk", [2, D, D], F32, kind="ExternalInput").ap()
    wv_d = nc.dram_tensor("wv", [2, D, D], F32, kind="ExternalInput").ap()
    wo_d = nc.dram_tensor("wo", [D, D], F32, kind="ExternalInput").ap()
    bq_d = nc.dram_tensor("bq", [1, D], F32, kind="ExternalInput").ap()
    bk_d = nc.dram_tensor("bk", [2, D], F32, kind="ExternalInput").ap()
    bv_d = nc.dram_tensor("bv", [2, D], F32, kind="ExternalInput").ap()
    bo_d = nc.dram_tensor("bo", [1, D], F32, kind="ExternalInput").ap()
    lng_d = nc.dram_tensor("lng", [1, D], F32, kind="ExternalInput").ap()
    lnb_d = nc.dram_tensor("lnb", [1, D], F32, kind="ExternalInput").ap()
    nw1_d = nc.dram_tensor("nw1", [1, D], F32, kind="ExternalInput").ap()
    nw2_d = nc.dram_tensor("nw2", [1, HID], F32, kind="ExternalInput").ap()
    acts_d = nc.dram_tensor("acts", [1, 1], F32, kind="ExternalInput").ap()
    actb_d = nc.dram_tensor("actb", [1, 1], F32, kind="ExternalInput").ap()
    w1t_d = nc.dram_tensor("w1t", [D, HID], F32, kind="ExternalInput").ap()
    w2t_d = nc.dram_tensor("w2t", [HID, D], F32, kind="ExternalInput").ap()
    out_d = nc.dram_tensor("out", [OWN, D], F32, kind="ExternalOutput").ap()

    ikind = "ExternalOutput" if debug else "Internal"
    xnbf_d = nc.dram_tensor("xnbf", [NTOK, D], BF16, kind=ikind).ap()
    xq_d = nc.dram_tensor("xqd", [OWN, D], I8, kind=ikind).ap()
    qt_d = nc.dram_tensor("qtd", [128, 8 * OWN], BF16, kind=ikind).ap()
    x2_d = nc.dram_tensor("x2d", [OWN, HID], I8, kind=ikind).ap()
    w1q_d = nc.dram_tensor("w1qd", [D, HID], I8, kind=ikind).ap()
    w2q_d = nc.dram_tensor("w2qd", [HID, D], I8, kind=ikind).ap()
    mrow_d = nc.dram_tensor("mrowd", [1, D], BF16, kind=ikind).ap()

    dbg = {}
    if debug:
        for name, shape, dt in [
            ("dbg_G", [128, 8 * D], BF16), ("dbg_m", [1, D], F32),
            ("dbg_stp", [64, D], F32),
            ("dbg_r", [NT_OWN * 128, HID], F32),
            ("dbg_y", [NT_OWN * 128, D], F32),
            ("dbg_oa", [NT_OWN * 128, D], F32),
            ("dbg_g", [NT_OWN * 128, D], F32),
            ("dbg_st2", [128, 1024], BF16),
        ]:
            dbg[name] = nc.dram_tensor(name, shape, dt, kind="ExternalOutput").ap()

    id_dram = nc.inline_tensor(np.eye(128, dtype=ml_dtypes.bfloat16), name="identbf")
    ones_dram = nc.inline_tensor(np.ones((1, 128), dtype=ml_dtypes.bfloat16), name="onesbf")
    onesf_dram = nc.inline_tensor(np.ones((1, 128), dtype=np.float32), name="onesf")
    onesbc_dram = nc.inline_tensor(np.ones((128, 1), dtype=ml_dtypes.bfloat16), name="onesbc")
    onesfc_dram = nc.inline_tensor(np.ones((128, 1), dtype=np.float32), name="onesfc")

    (C_A, C_B, C_BOA, C_EPSLN, C_EPSRMS, C_TH1, C_TH2, C_HALF, C_NHALF,
     C_S1W, C_S2W, C_NTH1, C_NTH2, C_EPSQ, C_A2, C_AB2, C_EB, C_S2F,
     C_S1F, C_T0) = range(20)

    em = Emitter()
    es = ExitStack()
    with es:
        def T(name, shape, dtype):
            return es.enter_context(nc.sbuf_tensor(name, shape, dtype))

        id_sb = T("id_sb", [128, 128], BF16)
        ones_sb = T("ones_sb", [1, 128], BF16)
        onesf_sb = T("onesf_sb", [1, 128], F32)
        onesbc_sb = T("onesbc_sb", [128, 1], BF16)
        onesfc_sb = T("onesfc_sb", [128, 1], F32)
        scal_f = T("scal_f", [1, 16], F32)
        cols_f = T("cols_f", [128, 20], F32)
        rq1_cols = T("rq1_cols", [128, NT_OWN], F32)
        scol_cols = T("scol_cols", [128, NT_OWN], F32)
        c1_cols = T("c1_cols", [128, NT_OWN], F32)
        ysc_cols = T("ysc_cols", [128, NT_OWN], F32)
        lncols = T("lncols", [128, 12], F32)
        qcols = T("qcols", [128, 8], F32)
        wstat = T("wstat", [128, 32], F32)
        stat_s = T("stat_s", [128, NT_ALL], F32)
        stat_r = T("stat_r", [128, NT_ALL], F32)
        mean_all = T("mean_all", [128, NT_ALL], F32)
        var_all = T("var_all", [128, NT_ALL], F32)
        rstd_all = T("rstd_all", [128, NT_ALL], F32)
        negm_all = T("negm_all", [128, NT_ALL], F32)
        scr8 = T("scr8", [128, 8], F32)
        rsq16 = T("rsq16", [128, NT_OWN], F32)
        nw1b_tile = T("nw1b", [128, D], BF16)
        mcol_b = T("mcol_b", [128, 8], BF16)
        statet_b = T("statet_b", [64, D], BF16)
        statet2 = T("statet2", [128, 1024], BF16)
        bq_tile = T("bq_tile", [128, D], BF16)
        lng_tile = T("lng_tile", [128, D], BF16)
        lnb_tile = T("lnb_tile", [128, D], BF16)
        bo_tile = T("bo_tile", [128, D], F32)
        nw1_tile = T("nw1_tile", [128, D], F32)
        anw2_tile = T("anw2_tile", [128, HID], F32)
        xbuf = T("xbuf", [128, 4 * D], F32)
        sqscr = T("sqscr", [128, 2048], F32)
        wq_sb = T("wq_sb", [128, 8 * D], BF16)
        w1q_sb = T("w1q_sb", [128, 8 * HID], BF16)
        arena = T("arena", [128, ARENA_BYTES // 2], BF16)

        STREAMS = ("sy", "gp", "sc", "ve", "te", "syp", "syo", "gpl",
                   "gpt", "gpm", "gpd", "gpp",
                   "syx0", "syx1", "syf0", "syf1", "syw0", "syw1",
                   "sy20", "sy21", "syq0", "syq1", "gpn0", "gpn1",
                   "gpw0", "gpw1", "gpxq0", "gpxq1", "gpqt0", "gpqt1",
                   "gpx20", "gpx21", "gdy", "gdo", "gdg", "gdr",
                   "gpb0", "gpb1", "syx2", "syx3", "syf2", "syf3")
        _sems = {k: es.enter_context(nc.semaphore("sem_" + k)) for k in STREAMS}
        PS = es.enter_context(nc.psum_tensor("PS", [128, 4096], F32))
        blk = es.enter_context(nc.Block())

        def carve(name, rows, cols, dtype):
            lo, hi, off, nbytes = CARVES[name]
            esz = {F32: 4, BF16: 2, I8: 1}[dtype]
            assert cols * esz <= nbytes, (name, cols, esz, nbytes)
            v = arena[:, off // 2:(off + cols * esz) // 2]
            v = v.bitcast(dtype)
            return v[0:rows, :]

        wstage = carve("wstage", 128, 4096, F32)
        xnb = carve("xnb", 128, 2 * D, BF16)
        xbb = carve("xbb", 128, 16 * D, BF16)
        wdump = carve("wdump", 128, 2048, F32)
        fbuf = carve("fbuf", 128, 2 * D, BF16)
        fbuf4 = carve("fbuf4", 128, 4 * D, BF16)
        mrow_f = carve("mrow_f", 1, D, F32)
        mrow_b = carve("mrow_b", 1, D, BF16)
        lngrow = carve("lngrow", 1, D, BF16)
        lnbrow = carve("lnbrow", 1, D, BF16)
        nw2row = carve("nw2row", 1, HID, F32)
        nw1row = carve("nw1row", 1, D, F32)
        borow = carve("borow", 1, D, F32)
        bqrow = carve("bqrow", 1, D, BF16)
        t1b = carve("t1s", 128, D, BF16)
        e1s = carve("e1s", 128, D, F32)
        xnT = carve("xnT", 128, D, BF16)
        q0s = carve("q0s", 128, D, I8)
        sAs = carve("sAs", 128, D, BF16)
        sBs = carve("sBs", 128, D, BF16)
        adjs = carve("adjs", 128, D, BF16)
        xq8 = carve("xq8", 128, D, I8)
        qtok = carve("qtok", 128, D, BF16)
        qtile = carve("qtile", 128, D, BF16)
        wsB = carve("wsB", 128, 2048, BF16)
        wtrn = carve("wtrn", 128, 2048, I8)
        wkv_sb = carve("wkv", 128, 8 * D, BF16)
        t_bf = carve("t_bf", 128, 8 * D, BF16)
        states_sb = carve("states", 64, 2 * D, F32)
        wrow_f = carve("wrow_f", 1, D, F32)
        wrow_b = carve("wrow_b", 1, D, BF16)
        bkbv_b = carve("bkbv", 1, 4 * D, BF16)
        bvrow = carve("bvrow", 1, D, F32)
        t2f_sb = carve("t2f", 128, HID, F32)
        u0_sb = carve("u0", 128, HID, F32)
        e2h_sb = carve("e2h", 128, 2048, F32)
        sA2_sb = carve("sA2", 128, 2 * D, BF16)
        sB2_sb = carve("sB2", 128, D, BF16)
        sB2b_sb = carve("sB2b", 128, D, BF16)
        q0b_sb = carve("q0b", 128, HID, I8)
        x2b_sb = carve("x2b", 128, HID, I8)
        xqb2 = carve("xqb2", 128, 2 * D, BF16)
        xqT_sb = carve("xqT", 128, 2 * D, BF16)
        x2bf2 = carve("x2bf2", 128, 2 * HID, BF16)
        x2T_sb = carve("x2T", 128, HID, BF16)
        qtb = carve("qtb", 128, 2 * D, BF16)
        y_sb = carve("ys", 128, D, F32)
        th_sb = carve("th", 128, D, F32)
        gh_sb = carve("gh", 128, D, F32)
        g_t = carve("g_t", 128, D, F32)
        outa_sb = carve("outa", 128, D, F32)
        outn_sb = carve("outn", 128, D, BF16)
        outnT_sb = carve("outnT", 128, D, BF16)
        s_sb = carve("s_s", 128, D, F32)
        sg_sb = carve("sg", 128, D, F32)
        res_f = carve("res", 128, D, F32)

        wsA = sqscr[:, :].bitcast(BF16)[:, 0:2048]
        g_bf = w1q_sb[:, 0:8 * D]
        r_sb = wq_sb[:, 0:8 * D].bitcast(F32)
        w2q_sb = w1q_sb[:, 0:8 * HID]
        wo_sb = wq_sb[:, 0:8 * D]
        PT = PS[:, 3584:4096].bitcast(BF16)
        PT2 = PS[:, 3072:4096].bitcast(BF16)
        PTo = PS[:, 1024:2048].bitcast(BF16)

        def prog():
            E = em.op

            def de(*ws):
                return ws if detect_races else ()

            p0loads = [
                (id_sb[:, :], id_dram.ap()[:, :]),
                (ones_sb[:, :], ones_dram.ap()[:, :]),
                (onesf_sb[:, :], onesf_dram.ap()[:, :]),
                (onesbc_sb[:, :], onesbc_dram.ap()[:, :]),
                (onesfc_sb[:, :], onesfc_dram.ap()[:, :]),
                (nw1row[:, :], nw1_d[:, :]),
                (borow[:, :], bo_d[:, :]),
                (nw2row[:, :], nw2_d[:, :]),
                (scal_f[0:1, 0:1], acts_d[:, :]),
                (scal_f[0:1, 1:2], actb_d[:, :]),
            ]
            for i, (dst, src) in enumerate(p0loads):
                E("sy", lambda e, d=dst, s=src: e.dma_start(out=d, in_=s),
                  mark="p0ld" if i == len(p0loads) - 1 else None, dma=True,
                  sem="syp")
            p0casts = [
                (bqrow[:, :], bq_d[:, :]),
                (lngrow[:, :], lng_d[:, :]),
                (lnbrow[:, :], lnb_d[:, :]),
            ]
            for i, (dst, src) in enumerate(p0casts):
                E("gp", lambda e, d=dst, s=src: e.dma_start(out=d, in_=s),
                  mark="p0g" if i == len(p0casts) - 1 else None, dma=True,
                  sem="gpl")
            wq_view = wq_d.rearrange("(c p) j -> p c j", p=128)
            E("gp", lambda e: e.dma_start(
                out=wq_sb[:, :].rearrange("p (c j) -> p c j", c=8), in_=wq_view),
              mark="wqld", waits=("p0g",), dma=True, sem="gpl")

            def k1mm(e, outsl, lhsT, rhs, st):
                return e.matmul(outsl, lhsT, rhs, start=st, stop=True)

            for i, (row, dst) in enumerate([(bqrow, bq_tile), (lngrow, lng_tile),
                                            (lnbrow, lnb_tile)]):
                for h in range(2):
                    E("te", lambda e, i=i, h=h, row=row: k1mm(
                        e, PS[:, i * 1024 + h * 512:i * 1024 + (h + 1) * 512],
                        ones_sb[0:1, :], row[0:1, h * 512:(h + 1) * 512], st=True),
                      mark=f"bcmm{i}_{h}",
                      waits=("p0ld", "p0g") if (i == 0 and h == 0) else ())
                E("ve", lambda e, i=i, dst=dst: e.tensor_copy(
                    dst[:, :], PS[:, i * 1024:(i + 1) * 1024]),
                  mark=f"bcev{i}", waits=(f"bcmm{i}_0", f"bcmm{i}_1"))
            E("te", lambda e: k1mm(e, PS[:, 3072:3584], onesf_sb[0:1, :],
                                   nw1row[0:1, 0:512], True),
              mark="nw1a", waits=("p0ld",))
            E("te", lambda e: k1mm(e, PS[:, 0:512], onesf_sb[0:1, :],
                                   nw1row[0:1, 512:1024], True),
              mark="nw1b", waits=("bcev0",))
            E("ve", lambda e: e.tensor_copy(nw1_tile[:, 0:512], PS[:, 3072:3584]),
              mark="nw1ev1", waits=("nw1a",))
            E("ve", lambda e: e.tensor_copy(nw1_tile[:, 512:1024], PS[:, 0:512]),
              mark="nw1ev2", waits=("nw1b",))
            E("ve", lambda e: e.tensor_copy(nw1b_tile[:, :], nw1_tile[:, :]),
              mark="nw1bf", waits=("nw1ev2",))
            E("te", lambda e: k1mm(e, PS[:, 512:1024], onesf_sb[0:1, :],
                                   borow[0:1, 0:512], True),
              mark="boa", waits=("bcev0",))
            E("te", lambda e: k1mm(e, PS[:, 1024:1536], onesf_sb[0:1, :],
                                   borow[0:1, 512:1024], True),
              mark="bob", waits=("bcev1",))
            E("ve", lambda e: e.tensor_copy(bo_tile[:, 0:512], PS[:, 512:1024]),
              mark="boev1", waits=("boa",))
            E("ve", lambda e: e.tensor_copy(bo_tile[:, 512:1024], PS[:, 1024:1536]),
              mark="boev2", waits=("bob",))
            E("te", lambda e: k1mm(e, PS[:, 2560:2561], onesf_sb[0:1, :],
                                   scal_f[0:1, 0:1], True),
              mark="acol", waits=("bcev2", "p0ld"))
            E("te", lambda e: k1mm(e, PS[:, 2561:2562], onesf_sb[0:1, :],
                                   scal_f[0:1, 1:2], True), mark="bcol")
            E("ve", lambda e: e.tensor_copy(cols_f[:, C_A:C_B + 1],
                                            PS[:, 2560:2562]),
              mark="abev", waits=("acol", "bcol"))
            E("ve", lambda e: e.reciprocal(cols_f[:, C_BOA:C_BOA + 1],
                                           cols_f[:, C_A:C_A + 1]),
              mark="rcp0", waits=("abev",))
            E("ve", lambda e: e.tensor_tensor(cols_f[:, C_BOA:C_BOA + 1],
                                              cols_f[:, C_BOA:C_BOA + 1],
                                              cols_f[:, C_B:C_B + 1], ALU.mult),
              mark="boacol", waits=("rcp0",))
            E("ve", lambda e: e.tensor_tensor(
                cols_f[:, C_A2:C_A2 + 1], cols_f[:, C_A:C_A + 1],
                cols_f[:, C_A:C_A + 1], ALU.mult),
              mark="a2c", waits=("abev",))
            E("ve", lambda e: e.tensor_tensor(
                cols_f[:, C_AB2:C_AB2 + 1], cols_f[:, C_A:C_A + 1],
                cols_f[:, C_B:C_B + 1], ALU.mult),
              mark="ab2a")
            E("ve", lambda e: e.tensor_scalar(
                cols_f[:, C_AB2:C_AB2 + 1], cols_f[:, C_AB2:C_AB2 + 1],
                2.0, None, ALU.mult),
              mark="ab2c", waits=("ab2a",))
            E("ve", lambda e: e.tensor_tensor(
                cols_f[:, C_EB:C_EB + 1], cols_f[:, C_B:C_B + 1],
                cols_f[:, C_B:C_B + 1], ALU.mult),
              mark="eba")
            E("ve", lambda e: e.tensor_scalar(
                cols_f[:, C_EB:C_EB + 1], cols_f[:, C_EB:C_EB + 1],
                1e-8, None, ALU.add),
              mark="ebc", waits=("eba",))
            for cslot, val in [(C_EPSLN, 1e-5), (C_EPSRMS, 1e-8),
                               (C_HALF, 0.5), (C_NHALF, -0.5), (C_EPSQ, 1e-5)]:
                E("ve", lambda e, c=cslot, v=val: e.memset(cols_f[:, c:c + 1], v),
                  mark=f"ms{cslot}")
            for c in range(8):
                E("ve", lambda e, c=c: e.tensor_tensor(
                    wq_sb[:, c * 1024 + c * 128:c * 1024 + (c + 1) * 128],
                    wq_sb[:, c * 1024 + c * 128:c * 1024 + (c + 1) * 128],
                    id_sb[:, :], ALU.add),
                  waits=("wqld", "p0ld") if c == 0 else (),
                  mark="wqf" if c == 7 else None)
            for h in range(8):
                E("te", lambda e, h=h: k1mm(
                    e, PS[:, 3072:3584], onesf_sb[0:1, :],
                    nw2row[0:1, h * 512:(h + 1) * 512], True),
                  mark=f"nw2mm{h}",
                  waits=("nw1ev1",) if h == 0 else (f"nw2ev{h-1}",))
                E("ve", lambda e, h=h: e.tensor_scalar(
                    anw2_tile[:, h * 512:(h + 1) * 512], PS[:, 3072:3584],
                    cols_f[:, C_A:C_A + 1], None, ALU.mult),
                  mark=f"nw2ev{h}", waits=(f"nw2mm{h}", "abev"))

            if PHASE_LIMIT < 1:
                E("ve", fn=None, waits=("nw2ev7", "boacol"))
                return
            NG = NT_ALL // 8
            for g in range(NG):
                for j in range(8):
                    t = g * 8 + j
                    b = t % 4
                    w_ld = [f"sq{t-4}", f"cst{t-4}"] if t >= 4 else ["p0ld"]
                    E("sy" if t % 2 == 0 else "sc",
                      lambda e, t=t, b=b: e.dma_start(
                        out=xbuf[:, b * D:(b + 1) * D],
                        in_=x_d[t * 128:(t + 1) * 128, :]),
                      mark=f"xld{t}", waits=w_ld, dma=True, sem=f"syx{t%4}")
                    E("ve", lambda e, t=t, b=b: e.tensor_tensor(
                        sqscr[:, (b % 2) * D:(b % 2 + 1) * D],
                        xbuf[:, b * D:(b + 1) * D],
                        xbuf[:, b * D:(b + 1) * D], ALU.mult),
                      mark=f"sqm{t}",
                      waits=(f"xld{t}",) +
                      (de(f"sq{t-2}") if t >= 2 else ()))
                    E("ve", lambda e, t=t, b=b: e.reduce_sum(
                        stat_r[:, t:t + 1], xbuf[:, b * D:(b + 1) * D], AX.X),
                      mark=f"rd{t}", waits=(f"xld{t}",))
                    E("ve", lambda e, t=t, b=b: e.reduce_sum(
                        stat_s[:, t:t + 1],
                        sqscr[:, (b % 2) * D:(b % 2 + 1) * D], AX.X),
                      mark=f"sq{t}", waits=(f"sqm{t}",))
                    E("ve", lambda e, t=t, b=b: e.tensor_copy(
                        xbb[:, (t % 16) * D:(t % 16 + 1) * D],
                        xbuf[:, b * D:(b + 1) * D]),
                      mark=f"cst{t}",
                      waits=(f"rd{t}",) +
                      (("nw2mm7",) if t < 2 else ()) +
                      ((f"gram{t-16}", f"xnst{t-16}") if t >= 16 else ()))
                for i in range(4 * g, 4 * g + 4):
                    if i < 16:
                        wsrc = w1t_d[(i // 2) * 128:(i // 2 + 1) * 128,
                                     (i % 2) * 2048:(i % 2 + 1) * 2048]
                        wdst0 = wstage[:, (i % 2) * 2048:(i % 2 + 1) * 2048]
                    else:
                        wsrc = w2t_d.rearrange("(a p) d -> p a d", p=128)[
                            :, (i - 16) * 2:(i - 16) * 2 + 2, :]
                        wdst0 = wstage[:, (i % 2) * 2048:(i % 2 + 1) * 2048
                                       ].rearrange("p (a d) -> p a d", a=2)
                    E("gp", lambda e, w=wsrc, d0=wdst0: e.dma_start(
                        out=d0, in_=w),
                      mark=f"wld{i}",
                      waits=(f"wstat{i-2}",) if i >= 2 else (),
                      dma=True, sem=f"gpw{i%2}")
                    E("sc", lambda e, i=i: e.activation(
                        wdump[:, :],
                        wstage[:, (i % 2) * 2048:(i % 2 + 1) * 2048],
                        AF.Abs, accum_out=wstat[:, i:i + 1]),
                      mark=f"wstat{i}",
                      waits=(f"wld{i}",) +
                      (de(f"wstat{i-1}") if i >= 1 else ("nw2mm7",)))
                sl = slice(g * 8, g * 8 + 8)
                E("ve", lambda e, sl=sl: e.tensor_scalar(
                    mean_all[:, sl], stat_r[:, sl], 1.0 / D, None, ALU.mult),
                  mark=f"ga{g}", waits=(f"rd{g*8+7}",))
                E("ve", lambda e, sl=sl: e.tensor_scalar(
                    var_all[:, sl], stat_s[:, sl], 1.0 / D, None, ALU.mult),
                  mark=f"gb{g}", waits=(f"sq{g*8+7}",))
                E("ve", lambda e, sl=sl: e.tensor_tensor(
                    scr8[:, :], mean_all[:, sl], mean_all[:, sl], ALU.mult),
                  mark=f"gc{g}", waits=(f"ga{g}",))
                E("ve", lambda e, sl=sl: e.tensor_tensor(
                    var_all[:, sl], var_all[:, sl], scr8[:, :], ALU.subtract),
                  mark=f"gd{g}", waits=(f"gb{g}", f"gc{g}"))
                E("sc", lambda e, sl=sl: e.activation(
                    rstd_all[:, sl], var_all[:, sl], AF.Ln,
                    bias=cols_f[:, C_EPSLN:C_EPSLN + 1]),
                  mark=f"ge{g}", waits=(f"gd{g}",))
                E("sc", lambda e, sl=sl: e.activation(
                    rstd_all[:, sl], rstd_all[:, sl], AF.Exp, scale=-0.5),
                  mark=f"gf{g}", waits=de(f"ge{g}"))
                E("ve", lambda e, sl=sl: e.scalar_tensor_tensor(
                    negm_all[:, sl], mean_all[:, sl], -1.0, rstd_all[:, sl],
                    ALU.mult, ALU.mult),
                  mark=f"cmg{g}", waits=(f"gf{g}",))
                for j in range(8):
                    t = g * 8 + j
                    s16 = (t % 16) * D
                    E("ve", lambda e, t=t, s16=s16: e.tensor_scalar(
                        xbb[:, s16:s16 + D], xbb[:, s16:s16 + D],
                        rstd_all[:, t:t + 1], negm_all[:, t:t + 1],
                        ALU.mult, ALU.add),
                      mark=f"nx{t}", waits=(f"cst{t}", f"cmg{g}"))
                    for c in range(4):
                        for h in range(2):
                            E("te", lambda e, t=t, s16=s16, c=c, h=h: e.matmul(
                                PS[:, (2 * c + h) * 512:(2 * c + h + 1) * 512],
                                xbb[:, s16 + c * 128:s16 + (c + 1) * 128],
                                xbb[:, s16 + h * 512:s16 + (h + 1) * 512],
                                start=(t == 0), stop=(t == NT_ALL - 1)),
                              mark=f"gram{t}" if (c == 3 and h == 1) else None,
                              waits=((f"nx{t}",) +
                                     (("nw2ev7",) if t == 0 else ()))
                              if (c == 0 and h == 0) else ())
                    E("sy" if t % 2 == 0 else "sc",
                      lambda e, t=t, s16=s16: e.dma_start(
                        out=xnbf_d[t * 128:(t + 1) * 128, :],
                        in_=xbb[:, s16:s16 + D]),
                      mark=f"xnst{t}",
                      waits=(f"nx{t}",) +
                      ((f"xnst{t-2}",) if t >= 2 else ()),
                      dma=True, sem=f"gpn{t%2}")

            for c in range(4):
                E("ve", lambda e, c=c: e.tensor_copy(
                    g_bf[:, c * 1024:(c + 1) * 1024],
                    PS[:, c * 1024:(c + 1) * 1024]),
                  mark=f"gh1ev{c}", waits=(f"gram{NT_ALL-1}",) if c == 0 else ())

            E("ve", lambda e: e.reduce_sum(qcols[:, 0:1], wstat[:, 0:16], AX.X),
              waits=("wstat15",), mark="w1red")
            E("ve", lambda e: e.reduce_sum(qcols[:, 1:2], wstat[:, 16:32], AX.X),
              waits=("wstat31",), mark="w2red")
            E("te", lambda e: e.matmul(PS[0:1, 2048:2049], onesfc_sb[:, 0:1],
                                       qcols[:, 0:1], start=True, stop=True),
              waits=("w1red", "gh1ev3"), mark="w1s")
            E("te", lambda e: e.matmul(PS[0:1, 2049:2050], onesfc_sb[:, 0:1],
                                       qcols[:, 1:2], start=True, stop=True),
              waits=("w2red",), mark="w2s")
            E("ve", lambda e: e.tensor_copy(scal_f[0:1, 2:4], PS[0:1, 2048:2050]),
              waits=("w1s", "w2s"), mark="wsev")
            E("ve", lambda e: e.tensor_scalar(scal_f[0:1, 2:4], scal_f[0:1, 2:4],
                                              1.0 / (D * HID), 1e-5, ALU.mult,
                                              ALU.max), mark="wclip",
              waits=("wsev",))
            E("te", lambda e: e.matmul(PS[:, 2048:2049], onesf_sb[0:1, :],
                                       scal_f[0:1, 2:3], start=True, stop=True),
              waits=("wsev", "wclip"), mark="s1b")
            E("te", lambda e: e.matmul(PS[:, 2049:2050], onesf_sb[0:1, :],
                                       scal_f[0:1, 3:4], start=True, stop=True),
              mark="s2b")
            E("ve", lambda e: e.tensor_copy(cols_f[:, C_S1W:C_S2W + 1],
                                            PS[:, 2048:2050]),
              waits=("s1b", "s2b"), mark="swcols")
            E("ve", lambda e: e.tensor_scalar(cols_f[:, C_S2F:C_S2F + 1],
                                              cols_f[:, C_S2W:C_S2W + 1],
                                              0.5 / 127.0, None, ALU.mult),
              mark="s2fc", waits=("swcols",))
            E("ve", lambda e: e.tensor_scalar(cols_f[:, C_S1F:C_S1F + 1],
                                              cols_f[:, C_S1W:C_S1W + 1],
                                              0.5, None, ALU.mult),
              mark="s1fc")
            E("ve", lambda e: e.tensor_scalar(cols_f[:, C_TH1:C_TH1 + 1],
                                              cols_f[:, C_S1W:C_S1W + 1],
                                              0.5, None, ALU.mult),
              mark="th1c", waits=("swcols",))
            E("ve", lambda e: e.tensor_scalar(cols_f[:, C_TH2:C_TH2 + 1],
                                              cols_f[:, C_S2W:C_S2W + 1],
                                              0.5, None, ALU.mult),
              mark="th2c")
            E("ve", lambda e: e.tensor_scalar(cols_f[:, C_NTH1:C_NTH1 + 1],
                                              cols_f[:, C_TH1:C_TH1 + 1],
                                              -1.0, None, ALU.mult),
              waits=("th1c",))
            E("ve", lambda e: e.tensor_scalar(cols_f[:, C_NTH2:C_NTH2 + 1],
                                              cols_f[:, C_TH2:C_TH2 + 1],
                                              -1.0, None, ALU.mult),
              mark="thcols", waits=("th2c",))

            if PHASE_LIMIT < 2:
                E("ve", fn=None, waits=("thcols", f"gram{NT_ALL-1}",
                                        f"xnst{NT_ALL-2}", f"xnst{NT_ALL-1}",
                                        "wstat31"))
                return
            E("ve", lambda e: e.tensor_scalar(
                mean_all[:, 0:NT_OWN], var_all[:, 0:NT_OWN], 1e-5, None,
                ALU.add),
              mark="v2a", waits=("cmg1",))
            E("ve", lambda e: e.reciprocal(
                negm_all[:, 0:NT_OWN], mean_all[:, 0:NT_OWN]),
              mark="v2b", waits=("v2a", f"nx{NT_OWN-1}"))
            E("ve", lambda e: e.tensor_tensor(
                mean_all[:, 0:NT_OWN], var_all[:, 0:NT_OWN],
                negm_all[:, 0:NT_OWN], ALU.mult),
              mark="v2c", waits=("v2b",))
            E("sc", lambda e: e.activation(
                rsq16[:, :], mean_all[:, 0:NT_OWN], AF.Ln,
                bias=cols_f[:, C_EPSRMS:C_EPSRMS + 1]),
              mark="v2d", waits=("v2c",))
            E("sc", lambda e: e.activation(
                rsq16[:, :], rsq16[:, :], AF.Exp, scale=-0.5),
              mark="v2e", waits=de("v2d"))
            def tern_chunk(i):
                nth = C_NTH1 if i < 16 else C_NTH2
                pth = C_TH1 if i < 16 else C_TH2
                if i < 16:
                    wsrc = w1t_d[(i // 2) * 128:(i // 2 + 1) * 128,
                                 (i % 2) * 2048:(i % 2 + 1) * 2048]
                    wdst = w1q_d[(i // 2) * 128:(i // 2 + 1) * 128,
                                 (i % 2) * 2048:(i % 2 + 1) * 2048]
                    wsg0 = wstage[:, (i % 2) * 2048:(i % 2 + 1) * 2048]
                    wtr0 = wtrn[:, :]
                else:
                    wsrc = w2t_d.rearrange("(a p) d -> p a d", p=128)[
                        :, (i - 16) * 2:(i - 16) * 2 + 2, :]
                    wdst = w2q_d.rearrange("(a p) d -> p a d", p=128)[
                        :, (i - 16) * 2:(i - 16) * 2 + 2, :]
                    wsg0 = wstage[:, (i % 2) * 2048:(i % 2 + 1) * 2048
                                  ].rearrange("p (a d) -> p a d", a=2)
                    wtr0 = wtrn[:, :].rearrange("p (a d) -> p a d", a=2)
                E("sc" if i % 2 == 0 else "sy",
                  lambda e, w=wsrc, d0=wsg0: e.dma_start(out=d0, in_=w),
                  waits=(f"wtrn{i-2}",) if i >= 2 else ("thcols",),
                  mark=f"vwld{i}", dma=True, sem=f"syw{i%2}")
                E("sc", lambda e, i=i, nth=nth: e.sign(
                    wsA[:, :], wstage[:, (i % 2) * 2048:(i % 2 + 1) * 2048],
                    bias=cols_f[:, nth:nth + 1]),
                  mark=f"wsga{i}",
                  waits=(f"vwld{i}",) + ((f"wtrn{i-1}",) if i >= 1 else ()))
                E("sc", lambda e, i=i, pth=pth: e.sign(
                    wsB[:, :], wstage[:, (i % 2) * 2048:(i % 2 + 1) * 2048],
                    bias=cols_f[:, pth:pth + 1]),
                  mark=f"wsgn{i}")
                E("ve", lambda e: e.tensor_tensor(
                    wtrn[:, :], wsA[:, :], wsB[:, :], ALU.add),
                  waits=(f"wsga{i}", f"wsgn{i}") +
                  ((f"wtst{i-1}",) if i >= 1 else ()),
                  mark=f"wtrn{i}")
                E("gp", lambda e, w=wdst, s0=wtr0: e.dma_start(out=w, in_=s0),
                  waits=(f"wtrn{i}",), mark=f"wtst{i}", dma=True, sem="gpt")

            mseq = [0]

            def m_mms(src_ap, first, wmarks, last=False):
                for h in range(2):
                    E("te", lambda e, s=src_ap, h=h, f=first, l=last: e.matmul(
                        PS[0:1, 1024 + h * 512:1024 + (h + 1) * 512],
                        onesbc_sb[:, 0:1], s[:, h * 512:(h + 1) * 512],
                        start=f, stop=l),
                      mark=f"mmm{mseq[0]}" if h == 1 else None,
                      waits=wmarks if h == 0 else ())
                mseq[0] += 1

            for t in range(NT_OWN):
                b = t % 2
                w_ld = ([f"mmm{4*(t-2)}", f"t1m{t-2}"] if t >= 2
                        else [f"xnst{NT_ALL-2}", f"xnst{NT_ALL-1}",
                              f"gram{NT_ALL-1}"])
                E("sy", lambda e, t=t, b=b: e.dma_start(
                    out=xnb[:, b * D:(b + 1) * D],
                    in_=xnbf_d[t * 128:(t + 1) * 128, :]),
                  mark=f"jxld{t}", waits=w_ld, dma=True, sem=f"syx{t%2}")
                E("ve", lambda e, t=t, b=b: e.tensor_tensor(
                    t1b[:, :], xnb[:, b * D:(b + 1) * D], nw1b_tile[:, :],
                    ALU.mult),
                  waits=(f"jxld{t}",) +
                  (("nw1bf", f"gram{NT_ALL-1}") if t == 0
                   else (f"q0m{t-1}", f"em{t-1}")),
                  mark=f"t1m{t}")
                E("ve", lambda e: e.reduce_max(
                    qcols[:, 4:5], t1b[:, :], AX.X, apply_absolute_value=True),
                  mark=f"am1{t}", waits=(f"t1m{t}",))
                E("ve", lambda e, t=t: e.tensor_tensor(
                    qcols[:, 5:6], qcols[:, 4:5], rsq16[:, t:t + 1], ALU.mult),
                  mark=f"q5a{t}", waits=("v2e", f"am1{t}"))
                E("ve", lambda e: e.tensor_scalar(
                    qcols[:, 5:6], qcols[:, 5:6], 1e-5, None, ALU.max),
                  mark=f"q5b{t}", waits=(f"q5a{t}",))
                E("ve", lambda e, t=t: e.tensor_scalar(
                    rq1_cols[:, t:t + 1], qcols[:, 5:6], 1.0 / 127.0, None,
                    ALU.mult),
                  waits=(f"q5b{t}",))
                E("ve", lambda e: e.reciprocal(qcols[:, 6:7], qcols[:, 5:6]),
                  mark=f"rc1{t}")
                E("ve", lambda e: e.tensor_scalar(
                    qcols[:, 6:7], qcols[:, 6:7], 127.0, None, ALU.mult),
                  mark=f"q6b{t}", waits=(f"rc1{t}",))
                E("ve", lambda e, t=t: e.tensor_tensor(
                    c1_cols[:, t:t + 1], rsq16[:, t:t + 1], qcols[:, 6:7],
                    ALU.mult),
                  mark=f"c1m{t}", waits=(f"q6b{t}",))
                E("sc", lambda e, t=t: e.activation(
                    q0s[:, :], t1b[:, :], AF.Copy, scale=c1_cols[:, t:t + 1]),
                  waits=(f"c1m{t}",) +
                  ((f"xqm{t-1}",) if t >= 1 else de("wstat31")),
                  mark=f"q0m{t}")
                E("ve", lambda e, t=t: e.scalar_tensor_tensor(
                    e1s[:, :], t1b[:, :], c1_cols[:, t:t + 1], q0s[:, :],
                    ALU.mult, ALU.subtract),
                  waits=(f"q0m{t}", f"c1m{t}") +
                  ((f"sgn1{t-1}",) if t >= 1 else ()),
                  mark=f"em{t}")
                E("sc", lambda e: e.sign(sAs[:, :], e1s[:, :],
                                         bias=cols_f[:, C_NHALF:C_NHALF + 1]),
                  mark=f"sga{t}", waits=(f"em{t}",))
                E("sc", lambda e: e.sign(sBs[:, :], e1s[:, :],
                                         bias=cols_f[:, C_HALF:C_HALF + 1]),
                  mark=f"sgn1{t}")
                E("ve", lambda e: e.tensor_tensor(
                    adjs[:, :], sAs[:, :], sBs[:, :], ALU.add),
                  mark=f"adm{t}",
                  waits=(f"sga{t}", f"sgn1{t}") +
                  ((f"xqst{t-1}",) if t >= 1 else ()))
                E("ve", lambda e: e.scalar_tensor_tensor(
                    xq8[:, :], adjs[:, :], 0.5, q0s[:, :], ALU.mult, ALU.add),
                  mark=f"xqm{t}", waits=(f"adm{t}",))
                E("gp", lambda e, t=t: e.dma_start(
                    out=xq_d[t * 128:(t + 1) * 128, :], in_=xq8[:, :]),
                  waits=(f"xqm{t}",), mark=f"xqst{t}", dma=True, sem=f"gpxq{t%2}")
                for i in range(8):
                    E("te", lambda e, b=b, i=i: e.transpose(
                        PT[:, i * 128:(i + 1) * 128],
                        xnb[:, b * D + i * 128:b * D + (i + 1) * 128],
                        id_sb[:, :]),
                      waits=((f"jxld{t}",) + ((f"qTev{t-1}",) if t >= 1
                                              else ("gh1ev3",)))
                      if i == 0 else (),
                      mark=f"xnTmm{t}" if i == 7 else None)
                E("ve", lambda e: e.tensor_copy(xnT[:, :], PT[:, 0:1024]),
                  waits=(f"xnTmm{t}",), mark=f"xnTev{t}")
                for c in range(8):
                    for h in range(2):
                        E("te", lambda e, c=c, h=h: e.matmul(
                            PS[:, h * 512:(h + 1) * 512],
                            xnT[:, c * 128:(c + 1) * 128],
                            wq_sb[:, c * 1024 + h * 512:c * 1024 + (h + 1) * 512],
                            start=(c == 0), stop=(c == 7)),
                          waits=((f"xnTev{t}", "wqf") +
                                 ((f"qev{t-1}",) if t >= 1 else ()))
                          if (c == 0 and h == 0) else (),
                          mark=f"qmm{t}" if (c == 7 and h == 1) else None)
                E("ve", lambda e: e.tensor_tensor(
                    qtok[:, :], PS[:, 0:1024], bq_tile[:, :], ALU.add),
                  waits=(f"qmm{t}",), mark=f"qev{t}")
                for i in range(8):
                    E("te", lambda e, i=i: e.transpose(
                        PT[:, i * 128:(i + 1) * 128],
                        qtok[:, i * 128:(i + 1) * 128], id_sb[:, :]),
                      waits=(f"qev{t}", f"xnTev{t}") if i == 0 else (),
                      mark=f"qTmm{t}" if i == 7 else None)
                E("ve", lambda e: e.tensor_copy(qtile[:, :], PT[:, 0:1024]),
                  waits=(f"qTmm{t}",) + ((f"qtst{t-1}",) if t >= 1 else ()),
                  mark=f"qTev{t}")
                E("gp", lambda e, t=t: e.dma_start(
                    out=qt_d.rearrange("p (c u) -> p c u", c=8)[
                        :, :, t * 128:(t + 1) * 128],
                    in_=qtile[:, :].rearrange("p (c u) -> p c u", c=8)),
                  waits=(f"qTev{t}",), mark=f"qtst{t}", dma=True, sem=f"gpqt{t%2}")
                tern_chunk(t)
                m_mms(xnb[:, b * D:(b + 1) * D], first=(t == 0),
                      wmarks=(f"jxld{t}",))
                for j in range(3):
                    u = NT_OWN + t * 3 + j
                    fb = u % 2
                    E("sy", lambda e, u=u, fb=fb: e.dma_start(
                        out=fbuf[:, fb * D:(fb + 1) * D],
                        in_=xnbf_d[u * 128:(u + 1) * 128, :]),
                      waits=(f"mmm{mseq[0]-2}",) if mseq[0] >= 2 else (),
                      mark=f"fld{u}", dma=True, sem=f"syf{u%2}")
                    m_mms(fbuf[:, fb * D:(fb + 1) * D], first=False,
                          wmarks=(f"fld{u}",),
                          last=(t == NT_OWN - 1 and j == 2))

            E("ve", lambda e: e.tensor_copy(mrow_f[0:1, :], PS[0:1, 1024:2048]),
              waits=(f"mmm{mseq[0]-1}",), mark="mev")
            E("ve", lambda e: e.tensor_copy(mrow_b[0:1, :], mrow_f[0:1, :]),
              mark="mbf", waits=("mev",))
            E("gp", lambda e: e.dma_start(out=mrow_d[:, :], in_=mrow_b[0:1, :]),
              waits=("mbf",), mark="mst", dma=True, sem="gpm")
            for c in range(8):
                E("gp", lambda e, c=c: e.dma_start(
                    out=mcol_b[:, c:c + 1],
                    in_=mrow_d[0:1, c * 128:(c + 1) * 128]),
                  waits=("mst",) if c == 0 else (), mark=f"mcol{c}", dma=True,
                  sem="gpm")
            if debug:
                E("gp", lambda e: e.dma_start(out=dbg["dbg_m"][:, :],
                                              in_=mrow_f[0:1, :]),
                  waits=("mev",), mark="dbgm", dma=True, sem="gpd")

            if PHASE_LIMIT < 3:
                E("ve", fn=None, waits=("mbf", f"qtst{NT_OWN-1}",
                                        f"xqst{NT_OWN-1}", f"mcol7"))
                return
            for r in range(2):
                for i in range(8):
                    k = r * 8 + i
                    f = 4 + k // 4
                    cb = k % 4
                    E("te", lambda e, i=i, f=f, cb=cb: e.transpose(
                        PT[:, i * 128:(i + 1) * 128],
                        g_bf[:, cb * 1024 + f * 128:cb * 1024 + (f + 1) * 128],
                        id_sb[:, :]),
                      waits=(("gh1ev3", f"qTev{NT_OWN-1}") if (r == 0 and i == 0)
                             else (("symev0",) if (r == 1 and i == 0) else ())),
                      mark=f"symt{r}" if i == 7 else None)
                fo = 4 + r * 2
                E("ve", lambda e, fo=fo: e.tensor_copy(
                    g_bf[:, fo * 1024:fo * 1024 + 512], PT[:, 0:512]),
                  waits=(f"symt{r}",))
                E("ve", lambda e, fo=fo: e.tensor_copy(
                    g_bf[:, (fo + 1) * 1024:(fo + 1) * 1024 + 512],
                    PT[:, 512:1024]),
                  mark=f"symev{r}")
            for u in range(NT_ALL):
                fb = u % 4
                E("sc" if u % 2 == 0 else "sy",
                  lambda e, u=u, fb=fb: e.dma_start(
                    out=fbuf4[:, fb * 512:(fb + 1) * 512],
                    in_=xnbf_d[u * 128:(u + 1) * 128, 512:1024]),
                  waits=(f"g2{u-4}",) if u >= 4 else (f"mmm{mseq[0]-1}",),
                  mark=f"f2ld{u}", dma=True, sem=f"syf{u%4}")
                for c in range(4):
                    E("te", lambda e, u=u, fb=fb, c=c: e.matmul(
                        PS[:, c * 512:(c + 1) * 512],
                        fbuf4[:, fb * 512 + c * 128:fb * 512 + (c + 1) * 128],
                        fbuf4[:, fb * 512:(fb + 1) * 512],
                        start=(u == 0), stop=(u == NT_ALL - 1)),
                      waits=((f"f2ld{u}", "mev", f"qTev{NT_OWN-1}")
                             if (c == 0 and u == 0) else
                             ((f"f2ld{u}",) if c == 0 else ())),
                      mark=f"g2{u}" if c == 3 else None)
                if u % 4 == 0:
                    tern_chunk(16 + u // 4)
            for c in range(4, 8):
                E("ve", lambda e, c=c: e.tensor_copy(
                    g_bf[:, c * 1024 + 512:(c + 1) * 1024],
                    PS[:, (c - 4) * 512:(c - 3) * 512]),
                  mark=f"gh2ev{c}",
                  waits=(f"g2{NT_ALL-1}", "symev1") if c == 4 else ())
            if debug:
                E("gp", lambda e: e.dma_start(out=dbg["dbg_G"][:, :],
                                              in_=g_bf[:, :]),
                  waits=("gh2ev7",), mark="dbgg", dma=True, sem="gpd")

            if PHASE_LIMIT < 4:
                E("ve", fn=None, waits=("gh2ev7", "wtst31"))
                return
            for idx, (dst, src) in enumerate([
                    (bkbv_b[0:1, 0:D], bk_d[0:1, :]),
                    (bkbv_b[0:1, D:2 * D], bk_d[1:2, :]),
                    (bkbv_b[0:1, 2 * D:3 * D], bv_d[0:1, :]),
                    (bkbv_b[0:1, 3 * D:4 * D], bv_d[1:2, :])]):
                E("gp", lambda e, d=dst, s=src: e.dma_start(out=d, in_=s),
                  waits=("wtst31", f"xqst{NT_OWN-1}") if idx == 0 else (),
                  mark=f"bkbvld{idx}", dma=True, sem="gpl")
            for k in range(2):
                E("sy", lambda e, k=k: e.dma_start(out=bvrow[0:1, :],
                                                   in_=bv_d[k:k + 1, :]),
                  waits=("vwld31", f"qtst{NT_OWN-1}") if k == 0
                  else (f"wradd{k-1}",),
                  mark=f"bvrowld{k}", dma=True, sem="syp")
                wvv = wv_d.rearrange("t (c p) j -> t p c j", p=128)[k]
                wkk = wk_d.rearrange("t (c p) j -> t p c j", p=128)[k]
                E("gp", lambda e, w=wvv: e.dma_start(
                    out=wkv_sb[:, :].rearrange("p (c j) -> p c j", c=8), in_=w),
                  waits=(f"smm{k-1}_{NH-1}",) if k >= 1 else ("bkbvld3",),
                  mark=f"wvld{k}", dma=True, sem="gpl")
                for c in range(8):
                    E("ve", lambda e, c=c: e.tensor_tensor(
                        wkv_sb[:, c * 1024 + c * 128:c * 1024 + (c + 1) * 128],
                        wkv_sb[:, c * 1024 + c * 128:c * 1024 + (c + 1) * 128],
                        id_sb[:, :], ALU.add),
                      waits=(f"wvld{k}", "gh2ev7") if c == 0 else (),
                      mark=f"wvf{k}" if c == 7 else None)
                for c in range(8):
                    for h in range(2):
                        E("te", lambda e, c=c, h=h: e.matmul(
                            PS[0:1, 2560 + h * 512:2560 + (h + 1) * 512],
                            mcol_b[:, c:c + 1],
                            wkv_sb[:, c * 1024 + h * 512:c * 1024 + (h + 1) * 512],
                            start=(c == 0), stop=(c == 7)),
                          waits=(f"wvf{k}", "mcol7", "swcols")
                          if (c == 0 and h == 0) else (),
                          mark=f"wrmm{k}" if (c == 7 and h == 1) else None)
                E("ve", lambda e, k=k: e.tensor_copy(wrow_f[0:1, :],
                                                PS[0:1, 2560:3584]),
                  waits=(f"wrmm{k}", f"bvrowld{k}"), mark=f"wrev{k}")
                E("ve", lambda e, k=k: e.scalar_tensor_tensor(
                    wrow_f[0:1, :], bvrow[0:1, :],
                    float(NTOK), wrow_f[0:1, :], ALU.mult, ALU.add),
                  mark=f"wradd{k}", waits=(f"wrev{k}",))
                E("ve", lambda e: e.tensor_copy(wrow_b[0:1, :], wrow_f[0:1, :]),
                  mark=f"wrbf{k}", waits=(f"wradd{k}",))
                for c in range(8):
                    pp = (c % 2) * 1024
                    prevt = (f"tev{k}_{c-2}",) if c >= 2 else \
                        ((f"tev{k-1}_{c+6}",) if k >= 1 else ())
                    for h in range(2):
                        E("te", lambda e, c=c, h=h, pp=pp, k=k: e.matmul(
                            PS[:, pp + h * 512:pp + (h + 1) * 512],
                            mrow_b[0:1, c * 128:(c + 1) * 128],
                            bkbv_b[0:1, (2 + k) * D + h * 512:
                                   (2 + k) * D + (h + 1) * 512],
                            start=True, stop=False),
                          waits=((f"wvf{k}", "mbf") + prevt) if h == 0 else ())
                    for f in range(8):
                        for h in range(2):
                            E("te", lambda e, c=c, f=f, h=h, pp=pp: e.matmul(
                                PS[:, pp + h * 512:pp + (h + 1) * 512],
                                g_bf[:, f * 1024 + c * 128:f * 1024 + (c + 1) * 128],
                                wkv_sb[:, f * 1024 + h * 512:
                                       f * 1024 + (h + 1) * 512],
                                start=False, stop=(f == 7)),
                              mark=f"tmm{k}_{c}" if (f == 7 and h == 1) else None)
                    E("ve", lambda e, c=c, pp=pp, k=k: e.tensor_copy(
                        t_bf[:, c * 1024:(c + 1) * 1024], PS[:, pp:pp + 1024]),
                      waits=(f"tmm{k}_{c}",), mark=f"tev{k}_{c}")
                E("gp", lambda e, w=wkk: e.dma_start(
                    out=wkv_sb[:, :].rearrange("p (c j) -> p c j", c=8), in_=w),
                  waits=(f"tmm{k}_7", f"wrmm{k}"),
                  mark=f"wkld{k}", dma=True, sem="gpl")
                for c in range(8):
                    E("ve", lambda e, c=c: e.tensor_tensor(
                        wkv_sb[:, c * 1024 + c * 128:c * 1024 + (c + 1) * 128],
                        wkv_sb[:, c * 1024 + c * 128:c * 1024 + (c + 1) * 128],
                        id_sb[:, :], ALU.add),
                      waits=(f"wkld{k}", f"tev{k}_7") if c == 0 else (),
                      mark=f"wkf{k}" if c == 7 else None)
                for h in range(NH):
                    E("te", lambda e, h=h, k=k: e.matmul(
                        PS[0:64, 2048 + h * 64:2048 + (h + 1) * 64],
                        bkbv_b[0:1, k * D + h * 64:k * D + (h + 1) * 64],
                        wrow_b[0:1, h * 64:(h + 1) * 64],
                        start=True, stop=False),
                      waits=(f"wrbf{k}", f"wkf{k}") if h == 0 else ())
                    for cc in range(8):
                        E("te", lambda e, h=h, cc=cc: e.matmul(
                            PS[0:64, 2048 + h * 64:2048 + (h + 1) * 64],
                            wkv_sb[:, cc * 1024 + h * 64:cc * 1024 + h * 64 + 64],
                            t_bf[:, cc * 1024 + h * 64:cc * 1024 + h * 64 + 64],
                            start=False, stop=(cc == 7)),
                          mark=f"smm{k}_{h}" if cc == 7 else None)
                E("ve", lambda e, k=k: e.tensor_copy(
                    states_sb[0:64, k * D:(k + 1) * D], PS[0:64, 2048:3072]),
                  waits=tuple(f"smm{k}_{h}" for h in range(NH)) +
                  (("wrev0",) if k == 0 else ()),
                  mark=f"stev{k}")
            E("ve", lambda e: e.tensor_tensor(
                states_sb[0:64, 0:D], states_sb[0:64, 0:D],
                states_sb[0:64, D:2 * D], ALU.mult),
              waits=("stev0", "stev1"), mark="stp1")
            E("ve", lambda e: e.tensor_scalar(
                statet_b[0:64, :], states_sb[0:64, 0:D], ISCALE2, None, ALU.mult),
              mark="stp", waits=("stp1",))
            E("ve", lambda e: e.memset(statet2[:, :], 0.0), mark="st2z")
            E("ve", lambda e: e.tensor_copy(
                statet2[0:64, 0:512].rearrange("p (j u) -> p j u", j=8),
                statet_b[0:64, :].rearrange("p (j x u) -> p j x u",
                                            j=8, x=2)[:, :, 0, :]),
              waits=("stp", "st2z"), mark="st2e")
            for j in range(8):
                E("te", lambda e, j=j: e.matmul(
                    PS[64:128, 2048 + j * 64:2048 + (j + 1) * 64],
                    id_sb[0:64, 0:64],
                    statet_b[0:64, j * 128 + 64:j * 128 + 128],
                    start=True, stop=True),
                  waits=("stp",) if j == 0 else (),
                  mark="st2mm" if j == 7 else None)
            E("ve", lambda e: e.tensor_copy(statet2[64:128, 512:1024],
                                            PS[64:128, 2048:2560]),
              waits=("st2mm",), mark="st2o")
            if debug:
                E("gp", lambda e: e.dma_start(out=dbg["dbg_stp"][:, :],
                                              in_=states_sb[0:64, 0:D]),
                  waits=("stp1",), mark="dbgstp", dma=True, sem="gpd")

            if PHASE_LIMIT < 5:
                E("ve", fn=None, waits=("st2o", "st2e"))
                return
            for c in range(8):
                E("gp", lambda e, c=c: e.dma_start(
                    out=w1q_sb[:, c * HID:(c + 1) * HID],
                    in_=w1q_d[c * 128:(c + 1) * 128, :]),
                  waits=(("stev1",) + (("dbgstp",) if debug else ()))
                  if c == 0 else (),
                  mark=f"w1qld{c}", dma=True, sem="gpl")
            E("ve", lambda e: e.tensor_scalar(
                scol_cols[:, :], rq1_cols[:, :],
                cols_f[:, C_S1F:C_S1F + 1], None, ALU.mult),
              mark="scolc", waits=("s1fc", f"c1m{NT_OWN-1}"))

            def b1Aload(t):
                b = t % 2
                E("gp", lambda e, t=t, b=b: e.dma_start(
                    out=xqb2[:, b * D:(b + 1) * D],
                    in_=xq_d[t * 128:(t + 1) * 128, :]),
                  waits=(f"xqst{t}",) + ((f"xqTmm{t-2}",) if t >= 2
                                         else ("stp",)),
                  mark=f"xqld{t}", dma=True, sem=f"gpb{t%2}")
                for i in range(8):
                    E("te", lambda e, b=b, i=i: e.transpose(
                        PT[:, i * 128:(i + 1) * 128],
                        xqb2[:, b * D + i * 128:b * D + (i + 1) * 128],
                        id_sb[:, :]),
                      waits=((f"xqld{t}",) +
                             ((f"xqTev{t-1}",) if t >= 1 else ("st2o",)) +
                             ((f"mm1_{t-2}_2",) if t >= 2 else ()))
                      if i == 0 else (),
                      mark=f"xqTmm{t}" if i == 7 else None)
                E("ve", lambda e, b=b: e.tensor_copy(
                    xqT_sb[:, b * D:(b + 1) * D], PT[:, 0:1024]),
                  waits=(f"xqTmm{t}",), mark=f"xqTev{t}")

            def b1Amm(t):
                b = t % 2
                groups = [(0, 3, 0), (3, 3, 1536), (6, 2, 0)]
                for g, (nq0, nn, off) in enumerate(groups):
                    for c in range(8):
                        for j in range(nn):
                            nq = nq0 + j
                            wl = []
                            if c == 0 and j == 0:
                                wl = [f"xqTev{t}", "w1qld7"]
                                if g == 2:
                                    wl.append(f"rev{t}_0")
                                elif t >= 1:
                                    wl.append(f"rev{t-1}_2" if g == 0
                                              else f"rev{t-1}_{g}")
                            E("te", lambda e, c=c, nq=nq, off=off, j=j,
                              b=b: e.matmul(
                                PS[:, off + j * 512:off + (j + 1) * 512],
                                xqT_sb[:, b * D + c * 128:
                                       b * D + (c + 1) * 128],
                                w1q_sb[:, c * HID + nq * 512:
                                       c * HID + (nq + 1) * 512],
                                start=(c == 0), stop=(c == 7)),
                              waits=wl,
                              mark=f"mm1_{t}_{g}" if (c == 7 and j == nn - 1)
                              else None)

            def b1S(t):
                b = t % 2
                gws = [(0, 1536, 0), (3, 1536, 1536), (6, 1024, 0)]
                for g, (nq0, gw, off) in enumerate(gws):
                    E("sc", lambda e, g=g, off=off, gw=gw, nq0=nq0,
                      t=t: e.activation(
                        r_sb[:, nq0 * 512:nq0 * 512 + gw], PS[:, off:off + gw],
                        AF.Relu, scale=scol_cols[:, t:t + 1]),
                      waits=(f"mm1_{t}_{g}",) +
                      (("scolc",) if (t == 0 and g == 0) else ()) +
                      ((f"u0m{t-1}_1",) if (t >= 1 and g == 0) else ()),
                      mark=f"rev{t}_{g}")
                if debug:
                    E("gp", lambda e, t=t: e.dma_start(
                        out=dbg["dbg_r"][t * 128:(t + 1) * 128, :],
                        in_=r_sb[:, 0:HID]),
                      waits=(f"rev{t}_2",), mark=f"dbgr{t}", dma=True,
                      sem="gdr")
                E("sc", lambda e: e.activation(
                    u0_sb[:, :], r_sb[:, :], AF.Square,
                    accum_out=qcols[:, 0:1]),
                  waits=(f"rev{t}_2",) +
                  ((f"u0p{t-1}", f"t2m{t-1}") if t >= 1 else ()),
                  mark=f"sqa{t}_0")
                E("sc", lambda e: e.activation(
                    r_sb[:, :], u0_sb[:, :], AF.Square,
                    accum_out=qcols[:, 2:3]),
                  mark=f"u0m{t}_1",
                  waits=de(f"sqa{t}_0") +
                  ((f"dbgr{t}",) if debug else ()))
            def b1V(t):
                b = t % 2
                E("ve", lambda e: e.tensor_tensor(
                    qcols[:, 2:3], qcols[:, 2:3], cols_f[:, C_A2:C_A2 + 1],
                    ALU.mult),
                  mark=f"k2{t}", waits=(f"u0m{t}_1",) +
                  (("a2c", "ebc") if t == 0 else ()))
                E("ve", lambda e: e.tensor_scalar(
                    u0_sb[:, :], u0_sb[:, :], cols_f[:, C_BOA:C_BOA + 1],
                    None, ALU.add),
                  mark=f"u0p{t}")
                E("ve", lambda e: e.scalar_tensor_tensor(
                    qcols[:, 0:1], qcols[:, 0:1], cols_f[:, C_AB2:C_AB2 + 1],
                    qcols[:, 2:3], ALU.mult, ALU.add),
                  mark=f"msm{t}", waits=de(f"k2{t}"))
                E("sc", lambda e: e.activation(
                    qcols[:, 4:5], qcols[:, 0:1], AF.Ln, scale=1.0 / HID,
                    bias=cols_f[:, C_EB:C_EB + 1]),
                  mark=f"ln2q{t}", waits=(f"msm{t}",))
                E("sc", lambda e: e.activation(
                    qcols[:, 4:5], qcols[:, 4:5], AF.Exp, scale=-0.5),
                  mark=f"rs2{t}", waits=de(f"ln2q{t}"))
                E("ve", lambda e: e.tensor_tensor(
                    t2f_sb[:, :], u0_sb[:, :], anw2_tile[:, :], ALU.mult),
                  waits=de(f"u0p{t}") + ((f"q0bm{t-1}",) if t >= 1 else ()),
                  mark=f"t2m{t}")
                E("ve", lambda e: e.memset(scr8[:, 0:1], 0.0))
                E("ve", lambda e: e.reduce_max(
                    qcols[:, 5:6], t2f_sb[:, :], AX.X, apply_absolute_value=True),
                  waits=de(f"t2m{t}"), mark=f"am2{t}")
                E("ve", lambda e: e.memset(scr8[:, 1:2], 0.0))
                E("ve", lambda e: e.tensor_scalar(
                    qcols[:, 5:6], qcols[:, 5:6], qcols[:, 4:5], 1e-5,
                    ALU.mult, ALU.max),
                  mark=f"yb{t}", waits=(f"rs2{t}",) + de(f"am2{t}"))
                E("ve", lambda e: e.reciprocal(qcols[:, 6:7], qcols[:, 5:6]),
                  mark=f"rc2{t}", waits=(f"yb{t}",))
                E("ve", lambda e, t=t: e.tensor_tensor(
                    ysc_cols[:, t:t + 1], qcols[:, 5:6],
                    cols_f[:, C_S2F:C_S2F + 1], ALU.mult),
                  mark=f"ysc{t}", waits=de(f"yb{t}") +
                  (("s2fc",) if t == 0 else ()))
                E("ve", lambda e: e.scalar_tensor_tensor(
                    qcols[:, 6:7], qcols[:, 6:7], 127.0, qcols[:, 4:5],
                    ALU.mult, ALU.mult),
                  mark=f"c2m{t}", waits=de(f"rc2{t}"))
                E("ve", lambda e: e.tensor_scalar(
                    q0b_sb[:, :], t2f_sb[:, :], qcols[:, 6:7], None, ALU.mult),
                  waits=(f"c2m{t}",) + ((f"x2m{t-1}_3",) if t >= 1 else ()),
                  mark=f"q0bm{t}")
            def b1Q(t):
                b = t % 2
                for hh in range(2):
                    E("ve", lambda e, hh=hh: e.scalar_tensor_tensor(
                        e2h_sb[:, :], t2f_sb[:, hh * 2048:(hh + 1) * 2048],
                        qcols[:, 6:7], q0b_sb[:, hh * 2048:(hh + 1) * 2048],
                        ALU.mult, ALU.subtract),
                      waits=(f"q0bm{t}",) +
                      ((f"sg2a{t}_1", f"sg2{t}_1") if hh >= 1 else ()),
                      mark=f"e2m{t}_{hh}")
                    for qq in range(2):
                        qi = hh * 2 + qq
                        p = qi % 2
                        sgb = sB2_sb if p == 0 else sB2b_sb
                        prevq = ((f"x2m{t}_{qi-2}",) if qi >= 2 else
                                 ((f"x2m{t-1}_{qi+2}",) if t >= 1 else ()))
                        E("sc", lambda e, qq=qq, p=p: e.sign(
                            sA2_sb[:, p * D:(p + 1) * D],
                            e2h_sb[:, qq * 1024:(qq + 1) * 1024],
                            bias=cols_f[:, C_NHALF:C_NHALF + 1]),
                          mark=f"sg2a{t}_{qi}",
                          waits=(f"e2m{t}_{hh}",) + prevq)
                        E("sc", lambda e, qq=qq, sgb=sgb: e.sign(
                            sgb[:, :], e2h_sb[:, qq * 1024:(qq + 1) * 1024],
                            bias=cols_f[:, C_HALF:C_HALF + 1]),
                          mark=f"sg2{t}_{qi}")
                        E("ve", lambda e, sgb=sgb, p=p: e.tensor_tensor(
                            sgb[:, :], sA2_sb[:, p * D:(p + 1) * D],
                            sgb[:, :], ALU.add),
                          mark=f"xma{t}_{qi}",
                          waits=(f"sg2a{t}_{qi}", f"sg2{t}_{qi}") +
                          ((f"x2st{t-1}",) if (qi == 0 and t >= 1) else ()))
                        E("ve", lambda e, qi=qi, sgb=sgb: e.scalar_tensor_tensor(
                            x2b_sb[:, qi * 1024:(qi + 1) * 1024], sgb[:, :],
                            0.5, q0b_sb[:, qi * 1024:(qi + 1) * 1024],
                            ALU.mult, ALU.add),
                          mark=f"x2m{t}_{qi}",
                          waits=(f"xma{t}_{qi}",))
                E("gp", lambda e, t=t: e.dma_start(
                    out=x2_d[t * 128:(t + 1) * 128, :], in_=x2b_sb[:, :]),
                  waits=(f"x2m{t}_3",), mark=f"x2st{t}", dma=True, sem=f"gpx2{t%2}")

            b1Aload(0)
            b1Aload(1)
            b1Amm(0)
            b1S(0)
            b1V(0)
            for t in range(NT_OWN):
                if t + 2 < NT_OWN:
                    b1Aload(t + 2)
                if t + 1 < NT_OWN:
                    b1Amm(t + 1)
                    b1S(t + 1)
                b1Q(t)
                if t + 1 < NT_OWN:
                    b1V(t + 1)

            if PHASE_LIMIT < 6:
                E("ve", fn=None, waits=(f"x2st{NT_OWN-1}",))
                return
            for c in range(4):
                E("gp", lambda e, c=c: e.dma_start(
                    out=w2q_sb[:, c * 8 * D:(c + 1) * 8 * D].rearrange(
                        "p (a j) -> p a j", a=8),
                    in_=w2q_d.rearrange("(a p) j -> p a j", p=128)[
                        :, c * 8:(c + 1) * 8, :]),
                  waits=(f"x2st{NT_OWN-1}",) if c == 0
                  else (f"w2qld{c-1}",),
                  mark=f"w2qld{c}", dma=True, sem="gpl")
            E("gp", lambda e: e.dma_start(
                out=wo_sb[:, :].rearrange("p (c j) -> p c j", c=8),
                in_=wo_d.rearrange("(c p) j -> p c j", p=128)),
              mark="wold", waits=("w2qld3",), dma=True, sem="gpl")
            def b2A(t):
                b = t % 2
                s8 = b * HID
                E("gp", lambda e, t=t, s8=s8: e.dma_start(
                    out=x2bf2[:, s8:s8 + HID],
                    in_=x2_d[t * 128:(t + 1) * 128, :]),
                  waits=(f"x2st{t}",) +
                  ((f"x2Tmm{t-2}_1",) if t >= 2 else
                   (f"e2m{NT_OWN-1}_1", f"t2m{NT_OWN-1}")),
                  mark=f"x2ld{t}", dma=True, sem=f"gpx2{t%2}")
                E("sy", lambda e, t=t, b=b: e.dma_start(
                    out=qtb[:, b * D:(b + 1) * D].rearrange(
                        "p (c u) -> p c u", c=8),
                    in_=qt_d.rearrange("p (c u) -> p c u", c=8)[
                        :, :, t * 128:(t + 1) * 128]),
                  waits=((f"omm{t-2}",) if (t >= 2 and B2SUB >= 2) else ()) +
                  ((f"t2m{NT_OWN-1}",) if t < 2 else ()),
                  mark=f"qtld{t}", dma=True, sem=f"syq{t%2}")
                E("sy", lambda e, t=t, b=b: e.dma_start(
                    out=xbuf[:, b * D:(b + 1) * D],
                    in_=x_d[t * 128:(t + 1) * 128, :]),
                  waits=(f"resm{t-2}",) if t >= 2 else (),
                  mark=f"bxld{t}", dma=True, sem=f"syx{t%2}")
                for bt in range(2):
                    for i in range(16):
                        w0 = ()
                        if i == 0:
                            w0 = (f"x2ld{t}",)
                            if bt == 1:
                                w0 += (f"x2Tev{t}_0",)
                            elif t >= 1:
                                w0 += (f"x2Tev{t-1}_1", f"mm2_{t-1}")
                            else:
                                w0 += ("st2o",)
                        E("te", lambda e, bt=bt, i=i, s8=s8: e.transpose(
                            PT2[:, i * 128:(i + 1) * 128],
                            x2bf2[:, s8 + bt * 2048 + i * 128:
                                  s8 + bt * 2048 + (i + 1) * 128],
                            id_sb[:, :]),
                          waits=w0,
                          mark=f"x2Tmm{t}_{bt}" if i == 15 else None)
                    E("ve", lambda e, bt=bt: e.tensor_copy(
                        x2T_sb[:, bt * 2048:(bt + 1) * 2048], PT2[:, 0:2048]),
                      waits=(f"x2Tmm{t}_{bt}",) +
                      ((f"x2m{NT_OWN-1}_3", f"t2m{NT_OWN-1}")
                       if (t == 0 and bt == 0) else ()),
                      mark=f"x2Tev{t}_{bt}")
            def b2Amm(t):
                b = t % 2
                for cc in range(32):
                    wl = []
                    if t == 0 and cc % 8 == 0:
                        wl.append(f"w2qld{cc//8}")
                    if cc == 0:
                        wl += [f"x2Tev{t}_1"]
                        wl.append(f"yev{t-1}" if t >= 1 else f"rev{NT_OWN-1}_2")
                    E("te", lambda e, cc=cc: e.matmul(
                        PS[:, 0:512],
                        x2T_sb[:, cc * 128:(cc + 1) * 128],
                        w2q_sb[:, cc * 1024:cc * 1024 + 512],
                        start=(cc == 0), stop=(cc == 31)),
                      waits=wl)
                    E("te", lambda e, cc=cc: e.matmul(
                        PS[:, 512:1024],
                        x2T_sb[:, cc * 128:(cc + 1) * 128],
                        w2q_sb[:, cc * 1024 + 512:(cc + 1) * 1024],
                        start=(cc == 0), stop=(cc == 31)),
                      mark=f"mm2_{t}" if cc == 31 else None)

            def b2Batt(t):
                b = t % 2
                for h in range(NH if B2SUB >= 2 else 0):
                    wl = []
                    if h == 0:
                        wl = [f"qtld{t}", "st2o"]
                        if t >= 1:
                            wl.append(f"onTev{t-1}")
                    E("te", lambda e, h=h, b=b: e.matmul(
                        PS[:, 1024 + h * 64:1024 + (h + 1) * 64],
                        qtb[0:128,
                            b * D + (h // 2) * 128:b * D + (h // 2) * 128 + 128],
                        statet2[0:128,
                                (h % 2) * 512 + (h // 2) * 64:
                                (h % 2) * 512 + (h // 2) * 64 + 64],
                        start=True, stop=True),
                      waits=wl, mark=f"omm{t}" if h == NH - 1 else None)

            def b2B(t):
                b = t % 2
                E("sc", lambda e, t=t: e.activation(
                    y_sb[:, :], PS[:, 0:1024], AF.Copy,
                    scale=ysc_cols[:, t:t + 1]),
                  waits=(f"mm2_{t}", f"ysc{t}") +
                  ((f"gm{t-1}",) if t >= 1
                   else (f"t2m{NT_OWN-1}", f"sg2{NT_OWN-1}_3")) +
                  ((f"dbgy{t-1}",) if (debug and t >= 1) else ()),
                  mark=f"yev{t}")
                E("sc", lambda e: e.activation(
                    th_sb[:, :], y_sb[:, :], AF.Tanh, scale=0.5),
                  mark=f"thm{t}", waits=de(f"yev{t}"))
                E("ve", lambda e: e.tensor_scalar(
                    gh_sb[:, :], th_sb[:, :], 0.5, 0.5, ALU.mult, ALU.add),
                  mark=f"ghm{t}", waits=(f"thm{t}",))
                E("ve", lambda e: e.tensor_tensor(
                    g_t[:, :], gh_sb[:, :], y_sb[:, :], ALU.mult),
                  mark=f"gm{t}",
                  waits=(f"ghm{t}",) +
                  ((f"dbggt{t-1}",) if (debug and t >= 1) else ()))
                if B2SUB < 2:
                    E("ve", lambda e, t=t, b=b: e.tensor_tensor(
                        res_f[:, :], g_t[:, :], xbuf[:, b * D:(b + 1) * D],
                        ALU.add),
                      waits=(f"bxld{t}", f"gm{t}") +
                      ((f"outst{t-1}",) if t >= 1 else ()),
                      mark=f"resm{t}")
                    E("sy", lambda e, t=t: e.dma_start(
                        out=out_d[t * 128:(t + 1) * 128, :], in_=res_f[:, :]),
                      waits=(f"resm{t}",), mark=f"outst{t}", dma=True, sem="syo")
                    return
                E("ve", lambda e: e.tensor_copy(
                    outa_sb[:, :], PS[:, 1024:2048]),
                  waits=(f"omm{t}",) +
                  ((f"dbgoa{t-1}",) if (debug and t >= 1)
                   else ((f"x2m{NT_OWN-1}_3",) if t == 0 else ())),
                  mark=f"oev{t}")
                E("ve", lambda e: e.reduce_sum(
                    lncols[:, 9:10], PS[:, 1024:2048], AX.X),
                  mark=f"orow{t}")
                E("ve", lambda e: e.tensor_tensor(
                    sqscr[:, 0:D], outa_sb[:, :], outa_sb[:, :], ALU.mult),
                  mark=f"osqm{t}", waits=(f"oev{t}",))
                E("ve", lambda e: e.tensor_scalar(
                    lncols[:, 4:5], lncols[:, 9:10], 1.0 / D, None, ALU.mult),
                  mark=f"oca{t}", waits=(f"orow{t}",))
                E("ve", lambda e: e.reduce_sum(
                    lncols[:, 10:11], sqscr[:, 0:D], AX.X),
                  mark=f"osq{t}", waits=(f"osqm{t}",))
                E("ve", lambda e: e.scalar_tensor_tensor(
                    lncols[:, 6:7], lncols[:, 4:5], 1.0, lncols[:, 4:5],
                    ALU.bypass, ALU.mult),
                  mark=f"occ{t}", waits=(f"oca{t}",))
                E("ve", lambda e: e.tensor_scalar(
                    lncols[:, 5:6], lncols[:, 10:11], 1.0 / D, None, ALU.mult),
                  mark=f"ocb{t}", waits=(f"osq{t}",))
                E("ve", lambda e: e.tensor_tensor(
                    lncols[:, 6:7], lncols[:, 5:6], lncols[:, 6:7], ALU.subtract),
                  mark=f"ocm{t}", waits=(f"ocb{t}", f"occ{t}"))
                E("sc", lambda e: e.activation(
                    lncols[:, 7:8], lncols[:, 6:7], AF.Ln,
                    bias=cols_f[:, C_EPSLN:C_EPSLN + 1]),
                  mark=f"oln{t}", waits=(f"ocm{t}",))
                E("sc", lambda e: e.activation(
                    lncols[:, 7:8], lncols[:, 7:8], AF.Exp, scale=-0.5),
                  mark=f"orstd{t}", waits=de(f"oln{t}"))
                E("ve", lambda e: e.scalar_tensor_tensor(
                    lncols[:, 8:9], lncols[:, 4:5], -1.0, lncols[:, 7:8],
                    ALU.mult, ALU.mult),
                  waits=(f"orstd{t}",), mark=f"onmr{t}")
                E("ve", lambda e: e.tensor_scalar(
                    outn_sb[:, :], outa_sb[:, :],
                    lncols[:, 7:8], lncols[:, 8:9], ALU.mult, ALU.add),
                  waits=(f"onmr{t}",) + ((f"onTmm{t-1}",) if t >= 1 else ()),
                  mark=f"onorm{t}")
                E("ve", lambda e: e.tensor_tensor(
                    outn_sb[:, :], outn_sb[:, :], lng_tile[:, :], ALU.mult),
                  mark=f"og1{t}", waits=(f"onorm{t}",))
                E("ve", lambda e: e.tensor_tensor(
                    outn_sb[:, :], outn_sb[:, :], lnb_tile[:, :], ALU.add),
                  mark=f"ognb{t}", waits=(f"og1{t}",))
                for i in range(8):
                    w0 = ()
                    if i == 0:
                        w0 = (f"ognb{t}", f"oev{t}")
                        if t >= 1:
                            w0 += (f"onTev{t-1}",)
                    E("te", lambda e, i=i: e.transpose(
                        PTo[:, i * 128:(i + 1) * 128],
                        outn_sb[:, i * 128:(i + 1) * 128], id_sb[:, :]),
                      waits=w0, mark=f"onTmm{t}" if i == 7 else None)
                E("ve", lambda e: e.tensor_copy(outnT_sb[:, :], PTo[:, 0:1024]),
                  waits=(f"onTmm{t}",), mark=f"onTev{t}")
                for c in range(8):
                    wl = []
                    if c == 0:
                        wl = [f"onTev{t}", "wold"]
                        if t >= 1:
                            wl.append(f"sev{t-1}")
                    E("te", lambda e, c=c: e.matmul(
                        PS[:, 2048:2560],
                        outnT_sb[:, c * 128:(c + 1) * 128],
                        wo_sb[:, c * 1024:c * 1024 + 512],
                        start=(c == 0), stop=(c == 7)),
                      waits=wl)
                    E("te", lambda e, c=c: e.matmul(
                        PS[:, 2560:3072],
                        outnT_sb[:, c * 128:(c + 1) * 128],
                        wo_sb[:, c * 1024 + 512:(c + 1) * 1024],
                        start=(c == 0), stop=(c == 7)),
                      mark=f"wmm{t}" if c == 7 else None)
                E("ve", lambda e: e.tensor_tensor(
                    s_sb[:, :], PS[:, 2048:3072], bo_tile[:, :], ALU.add),
                  waits=(f"wmm{t}",) +
                  ((f"x2st{NT_OWN-1}",) if t == 0 else ()),
                  mark=f"sev{t}")
                E("ve", lambda e: e.tensor_tensor(
                    sg_sb[:, :], s_sb[:, :], g_t[:, :], ALU.mult),
                  mark=f"sgm{t}", waits=(f"gm{t}", f"sev{t}"))
                E("ve", lambda e, t=t, b=b: e.tensor_tensor(
                    res_f[:, :], sg_sb[:, :], xbuf[:, b * D:(b + 1) * D],
                    ALU.add),
                  waits=(f"sgm{t}", f"bxld{t}") +
                  ((f"outst{t-1}",) if t >= 1 else ()),
                  mark=f"resm{t}")
                E("sy", lambda e, t=t: e.dma_start(
                    out=out_d[t * 128:(t + 1) * 128, :], in_=res_f[:, :]),
                  waits=(f"resm{t}",), mark=f"outst{t}", dma=True, sem="syo")
                if debug:
                    E("gp", lambda e, t=t: e.dma_start(
                        out=dbg["dbg_y"][t * 128:(t + 1) * 128, :],
                        in_=y_sb[:, :]),
                      waits=(f"gm{t}",), mark=f"dbgy{t}", dma=True, sem="gdy")
                    E("gp", lambda e, t=t: e.dma_start(
                        out=dbg["dbg_oa"][t * 128:(t + 1) * 128, :],
                        in_=outa_sb[:, :]),
                      waits=(f"osq{t}",), mark=f"dbgoa{t}", dma=True, sem="gdo")
                    E("gp", lambda e, t=t: e.dma_start(
                        out=dbg["dbg_g"][t * 128:(t + 1) * 128, :],
                        in_=g_t[:, :]),
                      waits=(f"resm{t}",), mark=f"dbggt{t}", dma=True,
                      sem="gdg")

            b2A(0)
            b2Amm(0)
            for t in range(NT_OWN):
                if t + 1 < NT_OWN:
                    b2A(t + 1)
                b2Batt(t)
                if t + 1 < NT_OWN:
                    b2Amm(t + 1)
                b2B(t)

            if debug:
                E("gp", lambda e: e.dma_start(out=dbg["dbg_st2"][:, :],
                                              in_=statet2[:, :]),
                  waits=tuple(f"outst{t}" for t in range(NT_OWN)),
                  mark="dbgst2", dma=True, sem="gpd")
                E("ve", fn=None, waits=("dbgst2",))
            E("ve", fn=None, waits=tuple(f"outst{t}" for t in range(NT_OWN)))
            E("ve", fn=None, waits=(f"qtst{NT_OWN-1}", "wtst31",
                                    f"x2st{NT_OWN-1}"))

        em.serial = SERIAL
        em.mode = "count"
        em.auto = 0
        em.auto_s = 0
        em.prev_mark = None
        prog()
        em.mode = "emit"
        em.sems = _sems

        def mk(eng):
            def body(e):
                em.cur_eng = eng
                em.handles = e
                em.auto = 0
                em.auto_s = 0
                em.prev_mark = None
                prog()
            return body
        blk.sync(mk("sy"))
        blk.gpsimd(mk("gp"))
        blk.scalar(mk("sc"))
        blk.vector(mk("ve"))
        blk.tensor(mk("te"))

    return nc


def kernel(L0_attn_x, L0_attn_Wq, L0_attn_bq, L0_attn_Wk, L0_attn_bk,
           L0_attn_Wv, L0_attn_bv, L0_attn_Wo, L0_attn_bo, L0_attn_ln_g,
           L0_attn_ln_b, L0_ffn_norm1_w, L0_ffn_w1, L0_ffn_act_scale,
           L0_ffn_act_bias, L0_ffn_norm2_w, L0_ffn_w2, _debug=False,
           _build_only=False, _detect_races=False):
    x = np.asarray(L0_attn_x, dtype=np.float32)
    assert x.shape == (2, NTOK, D)

    def f32c(a, shape=None):
        a = np.asarray(a, dtype=np.float32)
        if shape is not None:
            a = a.reshape(shape)
        return np.ascontiguousarray(a)

    common = {
        "wq": f32c(L0_attn_Wq), "wk": f32c(L0_attn_Wk), "wv": f32c(L0_attn_Wv),
        "wo": f32c(L0_attn_Wo), "bq": f32c(L0_attn_bq, (1, D)),
        "bk": f32c(L0_attn_bk), "bv": f32c(L0_attn_bv),
        "bo": f32c(L0_attn_bo, (1, D)), "lng": f32c(L0_attn_ln_g, (1, D)),
        "lnb": f32c(L0_attn_ln_b, (1, D)), "nw1": f32c(L0_ffn_norm1_w, (1, D)),
        "nw2": f32c(L0_ffn_norm2_w, (1, HID)),
        "acts": f32c(L0_ffn_act_scale, (1, 1)),
        "actb": f32c(L0_ffn_act_bias, (1, 1)),
        "w1t": np.ascontiguousarray(np.asarray(L0_ffn_w1, dtype=np.float32).T),
        "w2t": np.ascontiguousarray(np.asarray(L0_ffn_w2, dtype=np.float32).T),
    }
    in_maps = []
    for core in range(8):
        b, slot = divmod(core, 4)
        xb = x[b]
        xrot = np.ascontiguousarray(
            np.concatenate([xb[slot * OWN:(slot + 1) * OWN],
                            xb[:slot * OWN], xb[(slot + 1) * OWN:]], axis=0))
        m = dict(common)
        m["x"] = xrot
        in_maps.append(m)

    nc = build(debug=_debug, detect_races=_detect_races)
    if _build_only:
        return nc, in_maps
    res = run_bass_kernel_spmd(nc, in_maps, core_ids=list(range(8)))
    out = np.empty((2, NTOK, D), dtype=np.float32)
    for core in range(8):
        b, slot = divmod(core, 4)
        out[b, slot * OWN:(slot + 1) * OWN] = np.asarray(res.results[core]["out"])
    return out


# version 27
# speedup vs baseline: 1.1070x; 1.0005x over previous
import numpy as np
import ml_dtypes
from contextlib import ExitStack
import concourse.bass as bass
import concourse.mybir as mybir
from concourse.bass_utils import run_bass_kernel_spmd

F32 = mybir.dt.float32
BF16 = mybir.dt.bfloat16
I8 = mybir.dt.int8
AF = mybir.ActivationFunctionType
ALU = mybir.AluOpType
AX = mybir.AxisListType

NTOK = 8192
OWN = 2048
NT_ALL = 64
NT_OWN = 16
D = 1024
HID = 4096
NH = 16
ISCALE2 = float(2.0 ** -24)

ENGS = ("sy", "gp", "sc", "ve", "te")

PHASE_LIMIT = 6
B2SUB = 2
SERIAL = False

ARENA_BYTES = 65536
CARVES = {
    "wstage":  (1, 2, 0,     16384),
    "xnb":     (2, 2, 16384, 4096),
    "xbb":     (1, 1, 20480, 32768),
    "wdump":   (1, 1, 53248, 8192),
    "fbuf":    (2, 2, 20480, 4096),
    "fbuf4":   (3, 3, 16384, 8192),
    "mrow_f":  (2, 4, 32768, 4096),
    "mrow_b":  (2, 4, 36864, 2048),
    "lngrow":  (0, 0, 20480, 2048),
    "lnbrow":  (0, 0, 22528, 2048),
    "nw2row":  (0, 0, 38912, 16384),
    "nw1row":  (0, 0, 55296, 4096),
    "borow":   (0, 0, 59392, 4096),
    "bqrow":   (0, 0, 63488, 2048),
    "t1s":     (2, 2, 38912, 4096),
    "e1s":     (2, 2, 43008, 4096),
    "xnT":     (2, 2, 51200, 2048),
    "q0s":     (2, 2, 53248, 1024),
    "sAs":     (2, 2, 54272, 2048),
    "sBs":     (2, 2, 56320, 2048),
    "adjs":    (2, 2, 58368, 2048),
    "xq8":     (2, 2, 60416, 1024),
    "qtok":    (2, 2, 61440, 2048),
    "qtile":   (2, 2, 63488, 2048),
    "wsB":     (2, 2, 24576, 4096),
    "wtrn":    (2, 2, 28672, 2048),
    "wkv":     (4, 4, 0,     16384),
    "t_bf":    (4, 4, 16384, 16384),
    "states":  (4, 4, 38912, 8192),
    "wrow_f":  (4, 4, 47104, 4096),
    "wrow_b":  (4, 4, 51200, 2048),
    "bkbv":    (4, 4, 53248, 8192),
    "bvrow":   (4, 4, 61440, 4096),
    "t2f":     (5, 5, 0,     16384),
    "u0":      (5, 5, 16384, 16384),
    "e2h":     (5, 5, 32768, 8192),
    "sA2":     (5, 5, 40960, 4096),
    "sB2":     (5, 5, 45056, 2048),
    "sB2b":    (5, 5, 63488, 2048),
    "q0b":     (5, 5, 47104, 4096),
    "x2b":     (5, 5, 51200, 4096),
    "xqb2":    (5, 5, 55296, 4096),
    "xqT":     (5, 5, 59392, 4096),
    "x2bf2":   (6, 6, 0,     16384),
    "x2T":     (6, 6, 16384, 8192),
    "qtb":     (6, 6, 24576, 4096),
    "ys":      (6, 6, 28672, 4096),
    "th":      (6, 6, 32768, 4096),
    "gh":      (6, 6, 36864, 4096),
    "g_t":     (6, 6, 40960, 4096),
    "outa":    (6, 6, 45056, 4096),
    "outn":    (6, 6, 49152, 2048),
    "outnT":   (6, 6, 51200, 2048),
    "s_s":     (6, 6, 53248, 4096),
    "sg":      (6, 6, 57344, 4096),
    "res":     (6, 6, 61440, 4096),
}


def _check_carves():
    items = list(CARVES.items())
    for i in range(len(items)):
        n1, (l1, h1, o1, b1) = items[i]
        assert o1 + b1 <= ARENA_BYTES, n1
        for j in range(i + 1, len(items)):
            n2, (l2, h2, o2, b2) = items[j]
            if max(l1, l2) <= min(h1, h2):
                assert o1 + b1 <= o2 or o2 + b2 <= o1, f"arena overlap {n1} {n2}"


_check_carves()


class Emitter:

    def __init__(self):
        self.mode = "count"
        self.cnt = {k: 0 for k in ENGS}
        self.marks = {}
        self.cur_eng = None
        self.handles = None
        self.sems = None
        self.auto = 0
        self.serial = False
        self.auto_s = 0
        self.prev_mark = None

    def op(self, eng, fn=None, mark=None, waits=(), dma=False, sem=None):
        key = sem if sem is not None else eng
        if dma and mark is None:
            mark = f"_dma{self.auto}"
            self.auto += 1
        if self.serial and fn is not None:
            if mark is None:
                mark = f"_ser{self.auto_s}"
                self.auto_s += 1
            if self.prev_mark is not None:
                waits = tuple(waits) + (self.prev_mark,)
            self.prev_mark = mark
        if self.mode == "count":
            if mark is not None:
                assert mark not in self.marks, f"dup mark {mark}"
                self.cnt[key] = self.cnt.get(key, 0) + (16 if dma else 1)
                self.marks[mark] = (key, self.cnt[key])
            return
        if eng != self.cur_eng:
            return
        e = self.handles
        last = None
        for w in waits:
            sem_key, val = self.marks[w]
            if last != (sem_key, val):
                e.wait_ge(self.sems[sem_key], val)
            last = (sem_key, val)
        if fn is not None:
            inst = fn(e)
            if mark is not None:
                inst.then_inc(self.sems[key], 16 if dma else 1)


def build(debug=False, detect_races=False):
    nc = bass.Bass("TRN2", target_bir_lowering=False,
                   detect_race_conditions=detect_races)

    x_d = nc.dram_tensor("x", [NTOK, D], F32, kind="ExternalInput").ap()
    wq_d = nc.dram_tensor("wq", [D, D], F32, kind="ExternalInput").ap()
    wk_d = nc.dram_tensor("wk", [2, D, D], F32, kind="ExternalInput").ap()
    wv_d = nc.dram_tensor("wv", [2, D, D], F32, kind="ExternalInput").ap()
    wo_d = nc.dram_tensor("wo", [D, D], F32, kind="ExternalInput").ap()
    bq_d = nc.dram_tensor("bq", [1, D], F32, kind="ExternalInput").ap()
    bk_d = nc.dram_tensor("bk", [2, D], F32, kind="ExternalInput").ap()
    bv_d = nc.dram_tensor("bv", [2, D], F32, kind="ExternalInput").ap()
    bo_d = nc.dram_tensor("bo", [1, D], F32, kind="ExternalInput").ap()
    lng_d = nc.dram_tensor("lng", [1, D], F32, kind="ExternalInput").ap()
    lnb_d = nc.dram_tensor("lnb", [1, D], F32, kind="ExternalInput").ap()
    nw1_d = nc.dram_tensor("nw1", [1, D], F32, kind="ExternalInput").ap()
    nw2_d = nc.dram_tensor("nw2", [1, HID], F32, kind="ExternalInput").ap()
    acts_d = nc.dram_tensor("acts", [1, 1], F32, kind="ExternalInput").ap()
    actb_d = nc.dram_tensor("actb", [1, 1], F32, kind="ExternalInput").ap()
    w1t_d = nc.dram_tensor("w1t", [D, HID], F32, kind="ExternalInput").ap()
    w2t_d = nc.dram_tensor("w2t", [HID, D], F32, kind="ExternalInput").ap()
    out_d = nc.dram_tensor("out", [OWN, D], F32, kind="ExternalOutput").ap()

    ikind = "ExternalOutput" if debug else "Internal"
    xnbf_d = nc.dram_tensor("xnbf", [NTOK, D], BF16, kind=ikind).ap()
    xq_d = nc.dram_tensor("xqd", [OWN, D], I8, kind=ikind).ap()
    qt_d = nc.dram_tensor("qtd", [128, 8 * OWN], BF16, kind=ikind).ap()
    x2_d = nc.dram_tensor("x2d", [OWN, HID], I8, kind=ikind).ap()
    w1q_d = nc.dram_tensor("w1qd", [D, HID], I8, kind=ikind).ap()
    w2q_d = nc.dram_tensor("w2qd", [HID, D], I8, kind=ikind).ap()
    mrow_d = nc.dram_tensor("mrowd", [1, D], BF16, kind=ikind).ap()

    dbg = {}
    if debug:
        for name, shape, dt in [
            ("dbg_G", [128, 8 * D], BF16), ("dbg_m", [1, D], F32),
            ("dbg_stp", [64, D], F32),
            ("dbg_r", [NT_OWN * 128, HID], F32),
            ("dbg_y", [NT_OWN * 128, D], F32),
            ("dbg_oa", [NT_OWN * 128, D], F32),
            ("dbg_g", [NT_OWN * 128, D], F32),
            ("dbg_st2", [128, 1024], BF16),
        ]:
            dbg[name] = nc.dram_tensor(name, shape, dt, kind="ExternalOutput").ap()

    id_dram = nc.inline_tensor(np.eye(128, dtype=ml_dtypes.bfloat16), name="identbf")
    ones_dram = nc.inline_tensor(np.ones((1, 128), dtype=ml_dtypes.bfloat16), name="onesbf")
    onesf_dram = nc.inline_tensor(np.ones((1, 128), dtype=np.float32), name="onesf")
    onesbc_dram = nc.inline_tensor(np.ones((128, 1), dtype=ml_dtypes.bfloat16), name="onesbc")
    onesfc_dram = nc.inline_tensor(np.ones((128, 1), dtype=np.float32), name="onesfc")

    (C_A, C_B, C_BOA, C_EPSLN, C_EPSRMS, C_TH1, C_TH2, C_HALF, C_NHALF,
     C_S1W, C_S2W, C_NTH1, C_NTH2, C_EPSQ, C_A2, C_AB2, C_EB, C_S2F,
     C_S1F, C_T0) = range(20)

    em = Emitter()
    es = ExitStack()
    with es:
        def T(name, shape, dtype):
            return es.enter_context(nc.sbuf_tensor(name, shape, dtype))

        id_sb = T("id_sb", [128, 128], BF16)
        ones_sb = T("ones_sb", [1, 128], BF16)
        onesf_sb = T("onesf_sb", [1, 128], F32)
        onesbc_sb = T("onesbc_sb", [128, 1], BF16)
        onesfc_sb = T("onesfc_sb", [128, 1], F32)
        scal_f = T("scal_f", [1, 16], F32)
        cols_f = T("cols_f", [128, 20], F32)
        rq1_cols = T("rq1_cols", [128, NT_OWN], F32)
        scol_cols = T("scol_cols", [128, NT_OWN], F32)
        c1_cols = T("c1_cols", [128, NT_OWN], F32)
        ysc_cols = T("ysc_cols", [128, NT_OWN], F32)
        lncols = T("lncols", [128, 12], F32)
        qcols = T("qcols", [128, 8], F32)
        wstat = T("wstat", [128, 32], F32)
        stat_s = T("stat_s", [128, NT_ALL], F32)
        stat_r = T("stat_r", [128, NT_ALL], F32)
        mean_all = T("mean_all", [128, NT_ALL], F32)
        var_all = T("var_all", [128, NT_ALL], F32)
        rstd_all = T("rstd_all", [128, NT_ALL], F32)
        negm_all = T("negm_all", [128, NT_ALL], F32)
        scr8 = T("scr8", [128, 8], F32)
        rsq16 = T("rsq16", [128, NT_OWN], F32)
        nw1b_tile = T("nw1b", [128, D], BF16)
        mcol_b = T("mcol_b", [128, 8], BF16)
        statet_b = T("statet_b", [64, D], BF16)
        statet2 = T("statet2", [128, 1024], BF16)
        bq_tile = T("bq_tile", [128, D], BF16)
        lng_tile = T("lng_tile", [128, D], BF16)
        lnb_tile = T("lnb_tile", [128, D], BF16)
        bo_tile = T("bo_tile", [128, D], F32)
        nw1_tile = T("nw1_tile", [128, D], F32)
        anw2_tile = T("anw2_tile", [128, HID], F32)
        xbuf = T("xbuf", [128, 4 * D], F32)
        sqscr = T("sqscr", [128, 2048], F32)
        wq_sb = T("wq_sb", [128, 8 * D], BF16)
        w1q_sb = T("w1q_sb", [128, 8 * HID], BF16)
        arena = T("arena", [128, ARENA_BYTES // 2], BF16)

        STREAMS = ("sy", "gp", "sc", "ve", "te", "syp", "syo", "gpl",
                   "gpt", "gpm", "gpd", "gpp",
                   "syx0", "syx1", "syf0", "syf1", "syw0", "syw1",
                   "sy20", "sy21", "syq0", "syq1", "gpn0", "gpn1",
                   "gpw0", "gpw1", "gpxq0", "gpxq1", "gpqt0", "gpqt1",
                   "gpx20", "gpx21", "gdy", "gdo", "gdg", "gdr",
                   "gpb0", "gpb1", "syx2", "syx3", "syf2", "syf3")
        _sems = {k: es.enter_context(nc.semaphore("sem_" + k)) for k in STREAMS}
        PS = es.enter_context(nc.psum_tensor("PS", [128, 4096], F32))
        blk = es.enter_context(nc.Block())

        def carve(name, rows, cols, dtype):
            lo, hi, off, nbytes = CARVES[name]
            esz = {F32: 4, BF16: 2, I8: 1}[dtype]
            assert cols * esz <= nbytes, (name, cols, esz, nbytes)
            v = arena[:, off // 2:(off + cols * esz) // 2]
            v = v.bitcast(dtype)
            return v[0:rows, :]

        wstage = carve("wstage", 128, 4096, F32)
        xnb = carve("xnb", 128, 2 * D, BF16)
        xbb = carve("xbb", 128, 16 * D, BF16)
        wdump = carve("wdump", 128, 2048, F32)
        fbuf = carve("fbuf", 128, 2 * D, BF16)
        fbuf4 = carve("fbuf4", 128, 4 * D, BF16)
        mrow_f = carve("mrow_f", 1, D, F32)
        mrow_b = carve("mrow_b", 1, D, BF16)
        lngrow = carve("lngrow", 1, D, BF16)
        lnbrow = carve("lnbrow", 1, D, BF16)
        nw2row = carve("nw2row", 1, HID, F32)
        nw1row = carve("nw1row", 1, D, F32)
        borow = carve("borow", 1, D, F32)
        bqrow = carve("bqrow", 1, D, BF16)
        t1b = carve("t1s", 128, D, BF16)
        e1s = carve("e1s", 128, D, F32)
        xnT = carve("xnT", 128, D, BF16)
        q0s = carve("q0s", 128, D, I8)
        sAs = carve("sAs", 128, D, BF16)
        sBs = carve("sBs", 128, D, BF16)
        adjs = carve("adjs", 128, D, BF16)
        xq8 = carve("xq8", 128, D, I8)
        qtok = carve("qtok", 128, D, BF16)
        qtile = carve("qtile", 128, D, BF16)
        wsB = carve("wsB", 128, 2048, BF16)
        wtrn = carve("wtrn", 128, 2048, I8)
        wkv_sb = carve("wkv", 128, 8 * D, BF16)
        t_bf = carve("t_bf", 128, 8 * D, BF16)
        states_sb = carve("states", 64, 2 * D, F32)
        wrow_f = carve("wrow_f", 1, D, F32)
        wrow_b = carve("wrow_b", 1, D, BF16)
        bkbv_b = carve("bkbv", 1, 4 * D, BF16)
        bvrow = carve("bvrow", 1, D, F32)
        t2f_sb = carve("t2f", 128, HID, F32)
        u0_sb = carve("u0", 128, HID, F32)
        e2h_sb = carve("e2h", 128, 2048, F32)
        sA2_sb = carve("sA2", 128, 2 * D, BF16)
        sB2_sb = carve("sB2", 128, D, BF16)
        sB2b_sb = carve("sB2b", 128, D, BF16)
        q0b_sb = carve("q0b", 128, HID, I8)
        x2b_sb = carve("x2b", 128, HID, I8)
        xqb2 = carve("xqb2", 128, 2 * D, BF16)
        xqT_sb = carve("xqT", 128, 2 * D, BF16)
        x2bf2 = carve("x2bf2", 128, 2 * HID, BF16)
        x2T_sb = carve("x2T", 128, HID, BF16)
        qtb = carve("qtb", 128, 2 * D, BF16)
        y_sb = carve("ys", 128, D, F32)
        th_sb = carve("th", 128, D, F32)
        gh_sb = carve("gh", 128, D, F32)
        g_t = carve("g_t", 128, D, F32)
        outa_sb = carve("outa", 128, D, F32)
        outn_sb = carve("outn", 128, D, BF16)
        outnT_sb = carve("outnT", 128, D, BF16)
        s_sb = carve("s_s", 128, D, F32)
        sg_sb = carve("sg", 128, D, F32)
        res_f = carve("res", 128, D, F32)

        wsA = sqscr[:, :].bitcast(BF16)[:, 0:2048]
        g_bf = w1q_sb[:, 0:8 * D]
        r_sb = wq_sb[:, 0:8 * D].bitcast(F32)
        w2q_sb = w1q_sb[:, 0:8 * HID]
        wo_sb = wq_sb[:, 0:8 * D]
        PT = PS[:, 3584:4096].bitcast(BF16)
        PT2 = PS[:, 3072:4096].bitcast(BF16)
        PTo = PS[:, 1024:2048].bitcast(BF16)

        def prog():
            E = em.op

            def de(*ws):
                return ws if detect_races else ()

            p0loads = [
                (id_sb[:, :], id_dram.ap()[:, :]),
                (ones_sb[:, :], ones_dram.ap()[:, :]),
                (onesf_sb[:, :], onesf_dram.ap()[:, :]),
                (onesbc_sb[:, :], onesbc_dram.ap()[:, :]),
                (onesfc_sb[:, :], onesfc_dram.ap()[:, :]),
                (nw1row[:, :], nw1_d[:, :]),
                (borow[:, :], bo_d[:, :]),
                (nw2row[:, :], nw2_d[:, :]),
                (scal_f[0:1, 0:1], acts_d[:, :]),
                (scal_f[0:1, 1:2], actb_d[:, :]),
            ]
            for i, (dst, src) in enumerate(p0loads):
                E("sy", lambda e, d=dst, s=src: e.dma_start(out=d, in_=s),
                  mark="p0ld" if i == len(p0loads) - 1 else None, dma=True,
                  sem="syp")
            p0casts = [
                (bqrow[:, :], bq_d[:, :]),
                (lngrow[:, :], lng_d[:, :]),
                (lnbrow[:, :], lnb_d[:, :]),
            ]
            for i, (dst, src) in enumerate(p0casts):
                E("gp", lambda e, d=dst, s=src: e.dma_start(out=d, in_=s),
                  mark="p0g" if i == len(p0casts) - 1 else None, dma=True,
                  sem="gpl")
            wq_view = wq_d.rearrange("(c p) j -> p c j", p=128)
            E("gp", lambda e: e.dma_start(
                out=wq_sb[:, :].rearrange("p (c j) -> p c j", c=8), in_=wq_view),
              mark="wqld", waits=("p0g",), dma=True, sem="gpl")

            def k1mm(e, outsl, lhsT, rhs, st):
                return e.matmul(outsl, lhsT, rhs, start=st, stop=True)

            for i, (row, dst) in enumerate([(bqrow, bq_tile), (lngrow, lng_tile),
                                            (lnbrow, lnb_tile)]):
                for h in range(2):
                    E("te", lambda e, i=i, h=h, row=row: k1mm(
                        e, PS[:, i * 1024 + h * 512:i * 1024 + (h + 1) * 512],
                        ones_sb[0:1, :], row[0:1, h * 512:(h + 1) * 512], st=True),
                      mark=f"bcmm{i}_{h}",
                      waits=("p0ld", "p0g") if (i == 0 and h == 0) else ())
                E("ve", lambda e, i=i, dst=dst: e.tensor_copy(
                    dst[:, :], PS[:, i * 1024:(i + 1) * 1024]),
                  mark=f"bcev{i}", waits=(f"bcmm{i}_0", f"bcmm{i}_1"))
            E("te", lambda e: k1mm(e, PS[:, 3072:3584], onesf_sb[0:1, :],
                                   nw1row[0:1, 0:512], True),
              mark="nw1a", waits=("p0ld",))
            E("te", lambda e: k1mm(e, PS[:, 0:512], onesf_sb[0:1, :],
                                   nw1row[0:1, 512:1024], True),
              mark="nw1b", waits=("bcev0",))
            E("ve", lambda e: e.tensor_copy(nw1_tile[:, 0:512], PS[:, 3072:3584]),
              mark="nw1ev1", waits=("nw1a",))
            E("ve", lambda e: e.tensor_copy(nw1_tile[:, 512:1024], PS[:, 0:512]),
              mark="nw1ev2", waits=("nw1b",))
            E("ve", lambda e: e.tensor_copy(nw1b_tile[:, :], nw1_tile[:, :]),
              mark="nw1bf", waits=("nw1ev2",))
            E("te", lambda e: k1mm(e, PS[:, 512:1024], onesf_sb[0:1, :],
                                   borow[0:1, 0:512], True),
              mark="boa", waits=("bcev0",))
            E("te", lambda e: k1mm(e, PS[:, 1024:1536], onesf_sb[0:1, :],
                                   borow[0:1, 512:1024], True),
              mark="bob", waits=("bcev1",))
            E("ve", lambda e: e.tensor_copy(bo_tile[:, 0:512], PS[:, 512:1024]),
              mark="boev1", waits=("boa",))
            E("ve", lambda e: e.tensor_copy(bo_tile[:, 512:1024], PS[:, 1024:1536]),
              mark="boev2", waits=("bob",))
            E("te", lambda e: k1mm(e, PS[:, 2560:2561], onesf_sb[0:1, :],
                                   scal_f[0:1, 0:1], True),
              mark="acol", waits=("bcev2", "p0ld"))
            E("te", lambda e: k1mm(e, PS[:, 2561:2562], onesf_sb[0:1, :],
                                   scal_f[0:1, 1:2], True), mark="bcol")
            E("ve", lambda e: e.tensor_copy(cols_f[:, C_A:C_B + 1],
                                            PS[:, 2560:2562]),
              mark="abev", waits=("acol", "bcol"))
            E("ve", lambda e: e.reciprocal(cols_f[:, C_BOA:C_BOA + 1],
                                           cols_f[:, C_A:C_A + 1]),
              mark="rcp0", waits=("abev",))
            E("ve", lambda e: e.tensor_tensor(cols_f[:, C_BOA:C_BOA + 1],
                                              cols_f[:, C_BOA:C_BOA + 1],
                                              cols_f[:, C_B:C_B + 1], ALU.mult),
              mark="boacol", waits=("rcp0",))
            E("ve", lambda e: e.tensor_tensor(
                cols_f[:, C_A2:C_A2 + 1], cols_f[:, C_A:C_A + 1],
                cols_f[:, C_A:C_A + 1], ALU.mult),
              mark="a2c", waits=("abev",))
            E("ve", lambda e: e.tensor_tensor(
                cols_f[:, C_AB2:C_AB2 + 1], cols_f[:, C_A:C_A + 1],
                cols_f[:, C_B:C_B + 1], ALU.mult),
              mark="ab2a")
            E("ve", lambda e: e.tensor_scalar(
                cols_f[:, C_AB2:C_AB2 + 1], cols_f[:, C_AB2:C_AB2 + 1],
                2.0, None, ALU.mult),
              mark="ab2c", waits=("ab2a",))
            E("ve", lambda e: e.tensor_tensor(
                cols_f[:, C_EB:C_EB + 1], cols_f[:, C_B:C_B + 1],
                cols_f[:, C_B:C_B + 1], ALU.mult),
              mark="eba")
            E("ve", lambda e: e.tensor_scalar(
                cols_f[:, C_EB:C_EB + 1], cols_f[:, C_EB:C_EB + 1],
                1e-8, None, ALU.add),
              mark="ebc", waits=("eba",))
            for cslot, val in [(C_EPSLN, 1e-5), (C_EPSRMS, 1e-8),
                               (C_HALF, 0.5), (C_NHALF, -0.5), (C_EPSQ, 1e-5)]:
                E("ve", lambda e, c=cslot, v=val: e.memset(cols_f[:, c:c + 1], v),
                  mark=f"ms{cslot}")
            for c in range(8):
                E("ve", lambda e, c=c: e.tensor_tensor(
                    wq_sb[:, c * 1024 + c * 128:c * 1024 + (c + 1) * 128],
                    wq_sb[:, c * 1024 + c * 128:c * 1024 + (c + 1) * 128],
                    id_sb[:, :], ALU.add),
                  waits=("wqld", "p0ld") if c == 0 else (),
                  mark="wqf" if c == 7 else None)
            for h in range(8):
                E("te", lambda e, h=h: k1mm(
                    e, PS[:, 3072:3584], onesf_sb[0:1, :],
                    nw2row[0:1, h * 512:(h + 1) * 512], True),
                  mark=f"nw2mm{h}",
                  waits=("nw1ev1",) if h == 0 else (f"nw2ev{h-1}",))
                E("ve", lambda e, h=h: e.tensor_scalar(
                    anw2_tile[:, h * 512:(h + 1) * 512], PS[:, 3072:3584],
                    cols_f[:, C_A:C_A + 1], None, ALU.mult),
                  mark=f"nw2ev{h}", waits=(f"nw2mm{h}", "abev"))

            if PHASE_LIMIT < 1:
                E("ve", fn=None, waits=("nw2ev7", "boacol"))
                return
            NG = NT_ALL // 8
            for g in range(NG):
                for j in range(8):
                    t = g * 8 + j
                    b = t % 4
                    w_ld = [f"sq{t-4}", f"cst{t-4}"] if t >= 4 else ["p0ld"]
                    E("sy" if t % 2 == 0 else "sc",
                      lambda e, t=t, b=b: e.dma_start(
                        out=xbuf[:, b * D:(b + 1) * D],
                        in_=x_d[t * 128:(t + 1) * 128, :]),
                      mark=f"xld{t}", waits=w_ld, dma=True, sem=f"syx{t%4}")
                    E("ve", lambda e, t=t, b=b: e.tensor_tensor(
                        sqscr[:, (b % 2) * D:(b % 2 + 1) * D],
                        xbuf[:, b * D:(b + 1) * D],
                        xbuf[:, b * D:(b + 1) * D], ALU.mult),
                      mark=f"sqm{t}",
                      waits=(f"xld{t}",) +
                      (de(f"sq{t-2}") if t >= 2 else ()))
                    E("ve", lambda e, t=t, b=b: e.reduce_sum(
                        stat_r[:, t:t + 1], xbuf[:, b * D:(b + 1) * D], AX.X),
                      mark=f"rd{t}", waits=(f"xld{t}",))
                    E("ve", lambda e, t=t, b=b: e.reduce_sum(
                        stat_s[:, t:t + 1],
                        sqscr[:, (b % 2) * D:(b % 2 + 1) * D], AX.X),
                      mark=f"sq{t}", waits=(f"sqm{t}",))
                    E("ve", lambda e, t=t, b=b: e.tensor_copy(
                        xbb[:, (t % 16) * D:(t % 16 + 1) * D],
                        xbuf[:, b * D:(b + 1) * D]),
                      mark=f"cst{t}",
                      waits=(f"rd{t}",) +
                      (("nw2mm7",) if t < 2 else ()) +
                      ((f"gram{t-16}", f"xnst{t-16}") if t >= 16 else ()))
                for i in range(4 * g, 4 * g + 4):
                    if i < 16:
                        wsrc = w1t_d[(i // 2) * 128:(i // 2 + 1) * 128,
                                     (i % 2) * 2048:(i % 2 + 1) * 2048]
                        wdst0 = wstage[:, (i % 2) * 2048:(i % 2 + 1) * 2048]
                    else:
                        wsrc = w2t_d.rearrange("(a p) d -> p a d", p=128)[
                            :, (i - 16) * 2:(i - 16) * 2 + 2, :]
                        wdst0 = wstage[:, (i % 2) * 2048:(i % 2 + 1) * 2048
                                       ].rearrange("p (a d) -> p a d", a=2)
                    E("gp", lambda e, w=wsrc, d0=wdst0: e.dma_start(
                        out=d0, in_=w),
                      mark=f"wld{i}",
                      waits=(f"wstat{i-2}",) if i >= 2 else (),
                      dma=True, sem=f"gpw{i%2}")
                    E("sc", lambda e, i=i: e.activation(
                        wdump[:, :],
                        wstage[:, (i % 2) * 2048:(i % 2 + 1) * 2048],
                        AF.Abs, accum_out=wstat[:, i:i + 1]),
                      mark=f"wstat{i}",
                      waits=(f"wld{i}",) +
                      (de(f"wstat{i-1}") if i >= 1 else ("nw2mm7",)))
                sl = slice(g * 8, g * 8 + 8)
                E("ve", lambda e, sl=sl: e.tensor_scalar(
                    mean_all[:, sl], stat_r[:, sl], 1.0 / D, None, ALU.mult),
                  mark=f"ga{g}", waits=(f"rd{g*8+7}",))
                E("ve", lambda e, sl=sl: e.tensor_scalar(
                    var_all[:, sl], stat_s[:, sl], 1.0 / D, None, ALU.mult),
                  mark=f"gb{g}", waits=(f"sq{g*8+7}",))
                E("ve", lambda e, sl=sl: e.tensor_tensor(
                    scr8[:, :], mean_all[:, sl], mean_all[:, sl], ALU.mult),
                  mark=f"gc{g}", waits=(f"ga{g}",))
                E("ve", lambda e, sl=sl: e.tensor_tensor(
                    var_all[:, sl], var_all[:, sl], scr8[:, :], ALU.subtract),
                  mark=f"gd{g}", waits=(f"gb{g}", f"gc{g}"))
                E("sc", lambda e, sl=sl: e.activation(
                    rstd_all[:, sl], var_all[:, sl], AF.Ln,
                    bias=cols_f[:, C_EPSLN:C_EPSLN + 1]),
                  mark=f"ge{g}", waits=(f"gd{g}",))
                E("sc", lambda e, sl=sl: e.activation(
                    rstd_all[:, sl], rstd_all[:, sl], AF.Exp, scale=-0.5),
                  mark=f"gf{g}", waits=de(f"ge{g}"))
                E("ve", lambda e, sl=sl: e.scalar_tensor_tensor(
                    negm_all[:, sl], mean_all[:, sl], -1.0, rstd_all[:, sl],
                    ALU.mult, ALU.mult),
                  mark=f"cmg{g}", waits=(f"gf{g}",))
                for j in range(8):
                    t = g * 8 + j
                    s16 = (t % 16) * D
                    E("ve", lambda e, t=t, s16=s16: e.tensor_scalar(
                        xbb[:, s16:s16 + D], xbb[:, s16:s16 + D],
                        rstd_all[:, t:t + 1], negm_all[:, t:t + 1],
                        ALU.mult, ALU.add),
                      mark=f"nx{t}", waits=(f"cst{t}", f"cmg{g}"))
                    for c in range(4):
                        for h in range(2):
                            E("te", lambda e, t=t, s16=s16, c=c, h=h: e.matmul(
                                PS[:, (2 * c + h) * 512:(2 * c + h + 1) * 512],
                                xbb[:, s16 + c * 128:s16 + (c + 1) * 128],
                                xbb[:, s16 + h * 512:s16 + (h + 1) * 512],
                                start=(t == 0), stop=(t == NT_ALL - 1)),
                              mark=f"gram{t}" if (c == 3 and h == 1) else None,
                              waits=((f"nx{t}",) +
                                     (("nw2ev7",) if t == 0 else ()))
                              if (c == 0 and h == 0) else ())
                    E("sy" if t % 2 == 0 else "sc",
                      lambda e, t=t, s16=s16: e.dma_start(
                        out=xnbf_d[t * 128:(t + 1) * 128, :],
                        in_=xbb[:, s16:s16 + D]),
                      mark=f"xnst{t}",
                      waits=(f"nx{t}",) +
                      ((f"xnst{t-2}",) if t >= 2 else ()),
                      dma=True, sem=f"gpn{t%2}")

            for c in range(4):
                E("ve", lambda e, c=c: e.tensor_copy(
                    g_bf[:, c * 1024:(c + 1) * 1024],
                    PS[:, c * 1024:(c + 1) * 1024]),
                  mark=f"gh1ev{c}", waits=(f"gram{NT_ALL-1}",) if c == 0 else ())

            E("ve", lambda e: e.reduce_sum(qcols[:, 0:1], wstat[:, 0:16], AX.X),
              waits=("wstat15",), mark="w1red")
            E("ve", lambda e: e.reduce_sum(qcols[:, 1:2], wstat[:, 16:32], AX.X),
              waits=("wstat31",), mark="w2red")
            E("te", lambda e: e.matmul(PS[0:1, 2048:2049], onesfc_sb[:, 0:1],
                                       qcols[:, 0:1], start=True, stop=True),
              waits=("w1red", "gh1ev3"), mark="w1s")
            E("te", lambda e: e.matmul(PS[0:1, 2049:2050], onesfc_sb[:, 0:1],
                                       qcols[:, 1:2], start=True, stop=True),
              waits=("w2red",), mark="w2s")
            E("ve", lambda e: e.tensor_copy(scal_f[0:1, 2:4], PS[0:1, 2048:2050]),
              waits=("w1s", "w2s"), mark="wsev")
            E("ve", lambda e: e.tensor_scalar(scal_f[0:1, 2:4], scal_f[0:1, 2:4],
                                              1.0 / (D * HID), 1e-5, ALU.mult,
                                              ALU.max), mark="wclip",
              waits=("wsev",))
            E("te", lambda e: e.matmul(PS[:, 2048:2049], onesf_sb[0:1, :],
                                       scal_f[0:1, 2:3], start=True, stop=True),
              waits=("wsev", "wclip"), mark="s1b")
            E("te", lambda e: e.matmul(PS[:, 2049:2050], onesf_sb[0:1, :],
                                       scal_f[0:1, 3:4], start=True, stop=True),
              mark="s2b")
            E("ve", lambda e: e.tensor_copy(cols_f[:, C_S1W:C_S2W + 1],
                                            PS[:, 2048:2050]),
              waits=("s1b", "s2b"), mark="swcols")
            E("ve", lambda e: e.tensor_scalar(cols_f[:, C_S2F:C_S2F + 1],
                                              cols_f[:, C_S2W:C_S2W + 1],
                                              0.5 / 127.0, None, ALU.mult),
              mark="s2fc", waits=("swcols",))
            E("ve", lambda e: e.tensor_scalar(cols_f[:, C_S1F:C_S1F + 1],
                                              cols_f[:, C_S1W:C_S1W + 1],
                                              0.5, None, ALU.mult),
              mark="s1fc")
            E("ve", lambda e: e.tensor_scalar(cols_f[:, C_TH1:C_TH1 + 1],
                                              cols_f[:, C_S1W:C_S1W + 1],
                                              0.5, None, ALU.mult),
              mark="th1c", waits=("swcols",))
            E("ve", lambda e: e.tensor_scalar(cols_f[:, C_TH2:C_TH2 + 1],
                                              cols_f[:, C_S2W:C_S2W + 1],
                                              0.5, None, ALU.mult),
              mark="th2c")
            E("ve", lambda e: e.tensor_scalar(cols_f[:, C_NTH1:C_NTH1 + 1],
                                              cols_f[:, C_TH1:C_TH1 + 1],
                                              -1.0, None, ALU.mult),
              waits=("th1c",))
            E("ve", lambda e: e.tensor_scalar(cols_f[:, C_NTH2:C_NTH2 + 1],
                                              cols_f[:, C_TH2:C_TH2 + 1],
                                              -1.0, None, ALU.mult),
              mark="thcols", waits=("th2c",))

            if PHASE_LIMIT < 2:
                E("ve", fn=None, waits=("thcols", f"gram{NT_ALL-1}",
                                        f"xnst{NT_ALL-2}", f"xnst{NT_ALL-1}",
                                        "wstat31"))
                return
            E("ve", lambda e: e.tensor_scalar(
                mean_all[:, 0:NT_OWN], var_all[:, 0:NT_OWN], 1e-5, None,
                ALU.add),
              mark="v2a", waits=("cmg1",))
            E("ve", lambda e: e.reciprocal(
                negm_all[:, 0:NT_OWN], mean_all[:, 0:NT_OWN]),
              mark="v2b", waits=("v2a", f"nx{NT_OWN-1}"))
            E("ve", lambda e: e.tensor_tensor(
                mean_all[:, 0:NT_OWN], var_all[:, 0:NT_OWN],
                negm_all[:, 0:NT_OWN], ALU.mult),
              mark="v2c", waits=("v2b",))
            E("sc", lambda e: e.activation(
                rsq16[:, :], mean_all[:, 0:NT_OWN], AF.Ln,
                bias=cols_f[:, C_EPSRMS:C_EPSRMS + 1]),
              mark="v2d", waits=("v2c",))
            E("sc", lambda e: e.activation(
                rsq16[:, :], rsq16[:, :], AF.Exp, scale=-0.5),
              mark="v2e", waits=de("v2d"))
            def tern_chunk(i):
                nth = C_NTH1 if i < 16 else C_NTH2
                pth = C_TH1 if i < 16 else C_TH2
                if i < 16:
                    wsrc = w1t_d[(i // 2) * 128:(i // 2 + 1) * 128,
                                 (i % 2) * 2048:(i % 2 + 1) * 2048]
                    wdst = w1q_d[(i // 2) * 128:(i // 2 + 1) * 128,
                                 (i % 2) * 2048:(i % 2 + 1) * 2048]
                    wsg0 = wstage[:, (i % 2) * 2048:(i % 2 + 1) * 2048]
                    wtr0 = wtrn[:, :]
                else:
                    wsrc = w2t_d.rearrange("(a p) d -> p a d", p=128)[
                        :, (i - 16) * 2:(i - 16) * 2 + 2, :]
                    wdst = w2q_d.rearrange("(a p) d -> p a d", p=128)[
                        :, (i - 16) * 2:(i - 16) * 2 + 2, :]
                    wsg0 = wstage[:, (i % 2) * 2048:(i % 2 + 1) * 2048
                                  ].rearrange("p (a d) -> p a d", a=2)
                    wtr0 = wtrn[:, :].rearrange("p (a d) -> p a d", a=2)
                E("sc" if i % 2 == 0 else "sy",
                  lambda e, w=wsrc, d0=wsg0: e.dma_start(out=d0, in_=w),
                  waits=(f"wtrn{i-2}",) if i >= 2 else ("thcols",),
                  mark=f"vwld{i}", dma=True, sem=f"syw{i%2}")
                E("sc", lambda e, i=i, nth=nth: e.sign(
                    wsA[:, :], wstage[:, (i % 2) * 2048:(i % 2 + 1) * 2048],
                    bias=cols_f[:, nth:nth + 1]),
                  mark=f"wsga{i}",
                  waits=(f"vwld{i}",) + ((f"wtrn{i-1}",) if i >= 1 else ()))
                E("sc", lambda e, i=i, pth=pth: e.sign(
                    wsB[:, :], wstage[:, (i % 2) * 2048:(i % 2 + 1) * 2048],
                    bias=cols_f[:, pth:pth + 1]),
                  mark=f"wsgn{i}")
                E("ve", lambda e: e.tensor_tensor(
                    wtrn[:, :], wsA[:, :], wsB[:, :], ALU.add),
                  waits=(f"wsga{i}", f"wsgn{i}") +
                  ((f"wtst{i-1}",) if i >= 1 else ()),
                  mark=f"wtrn{i}")
                E("gp", lambda e, w=wdst, s0=wtr0: e.dma_start(out=w, in_=s0),
                  waits=(f"wtrn{i}",), mark=f"wtst{i}", dma=True, sem="gpt")

            mseq = [0]

            def m_mms(src_ap, first, wmarks, last=False):
                for h in range(2):
                    E("te", lambda e, s=src_ap, h=h, f=first, l=last: e.matmul(
                        PS[0:1, 1024 + h * 512:1024 + (h + 1) * 512],
                        onesbc_sb[:, 0:1], s[:, h * 512:(h + 1) * 512],
                        start=f, stop=l),
                      mark=f"mmm{mseq[0]}" if h == 1 else None,
                      waits=wmarks if h == 0 else ())
                mseq[0] += 1

            for t in range(NT_OWN):
                b = t % 2
                w_ld = ([f"mmm{4*(t-2)}", f"t1m{t-2}"] if t >= 2
                        else [f"xnst{NT_ALL-2}", f"xnst{NT_ALL-1}",
                              f"gram{NT_ALL-1}"])
                E("sy", lambda e, t=t, b=b: e.dma_start(
                    out=xnb[:, b * D:(b + 1) * D],
                    in_=xnbf_d[t * 128:(t + 1) * 128, :]),
                  mark=f"jxld{t}", waits=w_ld, dma=True, sem=f"syx{t%2}")
                E("ve", lambda e, t=t, b=b: e.tensor_tensor(
                    t1b[:, :], xnb[:, b * D:(b + 1) * D], nw1b_tile[:, :],
                    ALU.mult),
                  waits=(f"jxld{t}",) +
                  (("nw1bf", f"gram{NT_ALL-1}") if t == 0
                   else (f"q0m{t-1}", f"em{t-1}")),
                  mark=f"t1m{t}")
                E("ve", lambda e: e.reduce_max(
                    qcols[:, 4:5], t1b[:, :], AX.X, apply_absolute_value=True),
                  mark=f"am1{t}", waits=(f"t1m{t}",))
                E("ve", lambda e, t=t: e.tensor_tensor(
                    qcols[:, 5:6], qcols[:, 4:5], rsq16[:, t:t + 1], ALU.mult),
                  mark=f"q5a{t}", waits=("v2e", f"am1{t}"))
                E("ve", lambda e: e.tensor_scalar(
                    qcols[:, 5:6], qcols[:, 5:6], 1e-5, None, ALU.max),
                  mark=f"q5b{t}", waits=(f"q5a{t}",))
                E("ve", lambda e, t=t: e.tensor_scalar(
                    rq1_cols[:, t:t + 1], qcols[:, 5:6], 1.0 / 127.0, None,
                    ALU.mult),
                  waits=(f"q5b{t}",))
                E("ve", lambda e: e.reciprocal(qcols[:, 6:7], qcols[:, 5:6]),
                  mark=f"rc1{t}")
                E("ve", lambda e: e.tensor_scalar(
                    qcols[:, 6:7], qcols[:, 6:7], 127.0, None, ALU.mult),
                  mark=f"q6b{t}", waits=(f"rc1{t}",))
                E("ve", lambda e, t=t: e.tensor_tensor(
                    c1_cols[:, t:t + 1], rsq16[:, t:t + 1], qcols[:, 6:7],
                    ALU.mult),
                  mark=f"c1m{t}", waits=(f"q6b{t}",))
                E("sc", lambda e, t=t: e.activation(
                    q0s[:, :], t1b[:, :], AF.Copy, scale=c1_cols[:, t:t + 1]),
                  waits=(f"c1m{t}",) +
                  ((f"xqm{t-1}",) if t >= 1 else de("wstat31")),
                  mark=f"q0m{t}")
                E("ve", lambda e, t=t: e.scalar_tensor_tensor(
                    e1s[:, :], t1b[:, :], c1_cols[:, t:t + 1], q0s[:, :],
                    ALU.mult, ALU.subtract),
                  waits=(f"q0m{t}", f"c1m{t}") +
                  ((f"sgn1{t-1}",) if t >= 1 else ()),
                  mark=f"em{t}")
                E("sc", lambda e: e.sign(sAs[:, :], e1s[:, :],
                                         bias=cols_f[:, C_NHALF:C_NHALF + 1]),
                  mark=f"sga{t}", waits=(f"em{t}",))
                E("sc", lambda e: e.sign(sBs[:, :], e1s[:, :],
                                         bias=cols_f[:, C_HALF:C_HALF + 1]),
                  mark=f"sgn1{t}")
                E("ve", lambda e: e.tensor_tensor(
                    adjs[:, :], sAs[:, :], sBs[:, :], ALU.add),
                  mark=f"adm{t}",
                  waits=(f"sga{t}", f"sgn1{t}") +
                  ((f"xqst{t-1}",) if t >= 1 else ()))
                E("ve", lambda e: e.scalar_tensor_tensor(
                    xq8[:, :], adjs[:, :], 0.5, q0s[:, :], ALU.mult, ALU.add),
                  mark=f"xqm{t}", waits=(f"adm{t}",))
                E("gp", lambda e, t=t: e.dma_start(
                    out=xq_d[t * 128:(t + 1) * 128, :], in_=xq8[:, :]),
                  waits=(f"xqm{t}",), mark=f"xqst{t}", dma=True, sem=f"gpxq{t%2}")
                for i in range(8):
                    E("te", lambda e, b=b, i=i: e.transpose(
                        PT[:, i * 128:(i + 1) * 128],
                        xnb[:, b * D + i * 128:b * D + (i + 1) * 128],
                        id_sb[:, :]),
                      waits=((f"jxld{t}",) + ((f"qTev{t-1}",) if t >= 1
                                              else ("gh1ev3",)))
                      if i == 0 else (),
                      mark=f"xnTmm{t}" if i == 7 else None)
                E("ve", lambda e: e.tensor_copy(xnT[:, :], PT[:, 0:1024]),
                  waits=(f"xnTmm{t}",), mark=f"xnTev{t}")
                for c in range(8):
                    for h in range(2):
                        E("te", lambda e, c=c, h=h: e.matmul(
                            PS[:, h * 512:(h + 1) * 512],
                            xnT[:, c * 128:(c + 1) * 128],
                            wq_sb[:, c * 1024 + h * 512:c * 1024 + (h + 1) * 512],
                            start=(c == 0), stop=(c == 7)),
                          waits=((f"xnTev{t}", "wqf") +
                                 ((f"qev{t-1}",) if t >= 1 else ()))
                          if (c == 0 and h == 0) else (),
                          mark=f"qmm{t}" if (c == 7 and h == 1) else None)
                E("ve", lambda e: e.tensor_tensor(
                    qtok[:, :], PS[:, 0:1024], bq_tile[:, :], ALU.add),
                  waits=(f"qmm{t}",), mark=f"qev{t}")
                for i in range(8):
                    E("te", lambda e, i=i: e.transpose(
                        PT[:, i * 128:(i + 1) * 128],
                        qtok[:, i * 128:(i + 1) * 128], id_sb[:, :]),
                      waits=(f"qev{t}", f"xnTev{t}") if i == 0 else (),
                      mark=f"qTmm{t}" if i == 7 else None)
                E("ve", lambda e: e.tensor_copy(qtile[:, :], PT[:, 0:1024]),
                  waits=(f"qTmm{t}",) + ((f"qtst{t-1}",) if t >= 1 else ()),
                  mark=f"qTev{t}")
                E("gp", lambda e, t=t: e.dma_start(
                    out=qt_d.rearrange("p (c u) -> p c u", c=8)[
                        :, :, t * 128:(t + 1) * 128],
                    in_=qtile[:, :].rearrange("p (c u) -> p c u", c=8)),
                  waits=(f"qTev{t}",), mark=f"qtst{t}", dma=True, sem=f"gpqt{t%2}")
                tern_chunk(t)
                m_mms(xnb[:, b * D:(b + 1) * D], first=(t == 0),
                      wmarks=(f"jxld{t}",))
                for j in range(3):
                    u = NT_OWN + t * 3 + j
                    fb = u % 2
                    E("sy", lambda e, u=u, fb=fb: e.dma_start(
                        out=fbuf[:, fb * D:(fb + 1) * D],
                        in_=xnbf_d[u * 128:(u + 1) * 128, :]),
                      waits=(f"mmm{mseq[0]-2}",) if mseq[0] >= 2 else (),
                      mark=f"fld{u}", dma=True, sem=f"syf{u%2}")
                    m_mms(fbuf[:, fb * D:(fb + 1) * D], first=False,
                          wmarks=(f"fld{u}",),
                          last=(t == NT_OWN - 1 and j == 2))

            E("ve", lambda e: e.tensor_copy(mrow_f[0:1, :], PS[0:1, 1024:2048]),
              waits=(f"mmm{mseq[0]-1}",), mark="mev")
            E("ve", lambda e: e.tensor_copy(mrow_b[0:1, :], mrow_f[0:1, :]),
              mark="mbf", waits=("mev",))
            E("gp", lambda e: e.dma_start(out=mrow_d[:, :], in_=mrow_b[0:1, :]),
              waits=("mbf",), mark="mst", dma=True, sem="gpm")
            for c in range(8):
                E("gp", lambda e, c=c: e.dma_start(
                    out=mcol_b[:, c:c + 1],
                    in_=mrow_d[0:1, c * 128:(c + 1) * 128]),
                  waits=("mst",) if c == 0 else (), mark=f"mcol{c}", dma=True,
                  sem="gpm")
            if debug:
                E("gp", lambda e: e.dma_start(out=dbg["dbg_m"][:, :],
                                              in_=mrow_f[0:1, :]),
                  waits=("mev",), mark="dbgm", dma=True, sem="gpd")

            if PHASE_LIMIT < 3:
                E("ve", fn=None, waits=("mbf", f"qtst{NT_OWN-1}",
                                        f"xqst{NT_OWN-1}", f"mcol7"))
                return
            for r in range(2):
                for i in range(8):
                    k = r * 8 + i
                    f = 4 + k // 4
                    cb = k % 4
                    E("te", lambda e, i=i, f=f, cb=cb: e.transpose(
                        PT[:, i * 128:(i + 1) * 128],
                        g_bf[:, cb * 1024 + f * 128:cb * 1024 + (f + 1) * 128],
                        id_sb[:, :]),
                      waits=(("gh1ev3", f"qTev{NT_OWN-1}") if (r == 0 and i == 0)
                             else (("symev0",) if (r == 1 and i == 0) else ())),
                      mark=f"symt{r}" if i == 7 else None)
                fo = 4 + r * 2
                E("ve", lambda e, fo=fo: e.tensor_copy(
                    g_bf[:, fo * 1024:fo * 1024 + 512], PT[:, 0:512]),
                  waits=(f"symt{r}",))
                E("ve", lambda e, fo=fo: e.tensor_copy(
                    g_bf[:, (fo + 1) * 1024:(fo + 1) * 1024 + 512],
                    PT[:, 512:1024]),
                  mark=f"symev{r}")
            for u in range(NT_ALL):
                fb = u % 4
                E("sc" if u % 2 == 0 else "sy",
                  lambda e, u=u, fb=fb: e.dma_start(
                    out=fbuf4[:, fb * 512:(fb + 1) * 512],
                    in_=xnbf_d[u * 128:(u + 1) * 128, 512:1024]),
                  waits=(f"g2{u-4}",) if u >= 4 else (f"mmm{mseq[0]-1}",),
                  mark=f"f2ld{u}", dma=True, sem=f"syf{u%4}")
                for c in range(4):
                    E("te", lambda e, u=u, fb=fb, c=c: e.matmul(
                        PS[:, c * 512:(c + 1) * 512],
                        fbuf4[:, fb * 512 + c * 128:fb * 512 + (c + 1) * 128],
                        fbuf4[:, fb * 512:(fb + 1) * 512],
                        start=(u == 0), stop=(u == NT_ALL - 1)),
                      waits=((f"f2ld{u}", "mev", f"qTev{NT_OWN-1}")
                             if (c == 0 and u == 0) else
                             ((f"f2ld{u}",) if c == 0 else ())),
                      mark=f"g2{u}" if c == 3 else None)
                if u % 4 == 0:
                    tern_chunk(16 + u // 4)
            for c in range(4, 8):
                E("ve", lambda e, c=c: e.tensor_copy(
                    g_bf[:, c * 1024 + 512:(c + 1) * 1024],
                    PS[:, (c - 4) * 512:(c - 3) * 512]),
                  mark=f"gh2ev{c}",
                  waits=(f"g2{NT_ALL-1}", "symev1") if c == 4 else ())
            if debug:
                E("gp", lambda e: e.dma_start(out=dbg["dbg_G"][:, :],
                                              in_=g_bf[:, :]),
                  waits=("gh2ev7",), mark="dbgg", dma=True, sem="gpd")

            if PHASE_LIMIT < 4:
                E("ve", fn=None, waits=("gh2ev7", "wtst31"))
                return
            for idx, (dst, src) in enumerate([
                    (bkbv_b[0:1, 0:D], bk_d[0:1, :]),
                    (bkbv_b[0:1, D:2 * D], bk_d[1:2, :]),
                    (bkbv_b[0:1, 2 * D:3 * D], bv_d[0:1, :]),
                    (bkbv_b[0:1, 3 * D:4 * D], bv_d[1:2, :])]):
                E("gp", lambda e, d=dst, s=src: e.dma_start(out=d, in_=s),
                  waits=("wtst31", f"xqst{NT_OWN-1}") if idx == 0 else (),
                  mark=f"bkbvld{idx}", dma=True, sem="gpl")
            for k in range(2):
                E("sy", lambda e, k=k: e.dma_start(out=bvrow[0:1, :],
                                                   in_=bv_d[k:k + 1, :]),
                  waits=("vwld31", f"qtst{NT_OWN-1}") if k == 0
                  else (f"wradd{k-1}",),
                  mark=f"bvrowld{k}", dma=True, sem="syp")
                wvv = wv_d.rearrange("t (c p) j -> t p c j", p=128)[k]
                wkk = wk_d.rearrange("t (c p) j -> t p c j", p=128)[k]
                E("gp", lambda e, w=wvv: e.dma_start(
                    out=wkv_sb[:, :].rearrange("p (c j) -> p c j", c=8), in_=w),
                  waits=(f"smm{k-1}_{NH-1}",) if k >= 1 else ("bkbvld3",),
                  mark=f"wvld{k}", dma=True, sem="gpl")
                for c in range(8):
                    E("ve", lambda e, c=c: e.tensor_tensor(
                        wkv_sb[:, c * 1024 + c * 128:c * 1024 + (c + 1) * 128],
                        wkv_sb[:, c * 1024 + c * 128:c * 1024 + (c + 1) * 128],
                        id_sb[:, :], ALU.add),
                      waits=(f"wvld{k}", "gh2ev7") if c == 0 else (),
                      mark=f"wvf{k}" if c == 7 else None)
                for c in range(8):
                    for h in range(2):
                        E("te", lambda e, c=c, h=h: e.matmul(
                            PS[0:1, 2560 + h * 512:2560 + (h + 1) * 512],
                            mcol_b[:, c:c + 1],
                            wkv_sb[:, c * 1024 + h * 512:c * 1024 + (h + 1) * 512],
                            start=(c == 0), stop=(c == 7)),
                          waits=(f"wvf{k}", "mcol7", "swcols")
                          if (c == 0 and h == 0) else (),
                          mark=f"wrmm{k}" if (c == 7 and h == 1) else None)
                E("ve", lambda e, k=k: e.tensor_copy(wrow_f[0:1, :],
                                                PS[0:1, 2560:3584]),
                  waits=(f"wrmm{k}", f"bvrowld{k}"), mark=f"wrev{k}")
                E("ve", lambda e, k=k: e.scalar_tensor_tensor(
                    wrow_f[0:1, :], bvrow[0:1, :],
                    float(NTOK), wrow_f[0:1, :], ALU.mult, ALU.add),
                  mark=f"wradd{k}", waits=(f"wrev{k}",))
                E("ve", lambda e: e.tensor_copy(wrow_b[0:1, :], wrow_f[0:1, :]),
                  mark=f"wrbf{k}", waits=(f"wradd{k}",))
                for c in range(8):
                    pp = (c % 2) * 1024
                    prevt = (f"tev{k}_{c-2}",) if c >= 2 else \
                        ((f"tev{k-1}_{c+6}",) if k >= 1 else ())
                    for h in range(2):
                        E("te", lambda e, c=c, h=h, pp=pp, k=k: e.matmul(
                            PS[:, pp + h * 512:pp + (h + 1) * 512],
                            mrow_b[0:1, c * 128:(c + 1) * 128],
                            bkbv_b[0:1, (2 + k) * D + h * 512:
                                   (2 + k) * D + (h + 1) * 512],
                            start=True, stop=False),
                          waits=((f"wvf{k}", "mbf") + prevt) if h == 0 else ())
                    for f in range(8):
                        for h in range(2):
                            E("te", lambda e, c=c, f=f, h=h, pp=pp: e.matmul(
                                PS[:, pp + h * 512:pp + (h + 1) * 512],
                                g_bf[:, f * 1024 + c * 128:f * 1024 + (c + 1) * 128],
                                wkv_sb[:, f * 1024 + h * 512:
                                       f * 1024 + (h + 1) * 512],
                                start=False, stop=(f == 7)),
                              mark=f"tmm{k}_{c}" if (f == 7 and h == 1) else None)
                    E("ve", lambda e, c=c, pp=pp, k=k: e.tensor_copy(
                        t_bf[:, c * 1024:(c + 1) * 1024], PS[:, pp:pp + 1024]),
                      waits=(f"tmm{k}_{c}",), mark=f"tev{k}_{c}")
                E("gp", lambda e, w=wkk: e.dma_start(
                    out=wkv_sb[:, :].rearrange("p (c j) -> p c j", c=8), in_=w),
                  waits=(f"tmm{k}_7", f"wrmm{k}"),
                  mark=f"wkld{k}", dma=True, sem="gpl")
                for c in range(8):
                    E("ve", lambda e, c=c: e.tensor_tensor(
                        wkv_sb[:, c * 1024 + c * 128:c * 1024 + (c + 1) * 128],
                        wkv_sb[:, c * 1024 + c * 128:c * 1024 + (c + 1) * 128],
                        id_sb[:, :], ALU.add),
                      waits=(f"wkld{k}", f"tev{k}_7") if c == 0 else (),
                      mark=f"wkf{k}" if c == 7 else None)
                for h in range(NH):
                    E("te", lambda e, h=h, k=k: e.matmul(
                        PS[0:64, 2048 + h * 64:2048 + (h + 1) * 64],
                        bkbv_b[0:1, k * D + h * 64:k * D + (h + 1) * 64],
                        wrow_b[0:1, h * 64:(h + 1) * 64],
                        start=True, stop=False),
                      waits=(f"wrbf{k}", f"wkf{k}") if h == 0 else ())
                    for cc in range(8):
                        E("te", lambda e, h=h, cc=cc: e.matmul(
                            PS[0:64, 2048 + h * 64:2048 + (h + 1) * 64],
                            wkv_sb[:, cc * 1024 + h * 64:cc * 1024 + h * 64 + 64],
                            t_bf[:, cc * 1024 + h * 64:cc * 1024 + h * 64 + 64],
                            start=False, stop=(cc == 7)),
                          mark=f"smm{k}_{h}" if cc == 7 else None)
                E("ve", lambda e, k=k: e.tensor_copy(
                    states_sb[0:64, k * D:(k + 1) * D], PS[0:64, 2048:3072]),
                  waits=tuple(f"smm{k}_{h}" for h in range(NH)) +
                  (("wrev0",) if k == 0 else ()),
                  mark=f"stev{k}")
            E("ve", lambda e: e.tensor_tensor(
                states_sb[0:64, 0:D], states_sb[0:64, 0:D],
                states_sb[0:64, D:2 * D], ALU.mult),
              waits=("stev0", "stev1"), mark="stp1")
            E("ve", lambda e: e.tensor_scalar(
                statet_b[0:64, :], states_sb[0:64, 0:D], ISCALE2, None, ALU.mult),
              mark="stp", waits=("stp1",))
            E("ve", lambda e: e.memset(statet2[:, :], 0.0), mark="st2z")
            E("ve", lambda e: e.tensor_copy(
                statet2[0:64, 0:512].rearrange("p (j u) -> p j u", j=8),
                statet_b[0:64, :].rearrange("p (j x u) -> p j x u",
                                            j=8, x=2)[:, :, 0, :]),
              waits=("stp", "st2z"), mark="st2e")
            for j in range(8):
                E("te", lambda e, j=j: e.matmul(
                    PS[64:128, 2048 + j * 64:2048 + (j + 1) * 64],
                    id_sb[0:64, 0:64],
                    statet_b[0:64, j * 128 + 64:j * 128 + 128],
                    start=True, stop=True),
                  waits=("stp",) if j == 0 else (),
                  mark="st2mm" if j == 7 else None)
            E("ve", lambda e: e.tensor_copy(statet2[64:128, 512:1024],
                                            PS[64:128, 2048:2560]),
              waits=("st2mm",), mark="st2o")
            if debug:
                E("gp", lambda e: e.dma_start(out=dbg["dbg_stp"][:, :],
                                              in_=states_sb[0:64, 0:D]),
                  waits=("stp1",), mark="dbgstp", dma=True, sem="gpd")

            if PHASE_LIMIT < 5:
                E("ve", fn=None, waits=("st2o", "st2e"))
                return
            for c in range(8):
                E("gp", lambda e, c=c: e.dma_start(
                    out=w1q_sb[:, c * HID:(c + 1) * HID],
                    in_=w1q_d[c * 128:(c + 1) * 128, :]),
                  waits=(("stev1",) + (("dbgstp",) if debug else ()))
                  if c == 0 else (),
                  mark=f"w1qld{c}", dma=True, sem="gpl")
            E("ve", lambda e: e.tensor_scalar(
                scol_cols[:, :], rq1_cols[:, :],
                cols_f[:, C_S1F:C_S1F + 1], None, ALU.mult),
              mark="scolc", waits=("s1fc", f"c1m{NT_OWN-1}"))

            def b1Aload(t):
                b = t % 2
                E("gp", lambda e, t=t, b=b: e.dma_start(
                    out=xqb2[:, b * D:(b + 1) * D],
                    in_=xq_d[t * 128:(t + 1) * 128, :]),
                  waits=(f"xqst{t}",) + ((f"xqTmm{t-2}",) if t >= 2
                                         else ("stp",)),
                  mark=f"xqld{t}", dma=True, sem=f"gpb{t%2}")
                for i in range(8):
                    E("te", lambda e, b=b, i=i: e.transpose(
                        PT[:, i * 128:(i + 1) * 128],
                        xqb2[:, b * D + i * 128:b * D + (i + 1) * 128],
                        id_sb[:, :]),
                      waits=((f"xqld{t}",) +
                             ((f"xqTev{t-1}",) if t >= 1 else ("st2o",)) +
                             ((f"mm1_{t-2}_2",) if t >= 2 else ()))
                      if i == 0 else (),
                      mark=f"xqTmm{t}" if i == 7 else None)
                E("ve", lambda e, b=b: e.tensor_copy(
                    xqT_sb[:, b * D:(b + 1) * D], PT[:, 0:1024]),
                  waits=(f"xqTmm{t}",), mark=f"xqTev{t}")

            def b1Amm(t):
                b = t % 2
                groups = [(0, 3, 0), (3, 3, 1536), (6, 2, 0)]
                for g, (nq0, nn, off) in enumerate(groups):
                    for c in range(8):
                        for j in range(nn):
                            nq = nq0 + j
                            wl = []
                            if c == 0 and j == 0:
                                wl = [f"xqTev{t}", "w1qld7"]
                                if g == 2:
                                    wl.append(f"rev{t}_0")
                                elif t >= 1:
                                    wl.append(f"rev{t-1}_2" if g == 0
                                              else f"rev{t-1}_{g}")
                            E("te", lambda e, c=c, nq=nq, off=off, j=j,
                              b=b: e.matmul(
                                PS[:, off + j * 512:off + (j + 1) * 512],
                                xqT_sb[:, b * D + c * 128:
                                       b * D + (c + 1) * 128],
                                w1q_sb[:, c * HID + nq * 512:
                                       c * HID + (nq + 1) * 512],
                                start=(c == 0), stop=(c == 7)),
                              waits=wl,
                              mark=f"mm1_{t}_{g}" if (c == 7 and j == nn - 1)
                              else None)

            def b1S(t):
                b = t % 2
                gws = [(0, 1536, 0), (3, 1536, 1536), (6, 1024, 0)]
                for g, (nq0, gw, off) in enumerate(gws):
                    E("sc", lambda e, g=g, off=off, gw=gw, nq0=nq0,
                      t=t: e.activation(
                        r_sb[:, nq0 * 512:nq0 * 512 + gw], PS[:, off:off + gw],
                        AF.Relu, scale=scol_cols[:, t:t + 1]),
                      waits=(f"mm1_{t}_{g}",) +
                      (("scolc",) if (t == 0 and g == 0) else ()) +
                      ((f"u0m{t-1}_1",) if (t >= 1 and g == 0) else ()),
                      mark=f"rev{t}_{g}")
                if debug:
                    E("gp", lambda e, t=t: e.dma_start(
                        out=dbg["dbg_r"][t * 128:(t + 1) * 128, :],
                        in_=r_sb[:, 0:HID]),
                      waits=(f"rev{t}_2",), mark=f"dbgr{t}", dma=True,
                      sem="gdr")
                E("sc", lambda e: e.activation(
                    u0_sb[:, :], r_sb[:, :], AF.Square,
                    accum_out=qcols[:, 0:1]),
                  waits=(f"rev{t}_2",) +
                  ((f"u0p{t-1}", f"t2m{t-1}") if t >= 1 else ()),
                  mark=f"sqa{t}_0")
                E("sc", lambda e: e.activation(
                    r_sb[:, :], u0_sb[:, :], AF.Square,
                    accum_out=qcols[:, 2:3]),
                  mark=f"u0m{t}_1",
                  waits=de(f"sqa{t}_0") +
                  ((f"dbgr{t}",) if debug else ()))
            def b1V(t):
                b = t % 2
                E("ve", lambda e: e.tensor_tensor(
                    qcols[:, 2:3], qcols[:, 2:3], cols_f[:, C_A2:C_A2 + 1],
                    ALU.mult),
                  mark=f"k2{t}", waits=(f"u0m{t}_1",) +
                  (("a2c", "ebc") if t == 0 else ()))
                E("ve", lambda e: e.tensor_scalar(
                    u0_sb[:, :], u0_sb[:, :], cols_f[:, C_BOA:C_BOA + 1],
                    None, ALU.add),
                  mark=f"u0p{t}")
                E("ve", lambda e: e.scalar_tensor_tensor(
                    qcols[:, 0:1], qcols[:, 0:1], cols_f[:, C_AB2:C_AB2 + 1],
                    qcols[:, 2:3], ALU.mult, ALU.add),
                  mark=f"msm{t}", waits=de(f"k2{t}"))
                E("sc", lambda e: e.activation(
                    qcols[:, 4:5], qcols[:, 0:1], AF.Ln, scale=1.0 / HID,
                    bias=cols_f[:, C_EB:C_EB + 1]),
                  mark=f"ln2q{t}", waits=(f"msm{t}",))
                E("sc", lambda e: e.activation(
                    qcols[:, 4:5], qcols[:, 4:5], AF.Exp, scale=-0.5),
                  mark=f"rs2{t}", waits=de(f"ln2q{t}"))
                E("ve", lambda e: e.tensor_tensor(
                    t2f_sb[:, :], u0_sb[:, :], anw2_tile[:, :], ALU.mult),
                  waits=de(f"u0p{t}") + ((f"q0bm{t-1}",) if t >= 1 else ()),
                  mark=f"t2m{t}")
                E("ve", lambda e: e.memset(scr8[:, 0:1], 0.0))
                E("ve", lambda e: e.reduce_max(
                    qcols[:, 5:6], t2f_sb[:, :], AX.X, apply_absolute_value=True),
                  waits=de(f"t2m{t}"), mark=f"am2{t}")
                E("ve", lambda e: e.memset(scr8[:, 1:2], 0.0))
                E("ve", lambda e: e.tensor_scalar(
                    qcols[:, 5:6], qcols[:, 5:6], qcols[:, 4:5], 1e-5,
                    ALU.mult, ALU.max),
                  mark=f"yb{t}", waits=(f"rs2{t}",) + de(f"am2{t}"))
                E("ve", lambda e: e.reciprocal(qcols[:, 6:7], qcols[:, 5:6]),
                  mark=f"rc2{t}", waits=(f"yb{t}",))
                E("ve", lambda e, t=t: e.tensor_tensor(
                    ysc_cols[:, t:t + 1], qcols[:, 5:6],
                    cols_f[:, C_S2F:C_S2F + 1], ALU.mult),
                  mark=f"ysc{t}", waits=de(f"yb{t}") +
                  (("s2fc",) if t == 0 else ()))
                E("ve", lambda e: e.scalar_tensor_tensor(
                    qcols[:, 6:7], qcols[:, 6:7], 127.0, qcols[:, 4:5],
                    ALU.mult, ALU.mult),
                  mark=f"c2m{t}", waits=de(f"rc2{t}"))
                E("ve", lambda e: e.tensor_scalar(
                    q0b_sb[:, :], t2f_sb[:, :], qcols[:, 6:7], None, ALU.mult),
                  waits=(f"c2m{t}",) + ((f"x2m{t-1}_3",) if t >= 1 else ()),
                  mark=f"q0bm{t}")
            def b1Q(t):
                b = t % 2
                E("ve", lambda e: e.memset(scr8[:, 4:5], 0.0))
                for hh in range(2):
                    E("ve", lambda e, hh=hh: e.scalar_tensor_tensor(
                        e2h_sb[:, :], t2f_sb[:, hh * 2048:(hh + 1) * 2048],
                        qcols[:, 6:7], q0b_sb[:, hh * 2048:(hh + 1) * 2048],
                        ALU.mult, ALU.subtract),
                      waits=de(f"q0bm{t}") +
                      ((f"sg2a{t}_1", f"sg2{t}_1") if hh >= 1 else ()),
                      mark=f"e2m{t}_{hh}")
                    for qq in range(2):
                        qi = hh * 2 + qq
                        p = qi % 2
                        sgb = sB2_sb if p == 0 else sB2b_sb
                        prevq = ((f"x2m{t}_{qi-2}",) if qi >= 2 else
                                 ((f"x2m{t-1}_{qi+2}",) if t >= 1 else ()))
                        E("sc", lambda e, qq=qq, p=p: e.sign(
                            sA2_sb[:, p * D:(p + 1) * D],
                            e2h_sb[:, qq * 1024:(qq + 1) * 1024],
                            bias=cols_f[:, C_NHALF:C_NHALF + 1]),
                          mark=f"sg2a{t}_{qi}",
                          waits=(f"e2m{t}_{hh}",) + prevq)
                        E("sc", lambda e, qq=qq, sgb=sgb: e.sign(
                            sgb[:, :], e2h_sb[:, qq * 1024:(qq + 1) * 1024],
                            bias=cols_f[:, C_HALF:C_HALF + 1]),
                          mark=f"sg2{t}_{qi}")
                        E("ve", lambda e, sgb=sgb, p=p: e.tensor_tensor(
                            sgb[:, :], sA2_sb[:, p * D:(p + 1) * D],
                            sgb[:, :], ALU.add),
                          mark=f"xma{t}_{qi}",
                          waits=(f"sg2a{t}_{qi}", f"sg2{t}_{qi}") +
                          ((f"x2st{t-1}",) if (qi == 0 and t >= 1) else ()))
                        E("ve", lambda e, qi=qi, sgb=sgb: e.scalar_tensor_tensor(
                            x2b_sb[:, qi * 1024:(qi + 1) * 1024], sgb[:, :],
                            0.5, q0b_sb[:, qi * 1024:(qi + 1) * 1024],
                            ALU.mult, ALU.add),
                          mark=f"x2m{t}_{qi}",
                          waits=(f"xma{t}_{qi}",))
                E("gp", lambda e, t=t: e.dma_start(
                    out=x2_d[t * 128:(t + 1) * 128, :], in_=x2b_sb[:, :]),
                  waits=(f"x2m{t}_3",), mark=f"x2st{t}", dma=True, sem=f"gpx2{t%2}")

            b1Aload(0)
            b1Aload(1)
            b1Amm(0)
            b1S(0)
            b1V(0)
            for t in range(NT_OWN):
                if t + 2 < NT_OWN:
                    b1Aload(t + 2)
                if t + 1 < NT_OWN:
                    b1Amm(t + 1)
                    b1S(t + 1)
                b1Q(t)
                if t + 1 < NT_OWN:
                    b1V(t + 1)

            if PHASE_LIMIT < 6:
                E("ve", fn=None, waits=(f"x2st{NT_OWN-1}",))
                return
            for c in range(4):
                E("gp", lambda e, c=c: e.dma_start(
                    out=w2q_sb[:, c * 8 * D:(c + 1) * 8 * D].rearrange(
                        "p (a j) -> p a j", a=8),
                    in_=w2q_d.rearrange("(a p) j -> p a j", p=128)[
                        :, c * 8:(c + 1) * 8, :]),
                  waits=(f"x2st{NT_OWN-1}",) if c == 0
                  else (f"w2qld{c-1}",),
                  mark=f"w2qld{c}", dma=True, sem="gpl")
            E("gp", lambda e: e.dma_start(
                out=wo_sb[:, :].rearrange("p (c j) -> p c j", c=8),
                in_=wo_d.rearrange("(c p) j -> p c j", p=128)),
              mark="wold", waits=("w2qld3",), dma=True, sem="gpl")
            def b2A(t):
                b = t % 2
                s8 = b * HID
                E("gp", lambda e, t=t, s8=s8: e.dma_start(
                    out=x2bf2[:, s8:s8 + HID],
                    in_=x2_d[t * 128:(t + 1) * 128, :]),
                  waits=(f"x2st{t}",) +
                  ((f"x2Tmm{t-2}_1",) if t >= 2 else
                   (f"e2m{NT_OWN-1}_1", f"t2m{NT_OWN-1}")),
                  mark=f"x2ld{t}", dma=True, sem=f"gpx2{t%2}")
                E("sy", lambda e, t=t, b=b: e.dma_start(
                    out=qtb[:, b * D:(b + 1) * D].rearrange(
                        "p (c u) -> p c u", c=8),
                    in_=qt_d.rearrange("p (c u) -> p c u", c=8)[
                        :, :, t * 128:(t + 1) * 128]),
                  waits=((f"omm{t-2}",) if (t >= 2 and B2SUB >= 2) else ()) +
                  ((f"t2m{NT_OWN-1}",) if t < 2 else ()),
                  mark=f"qtld{t}", dma=True, sem=f"syq{t%2}")
                E("sy", lambda e, t=t, b=b: e.dma_start(
                    out=xbuf[:, b * D:(b + 1) * D],
                    in_=x_d[t * 128:(t + 1) * 128, :]),
                  waits=(f"resm{t-2}",) if t >= 2 else (),
                  mark=f"bxld{t}", dma=True, sem=f"syx{t%2}")
                for bt in range(2):
                    for i in range(16):
                        w0 = ()
                        if i == 0:
                            w0 = (f"x2ld{t}",)
                            if bt == 1:
                                w0 += (f"x2Tev{t}_0",)
                            elif t >= 1:
                                w0 += (f"x2Tev{t-1}_1", f"mm2_{t-1}")
                            else:
                                w0 += ("st2o",)
                        E("te", lambda e, bt=bt, i=i, s8=s8: e.transpose(
                            PT2[:, i * 128:(i + 1) * 128],
                            x2bf2[:, s8 + bt * 2048 + i * 128:
                                  s8 + bt * 2048 + (i + 1) * 128],
                            id_sb[:, :]),
                          waits=w0,
                          mark=f"x2Tmm{t}_{bt}" if i == 15 else None)
                    E("ve", lambda e, bt=bt: e.tensor_copy(
                        x2T_sb[:, bt * 2048:(bt + 1) * 2048], PT2[:, 0:2048]),
                      waits=(f"x2Tmm{t}_{bt}",) +
                      ((f"x2m{NT_OWN-1}_3", f"t2m{NT_OWN-1}")
                       if (t == 0 and bt == 0) else ()),
                      mark=f"x2Tev{t}_{bt}")
            def b2Amm(t):
                b = t % 2
                for cc in range(32):
                    wl = []
                    if t == 0 and cc % 8 == 0:
                        wl.append(f"w2qld{cc//8}")
                    if cc == 0:
                        wl += [f"x2Tev{t}_1"]
                        wl.append(f"yev{t-1}" if t >= 1 else f"rev{NT_OWN-1}_2")
                    E("te", lambda e, cc=cc: e.matmul(
                        PS[:, 0:512],
                        x2T_sb[:, cc * 128:(cc + 1) * 128],
                        w2q_sb[:, cc * 1024:cc * 1024 + 512],
                        start=(cc == 0), stop=(cc == 31)),
                      waits=wl)
                    E("te", lambda e, cc=cc: e.matmul(
                        PS[:, 512:1024],
                        x2T_sb[:, cc * 128:(cc + 1) * 128],
                        w2q_sb[:, cc * 1024 + 512:(cc + 1) * 1024],
                        start=(cc == 0), stop=(cc == 31)),
                      mark=f"mm2_{t}" if cc == 31 else None)

            def b2Batt(t):
                b = t % 2
                for h in range(NH if B2SUB >= 2 else 0):
                    wl = []
                    if h == 0:
                        wl = [f"qtld{t}", "st2o"]
                        if t >= 1:
                            wl.append(f"onTev{t-1}")
                    E("te", lambda e, h=h, b=b: e.matmul(
                        PS[:, 1024 + h * 64:1024 + (h + 1) * 64],
                        qtb[0:128,
                            b * D + (h // 2) * 128:b * D + (h // 2) * 128 + 128],
                        statet2[0:128,
                                (h % 2) * 512 + (h // 2) * 64:
                                (h % 2) * 512 + (h // 2) * 64 + 64],
                        start=True, stop=True),
                      waits=wl, mark=f"omm{t}" if h == NH - 1 else None)

            def b2B(t):
                b = t % 2
                E("sc", lambda e, t=t: e.activation(
                    y_sb[:, :], PS[:, 0:1024], AF.Copy,
                    scale=ysc_cols[:, t:t + 1]),
                  waits=(f"mm2_{t}", f"ysc{t}") +
                  ((f"gm{t-1}",) if t >= 1
                   else (f"t2m{NT_OWN-1}", f"sg2{NT_OWN-1}_3")) +
                  ((f"dbgy{t-1}",) if (debug and t >= 1) else ()),
                  mark=f"yev{t}")
                E("sc", lambda e: e.activation(
                    th_sb[:, :], y_sb[:, :], AF.Tanh, scale=0.5),
                  mark=f"thm{t}", waits=de(f"yev{t}"))
                E("ve", lambda e: e.tensor_scalar(
                    gh_sb[:, :], th_sb[:, :], 0.5, 0.5, ALU.mult, ALU.add),
                  mark=f"ghm{t}", waits=(f"thm{t}",))
                E("ve", lambda e: e.tensor_tensor(
                    g_t[:, :], gh_sb[:, :], y_sb[:, :], ALU.mult),
                  mark=f"gm{t}",
                  waits=(f"ghm{t}",) +
                  ((f"dbggt{t-1}",) if (debug and t >= 1) else ()))
                if B2SUB < 2:
                    E("ve", lambda e, t=t, b=b: e.tensor_tensor(
                        res_f[:, :], g_t[:, :], xbuf[:, b * D:(b + 1) * D],
                        ALU.add),
                      waits=(f"bxld{t}", f"gm{t}") +
                      ((f"outst{t-1}",) if t >= 1 else ()),
                      mark=f"resm{t}")
                    E("sy", lambda e, t=t: e.dma_start(
                        out=out_d[t * 128:(t + 1) * 128, :], in_=res_f[:, :]),
                      waits=(f"resm{t}",), mark=f"outst{t}", dma=True, sem="syo")
                    return
                E("ve", lambda e: e.tensor_copy(
                    outa_sb[:, :], PS[:, 1024:2048]),
                  waits=(f"omm{t}",) +
                  ((f"dbgoa{t-1}",) if (debug and t >= 1)
                   else ((f"x2m{NT_OWN-1}_3",) if t == 0 else ())),
                  mark=f"oev{t}")
                E("ve", lambda e: e.reduce_sum(
                    lncols[:, 9:10], PS[:, 1024:2048], AX.X),
                  mark=f"orow{t}")
                E("ve", lambda e: e.tensor_tensor(
                    sqscr[:, 0:D], outa_sb[:, :], outa_sb[:, :], ALU.mult),
                  mark=f"osqm{t}", waits=(f"oev{t}",))
                E("ve", lambda e: e.tensor_scalar(
                    lncols[:, 4:5], lncols[:, 9:10], 1.0 / D, None, ALU.mult),
                  mark=f"oca{t}", waits=(f"orow{t}",))
                E("ve", lambda e: e.reduce_sum(
                    lncols[:, 10:11], sqscr[:, 0:D], AX.X),
                  mark=f"osq{t}", waits=(f"osqm{t}",))
                E("ve", lambda e: e.scalar_tensor_tensor(
                    lncols[:, 6:7], lncols[:, 4:5], 1.0, lncols[:, 4:5],
                    ALU.bypass, ALU.mult),
                  mark=f"occ{t}", waits=(f"oca{t}",))
                E("ve", lambda e: e.tensor_scalar(
                    lncols[:, 5:6], lncols[:, 10:11], 1.0 / D, None, ALU.mult),
                  mark=f"ocb{t}", waits=(f"osq{t}",))
                E("ve", lambda e: e.tensor_tensor(
                    lncols[:, 6:7], lncols[:, 5:6], lncols[:, 6:7], ALU.subtract),
                  mark=f"ocm{t}", waits=(f"ocb{t}", f"occ{t}"))
                E("sc", lambda e: e.activation(
                    lncols[:, 7:8], lncols[:, 6:7], AF.Ln,
                    bias=cols_f[:, C_EPSLN:C_EPSLN + 1]),
                  mark=f"oln{t}", waits=(f"ocm{t}",))
                E("sc", lambda e: e.activation(
                    lncols[:, 7:8], lncols[:, 7:8], AF.Exp, scale=-0.5),
                  mark=f"orstd{t}", waits=de(f"oln{t}"))
                E("ve", lambda e: e.scalar_tensor_tensor(
                    lncols[:, 8:9], lncols[:, 4:5], -1.0, lncols[:, 7:8],
                    ALU.mult, ALU.mult),
                  waits=(f"orstd{t}",), mark=f"onmr{t}")
                E("ve", lambda e: e.tensor_scalar(
                    outn_sb[:, :], outa_sb[:, :],
                    lncols[:, 7:8], lncols[:, 8:9], ALU.mult, ALU.add),
                  waits=(f"onmr{t}",) + ((f"onTmm{t-1}",) if t >= 1 else ()),
                  mark=f"onorm{t}")
                E("ve", lambda e: e.tensor_tensor(
                    outn_sb[:, :], outn_sb[:, :], lng_tile[:, :], ALU.mult),
                  mark=f"og1{t}", waits=(f"onorm{t}",))
                E("ve", lambda e: e.tensor_tensor(
                    outn_sb[:, :], outn_sb[:, :], lnb_tile[:, :], ALU.add),
                  mark=f"ognb{t}", waits=(f"og1{t}",))
                for i in range(8):
                    w0 = ()
                    if i == 0:
                        w0 = (f"ognb{t}", f"oev{t}")
                        if t >= 1:
                            w0 += (f"onTev{t-1}",)
                    E("te", lambda e, i=i: e.transpose(
                        PTo[:, i * 128:(i + 1) * 128],
                        outn_sb[:, i * 128:(i + 1) * 128], id_sb[:, :]),
                      waits=w0, mark=f"onTmm{t}" if i == 7 else None)
                E("ve", lambda e: e.tensor_copy(outnT_sb[:, :], PTo[:, 0:1024]),
                  waits=(f"onTmm{t}",), mark=f"onTev{t}")
                for c in range(8):
                    wl = []
                    if c == 0:
                        wl = [f"onTev{t}", "wold"]
                        if t >= 1:
                            wl.append(f"sev{t-1}")
                    E("te", lambda e, c=c: e.matmul(
                        PS[:, 2048:2560],
                        outnT_sb[:, c * 128:(c + 1) * 128],
                        wo_sb[:, c * 1024:c * 1024 + 512],
                        start=(c == 0), stop=(c == 7)),
                      waits=wl)
                    E("te", lambda e, c=c: e.matmul(
                        PS[:, 2560:3072],
                        outnT_sb[:, c * 128:(c + 1) * 128],
                        wo_sb[:, c * 1024 + 512:(c + 1) * 1024],
                        start=(c == 0), stop=(c == 7)),
                      mark=f"wmm{t}" if c == 7 else None)
                E("ve", lambda e: e.tensor_tensor(
                    s_sb[:, :], PS[:, 2048:3072], bo_tile[:, :], ALU.add),
                  waits=(f"wmm{t}",) +
                  ((f"x2st{NT_OWN-1}",) if t == 0 else ()),
                  mark=f"sev{t}")
                E("ve", lambda e: e.tensor_tensor(
                    sg_sb[:, :], s_sb[:, :], g_t[:, :], ALU.mult),
                  mark=f"sgm{t}", waits=(f"gm{t}", f"sev{t}"))
                E("ve", lambda e, t=t, b=b: e.tensor_tensor(
                    res_f[:, :], sg_sb[:, :], xbuf[:, b * D:(b + 1) * D],
                    ALU.add),
                  waits=(f"sgm{t}", f"bxld{t}") +
                  ((f"outst{t-1}",) if t >= 1 else ()),
                  mark=f"resm{t}")
                E("sy", lambda e, t=t: e.dma_start(
                    out=out_d[t * 128:(t + 1) * 128, :], in_=res_f[:, :]),
                  waits=(f"resm{t}",), mark=f"outst{t}", dma=True, sem="syo")
                if debug:
                    E("gp", lambda e, t=t: e.dma_start(
                        out=dbg["dbg_y"][t * 128:(t + 1) * 128, :],
                        in_=y_sb[:, :]),
                      waits=(f"gm{t}",), mark=f"dbgy{t}", dma=True, sem="gdy")
                    E("gp", lambda e, t=t: e.dma_start(
                        out=dbg["dbg_oa"][t * 128:(t + 1) * 128, :],
                        in_=outa_sb[:, :]),
                      waits=(f"osq{t}",), mark=f"dbgoa{t}", dma=True, sem="gdo")
                    E("gp", lambda e, t=t: e.dma_start(
                        out=dbg["dbg_g"][t * 128:(t + 1) * 128, :],
                        in_=g_t[:, :]),
                      waits=(f"resm{t}",), mark=f"dbggt{t}", dma=True,
                      sem="gdg")

            b2A(0)
            b2Amm(0)
            for t in range(NT_OWN):
                if t + 1 < NT_OWN:
                    b2A(t + 1)
                b2Batt(t)
                if t + 1 < NT_OWN:
                    b2Amm(t + 1)
                b2B(t)

            if debug:
                E("gp", lambda e: e.dma_start(out=dbg["dbg_st2"][:, :],
                                              in_=statet2[:, :]),
                  waits=tuple(f"outst{t}" for t in range(NT_OWN)),
                  mark="dbgst2", dma=True, sem="gpd")
                E("ve", fn=None, waits=("dbgst2",))
            E("ve", fn=None, waits=tuple(f"outst{t}" for t in range(NT_OWN)))
            E("ve", fn=None, waits=(f"qtst{NT_OWN-1}", "wtst31",
                                    f"x2st{NT_OWN-1}"))

        em.serial = SERIAL
        em.mode = "count"
        em.auto = 0
        em.auto_s = 0
        em.prev_mark = None
        prog()
        em.mode = "emit"
        em.sems = _sems

        def mk(eng):
            def body(e):
                em.cur_eng = eng
                em.handles = e
                em.auto = 0
                em.auto_s = 0
                em.prev_mark = None
                prog()
            return body
        blk.sync(mk("sy"))
        blk.gpsimd(mk("gp"))
        blk.scalar(mk("sc"))
        blk.vector(mk("ve"))
        blk.tensor(mk("te"))

    return nc


def kernel(L0_attn_x, L0_attn_Wq, L0_attn_bq, L0_attn_Wk, L0_attn_bk,
           L0_attn_Wv, L0_attn_bv, L0_attn_Wo, L0_attn_bo, L0_attn_ln_g,
           L0_attn_ln_b, L0_ffn_norm1_w, L0_ffn_w1, L0_ffn_act_scale,
           L0_ffn_act_bias, L0_ffn_norm2_w, L0_ffn_w2, _debug=False,
           _build_only=False, _detect_races=False):
    x = np.asarray(L0_attn_x, dtype=np.float32)
    assert x.shape == (2, NTOK, D)

    def f32c(a, shape=None):
        a = np.asarray(a, dtype=np.float32)
        if shape is not None:
            a = a.reshape(shape)
        return np.ascontiguousarray(a)

    common = {
        "wq": f32c(L0_attn_Wq), "wk": f32c(L0_attn_Wk), "wv": f32c(L0_attn_Wv),
        "wo": f32c(L0_attn_Wo), "bq": f32c(L0_attn_bq, (1, D)),
        "bk": f32c(L0_attn_bk), "bv": f32c(L0_attn_bv),
        "bo": f32c(L0_attn_bo, (1, D)), "lng": f32c(L0_attn_ln_g, (1, D)),
        "lnb": f32c(L0_attn_ln_b, (1, D)), "nw1": f32c(L0_ffn_norm1_w, (1, D)),
        "nw2": f32c(L0_ffn_norm2_w, (1, HID)),
        "acts": f32c(L0_ffn_act_scale, (1, 1)),
        "actb": f32c(L0_ffn_act_bias, (1, 1)),
        "w1t": np.ascontiguousarray(np.asarray(L0_ffn_w1, dtype=np.float32).T),
        "w2t": np.ascontiguousarray(np.asarray(L0_ffn_w2, dtype=np.float32).T),
    }
    in_maps = []
    for core in range(8):
        b, slot = divmod(core, 4)
        xb = x[b]
        xrot = np.ascontiguousarray(
            np.concatenate([xb[slot * OWN:(slot + 1) * OWN],
                            xb[:slot * OWN], xb[(slot + 1) * OWN:]], axis=0))
        m = dict(common)
        m["x"] = xrot
        in_maps.append(m)

    nc = build(debug=_debug, detect_races=_detect_races)
    if _build_only:
        return nc, in_maps
    res = run_bass_kernel_spmd(nc, in_maps, core_ids=list(range(8)))
    out = np.empty((2, NTOK, D), dtype=np.float32)
    for core in range(8):
        b, slot = divmod(core, 4)
        out[b, slot * OWN:(slot + 1) * OWN] = np.asarray(res.results[core]["out"])
    return out


# version 28 (final)
# speedup vs baseline: 1.1130x; 1.0054x over previous
import numpy as np
import ml_dtypes
from contextlib import ExitStack
import concourse.bass as bass
import concourse.mybir as mybir
from concourse.bass_utils import run_bass_kernel_spmd

F32 = mybir.dt.float32
BF16 = mybir.dt.bfloat16
I8 = mybir.dt.int8
AF = mybir.ActivationFunctionType
ALU = mybir.AluOpType
AX = mybir.AxisListType

NTOK = 8192
OWN = 2048
NT_ALL = 64
NT_OWN = 16
D = 1024
HID = 4096
NH = 16
ISCALE2 = float(2.0 ** -24)

ENGS = ("sy", "gp", "sc", "ve", "te")

PHASE_LIMIT = 6
B2SUB = 2
SERIAL = False

ARENA_BYTES = 65536
CARVES = {
    "wstage":  (1, 2, 0,     16384),
    "xnb":     (2, 2, 16384, 4096),
    "xbb":     (1, 1, 20480, 32768),
    "wdump":   (1, 1, 53248, 8192),
    "fbuf":    (2, 2, 20480, 4096),
    "fbuf4":   (3, 3, 16384, 8192),
    "mrow_f":  (2, 4, 32768, 4096),
    "mrow_b":  (2, 4, 36864, 2048),
    "lngrow":  (0, 0, 20480, 2048),
    "lnbrow":  (0, 0, 22528, 2048),
    "nw2row":  (0, 0, 38912, 16384),
    "nw1row":  (0, 0, 55296, 4096),
    "borow":   (0, 0, 59392, 4096),
    "bqrow":   (0, 0, 63488, 2048),
    "t1s":     (2, 2, 38912, 4096),
    "e1s":     (2, 2, 43008, 4096),
    "xnT":     (2, 2, 51200, 2048),
    "q0s":     (2, 2, 53248, 1024),
    "sAs":     (2, 2, 54272, 2048),
    "sBs":     (2, 2, 56320, 2048),
    "adjs":    (2, 2, 58368, 2048),
    "xq8":     (2, 2, 60416, 1024),
    "qtok":    (2, 2, 61440, 2048),
    "qtile":   (2, 2, 63488, 2048),
    "wsB":     (2, 2, 24576, 4096),
    "wtrn":    (2, 2, 28672, 2048),
    "wkv":     (4, 4, 0,     16384),
    "t_bf":    (4, 4, 16384, 16384),
    "states":  (4, 4, 38912, 8192),
    "wrow_f":  (4, 4, 47104, 4096),
    "wrow_b":  (4, 4, 51200, 2048),
    "bkbv":    (4, 4, 53248, 8192),
    "bvrow":   (4, 4, 61440, 4096),
    "t2f":     (5, 5, 0,     16384),
    "u0":      (5, 5, 16384, 16384),
    "e2h":     (5, 5, 32768, 8192),
    "sA2":     (5, 5, 40960, 4096),
    "sB2":     (5, 5, 45056, 2048),
    "sB2b":    (5, 5, 63488, 2048),
    "q0b":     (5, 5, 47104, 4096),
    "x2b":     (5, 5, 51200, 4096),
    "xqb2":    (5, 5, 55296, 4096),
    "xqT":     (5, 5, 59392, 4096),
    "x2bf2":   (6, 6, 0,     16384),
    "x2T":     (6, 6, 16384, 8192),
    "qtb":     (6, 6, 24576, 4096),
    "ys":      (6, 6, 28672, 4096),
    "th":      (6, 6, 32768, 4096),
    "gh":      (6, 6, 36864, 4096),
    "g_t":     (6, 6, 40960, 4096),
    "outa":    (6, 6, 45056, 4096),
    "outn":    (6, 6, 49152, 2048),
    "outnT":   (6, 6, 51200, 2048),
    "s_s":     (6, 6, 53248, 4096),
    "sg":      (6, 6, 57344, 4096),
    "res":     (6, 6, 61440, 4096),
}


def _check_carves():
    items = list(CARVES.items())
    for i in range(len(items)):
        n1, (l1, h1, o1, b1) = items[i]
        assert o1 + b1 <= ARENA_BYTES, n1
        for j in range(i + 1, len(items)):
            n2, (l2, h2, o2, b2) = items[j]
            if max(l1, l2) <= min(h1, h2):
                assert o1 + b1 <= o2 or o2 + b2 <= o1, f"arena overlap {n1} {n2}"


_check_carves()


class Emitter:

    def __init__(self):
        self.mode = "count"
        self.cnt = {k: 0 for k in ENGS}
        self.marks = {}
        self.cur_eng = None
        self.handles = None
        self.sems = None
        self.auto = 0
        self.serial = False
        self.auto_s = 0
        self.prev_mark = None

    def op(self, eng, fn=None, mark=None, waits=(), dma=False, sem=None):
        key = sem if sem is not None else eng
        if dma and mark is None:
            mark = f"_dma{self.auto}"
            self.auto += 1
        if self.serial and fn is not None:
            if mark is None:
                mark = f"_ser{self.auto_s}"
                self.auto_s += 1
            if self.prev_mark is not None:
                waits = tuple(waits) + (self.prev_mark,)
            self.prev_mark = mark
        if self.mode == "count":
            if mark is not None:
                assert mark not in self.marks, f"dup mark {mark}"
                self.cnt[key] = self.cnt.get(key, 0) + (16 if dma else 1)
                self.marks[mark] = (key, self.cnt[key])
            return
        if eng != self.cur_eng:
            return
        e = self.handles
        last = None
        for w in waits:
            sem_key, val = self.marks[w]
            if last != (sem_key, val):
                e.wait_ge(self.sems[sem_key], val)
            last = (sem_key, val)
        if fn is not None:
            inst = fn(e)
            if mark is not None:
                inst.then_inc(self.sems[key], 16 if dma else 1)


def build(debug=False, detect_races=False):
    nc = bass.Bass("TRN2", target_bir_lowering=False,
                   detect_race_conditions=detect_races)

    x_d = nc.dram_tensor("x", [NTOK, D], F32, kind="ExternalInput").ap()
    wq_d = nc.dram_tensor("wq", [D, D], F32, kind="ExternalInput").ap()
    wk_d = nc.dram_tensor("wk", [2, D, D], F32, kind="ExternalInput").ap()
    wv_d = nc.dram_tensor("wv", [2, D, D], F32, kind="ExternalInput").ap()
    wo_d = nc.dram_tensor("wo", [D, D], F32, kind="ExternalInput").ap()
    bq_d = nc.dram_tensor("bq", [1, D], F32, kind="ExternalInput").ap()
    bk_d = nc.dram_tensor("bk", [2, D], F32, kind="ExternalInput").ap()
    bv_d = nc.dram_tensor("bv", [2, D], F32, kind="ExternalInput").ap()
    bo_d = nc.dram_tensor("bo", [1, D], F32, kind="ExternalInput").ap()
    lng_d = nc.dram_tensor("lng", [1, D], F32, kind="ExternalInput").ap()
    lnb_d = nc.dram_tensor("lnb", [1, D], F32, kind="ExternalInput").ap()
    nw1_d = nc.dram_tensor("nw1", [1, D], F32, kind="ExternalInput").ap()
    nw2_d = nc.dram_tensor("nw2", [1, HID], F32, kind="ExternalInput").ap()
    acts_d = nc.dram_tensor("acts", [1, 1], F32, kind="ExternalInput").ap()
    actb_d = nc.dram_tensor("actb", [1, 1], F32, kind="ExternalInput").ap()
    w1t_d = nc.dram_tensor("w1t", [D, HID], F32, kind="ExternalInput").ap()
    w2t_d = nc.dram_tensor("w2t", [HID, D], F32, kind="ExternalInput").ap()
    out_d = nc.dram_tensor("out", [OWN, D], F32, kind="ExternalOutput").ap()

    ikind = "ExternalOutput" if debug else "Internal"
    xnbf_d = nc.dram_tensor("xnbf", [NTOK, D], BF16, kind=ikind).ap()
    xq_d = nc.dram_tensor("xqd", [OWN, D], I8, kind=ikind).ap()
    qt_d = nc.dram_tensor("qtd", [128, 8 * OWN], BF16, kind=ikind).ap()
    x2_d = nc.dram_tensor("x2d", [OWN, HID], I8, kind=ikind).ap()
    w1q_d = nc.dram_tensor("w1qd", [D, HID], I8, kind=ikind).ap()
    w2q_d = nc.dram_tensor("w2qd", [HID, D], I8, kind=ikind).ap()
    mrow_d = nc.dram_tensor("mrowd", [1, D], BF16, kind=ikind).ap()

    dbg = {}
    if debug:
        for name, shape, dt in [
            ("dbg_G", [128, 8 * D], BF16), ("dbg_m", [1, D], F32),
            ("dbg_stp", [64, D], F32),
            ("dbg_r", [NT_OWN * 128, HID], F32),
            ("dbg_y", [NT_OWN * 128, D], F32),
            ("dbg_oa", [NT_OWN * 128, D], F32),
            ("dbg_g", [NT_OWN * 128, D], F32),
            ("dbg_st2", [128, 1024], BF16),
        ]:
            dbg[name] = nc.dram_tensor(name, shape, dt, kind="ExternalOutput").ap()

    id_dram = nc.inline_tensor(np.eye(128, dtype=ml_dtypes.bfloat16), name="identbf")
    ones_dram = nc.inline_tensor(np.ones((1, 128), dtype=ml_dtypes.bfloat16), name="onesbf")
    onesf_dram = nc.inline_tensor(np.ones((1, 128), dtype=np.float32), name="onesf")
    onesbc_dram = nc.inline_tensor(np.ones((128, 1), dtype=ml_dtypes.bfloat16), name="onesbc")
    onesfc_dram = nc.inline_tensor(np.ones((128, 1), dtype=np.float32), name="onesfc")

    (C_A, C_B, C_BOA, C_EPSLN, C_EPSRMS, C_TH1, C_TH2, C_HALF, C_NHALF,
     C_S1W, C_S2W, C_NTH1, C_NTH2, C_EPSQ, C_A2, C_AB2, C_EB, C_S2F,
     C_S1F, C_T0) = range(20)

    em = Emitter()
    es = ExitStack()
    with es:
        def T(name, shape, dtype):
            return es.enter_context(nc.sbuf_tensor(name, shape, dtype))

        id_sb = T("id_sb", [128, 128], BF16)
        ones_sb = T("ones_sb", [1, 128], BF16)
        onesf_sb = T("onesf_sb", [1, 128], F32)
        onesbc_sb = T("onesbc_sb", [128, 1], BF16)
        onesfc_sb = T("onesfc_sb", [128, 1], F32)
        scal_f = T("scal_f", [1, 16], F32)
        cols_f = T("cols_f", [128, 20], F32)
        rq1_cols = T("rq1_cols", [128, NT_OWN], F32)
        scol_cols = T("scol_cols", [128, NT_OWN], F32)
        c1_cols = T("c1_cols", [128, NT_OWN], F32)
        ysc_cols = T("ysc_cols", [128, NT_OWN], F32)
        lncols = T("lncols", [128, 12], F32)
        qcols = T("qcols", [128, 8], F32)
        wstat = T("wstat", [128, 32], F32)
        stat_s = T("stat_s", [128, NT_ALL], F32)
        stat_r = T("stat_r", [128, NT_ALL], F32)
        mean_all = T("mean_all", [128, NT_ALL], F32)
        var_all = T("var_all", [128, NT_ALL], F32)
        rstd_all = T("rstd_all", [128, NT_ALL], F32)
        negm_all = T("negm_all", [128, NT_ALL], F32)
        scr8 = T("scr8", [128, 8], F32)
        rsq16 = T("rsq16", [128, NT_OWN], F32)
        nw1b_tile = T("nw1b", [128, D], BF16)
        mcol_b = T("mcol_b", [128, 8], BF16)
        statet_b = T("statet_b", [64, D], BF16)
        statet2 = T("statet2", [128, 1024], BF16)
        bq_tile = T("bq_tile", [128, D], BF16)
        lng_tile = T("lng_tile", [128, D], BF16)
        lnb_tile = T("lnb_tile", [128, D], BF16)
        bo_tile = T("bo_tile", [128, D], F32)
        nw1_tile = T("nw1_tile", [128, D], F32)
        anw2_tile = T("anw2_tile", [128, HID], F32)
        xbuf = T("xbuf", [128, 4 * D], F32)
        sqscr = T("sqscr", [128, 2048], F32)
        wq_sb = T("wq_sb", [128, 8 * D], BF16)
        w1q_sb = T("w1q_sb", [128, 8 * HID], BF16)
        arena = T("arena", [128, ARENA_BYTES // 2], BF16)

        STREAMS = ("sy", "gp", "sc", "ve", "te", "syp", "syo", "gpl",
                   "gpt", "gpm", "gpd", "gpp",
                   "syx0", "syx1", "syf0", "syf1", "syw0", "syw1",
                   "sy20", "sy21", "syq0", "syq1", "gpn0", "gpn1",
                   "gpw0", "gpw1", "gpxq0", "gpxq1", "gpqt0", "gpqt1",
                   "gpx20", "gpx21", "gdy", "gdo", "gdg", "gdr",
                   "gpb0", "gpb1", "syx2", "syx3", "syf2", "syf3")
        _sems = {k: es.enter_context(nc.semaphore("sem_" + k)) for k in STREAMS}
        PS = es.enter_context(nc.psum_tensor("PS", [128, 4096], F32))
        blk = es.enter_context(nc.Block())

        def carve(name, rows, cols, dtype):
            lo, hi, off, nbytes = CARVES[name]
            esz = {F32: 4, BF16: 2, I8: 1}[dtype]
            assert cols * esz <= nbytes, (name, cols, esz, nbytes)
            v = arena[:, off // 2:(off + cols * esz) // 2]
            v = v.bitcast(dtype)
            return v[0:rows, :]

        wstage = carve("wstage", 128, 4096, F32)
        xnb = carve("xnb", 128, 2 * D, BF16)
        xbb = carve("xbb", 128, 16 * D, BF16)
        wdump = carve("wdump", 128, 2048, F32)
        fbuf = carve("fbuf", 128, 2 * D, BF16)
        fbuf4 = carve("fbuf4", 128, 4 * D, BF16)
        mrow_f = carve("mrow_f", 1, D, F32)
        mrow_b = carve("mrow_b", 1, D, BF16)
        lngrow = carve("lngrow", 1, D, BF16)
        lnbrow = carve("lnbrow", 1, D, BF16)
        nw2row = carve("nw2row", 1, HID, F32)
        nw1row = carve("nw1row", 1, D, F32)
        borow = carve("borow", 1, D, F32)
        bqrow = carve("bqrow", 1, D, BF16)
        t1b = carve("t1s", 128, D, BF16)
        e1s = carve("e1s", 128, D, F32)
        xnT = carve("xnT", 128, D, BF16)
        q0s = carve("q0s", 128, D, I8)
        sAs = carve("sAs", 128, D, BF16)
        sBs = carve("sBs", 128, D, BF16)
        adjs = carve("adjs", 128, D, BF16)
        xq8 = carve("xq8", 128, D, I8)
        qtok = carve("qtok", 128, D, BF16)
        qtile = carve("qtile", 128, D, BF16)
        wsB = carve("wsB", 128, 2048, BF16)
        wtrn = carve("wtrn", 128, 2048, I8)
        wkv_sb = carve("wkv", 128, 8 * D, BF16)
        t_bf = carve("t_bf", 128, 8 * D, BF16)
        states_sb = carve("states", 64, 2 * D, F32)
        wrow_f = carve("wrow_f", 1, D, F32)
        wrow_b = carve("wrow_b", 1, D, BF16)
        bkbv_b = carve("bkbv", 1, 4 * D, BF16)
        bvrow = carve("bvrow", 1, D, F32)
        t2f_sb = carve("t2f", 128, HID, F32)
        u0_sb = carve("u0", 128, HID, F32)
        e2h_sb = carve("e2h", 128, 2048, F32)
        sA2_sb = carve("sA2", 128, 2 * D, BF16)
        sB2_sb = carve("sB2", 128, D, BF16)
        sB2b_sb = carve("sB2b", 128, D, BF16)
        q0b_sb = carve("q0b", 128, HID, I8)
        x2b_sb = carve("x2b", 128, HID, I8)
        xqb2 = carve("xqb2", 128, 2 * D, BF16)
        xqT_sb = carve("xqT", 128, 2 * D, BF16)
        x2bf2 = carve("x2bf2", 128, 2 * HID, BF16)
        x2T_sb = carve("x2T", 128, HID, BF16)
        qtb = carve("qtb", 128, 2 * D, BF16)
        y_sb = carve("ys", 128, D, F32)
        th_sb = carve("th", 128, D, F32)
        gh_sb = carve("gh", 128, D, F32)
        g_t = carve("g_t", 128, D, F32)
        outa_sb = carve("outa", 128, D, F32)
        outn_sb = carve("outn", 128, D, BF16)
        outnT_sb = carve("outnT", 128, D, BF16)
        s_sb = carve("s_s", 128, D, F32)
        sg_sb = carve("sg", 128, D, F32)
        res_f = carve("res", 128, D, F32)

        wsA = sqscr[:, :].bitcast(BF16)[:, 0:2048]
        g_bf = w1q_sb[:, 0:8 * D]
        r_sb = wq_sb[:, 0:8 * D].bitcast(F32)
        w2q_sb = w1q_sb[:, 0:8 * HID]
        wo_sb = wq_sb[:, 0:8 * D]
        PT = PS[:, 3584:4096].bitcast(BF16)
        PT2 = PS[:, 3072:4096].bitcast(BF16)
        PTo = PS[:, 1024:2048].bitcast(BF16)

        def prog():
            E = em.op

            def de(*ws):
                return ws if detect_races else ()

            p0loads = [
                (id_sb[:, :], id_dram.ap()[:, :]),
                (ones_sb[:, :], ones_dram.ap()[:, :]),
                (onesf_sb[:, :], onesf_dram.ap()[:, :]),
                (onesbc_sb[:, :], onesbc_dram.ap()[:, :]),
                (onesfc_sb[:, :], onesfc_dram.ap()[:, :]),
                (nw1row[:, :], nw1_d[:, :]),
                (borow[:, :], bo_d[:, :]),
                (nw2row[:, :], nw2_d[:, :]),
                (scal_f[0:1, 0:1], acts_d[:, :]),
                (scal_f[0:1, 1:2], actb_d[:, :]),
            ]
            for i, (dst, src) in enumerate(p0loads):
                E("sy", lambda e, d=dst, s=src: e.dma_start(out=d, in_=s),
                  mark="p0ld" if i == len(p0loads) - 1 else None, dma=True,
                  sem="syp")
            p0casts = [
                (bqrow[:, :], bq_d[:, :]),
                (lngrow[:, :], lng_d[:, :]),
                (lnbrow[:, :], lnb_d[:, :]),
            ]
            for i, (dst, src) in enumerate(p0casts):
                E("gp", lambda e, d=dst, s=src: e.dma_start(out=d, in_=s),
                  mark="p0g" if i == len(p0casts) - 1 else None, dma=True,
                  sem="gpl")
            wq_view = wq_d.rearrange("(c p) j -> p c j", p=128)
            E("gp", lambda e: e.dma_start(
                out=wq_sb[:, :].rearrange("p (c j) -> p c j", c=8), in_=wq_view),
              mark="wqld", waits=("p0g",), dma=True, sem="gpl")

            def k1mm(e, outsl, lhsT, rhs, st):
                return e.matmul(outsl, lhsT, rhs, start=st, stop=True)

            for i, (row, dst) in enumerate([(bqrow, bq_tile), (lngrow, lng_tile),
                                            (lnbrow, lnb_tile)]):
                for h in range(2):
                    E("te", lambda e, i=i, h=h, row=row: k1mm(
                        e, PS[:, i * 1024 + h * 512:i * 1024 + (h + 1) * 512],
                        ones_sb[0:1, :], row[0:1, h * 512:(h + 1) * 512], st=True),
                      mark=f"bcmm{i}_{h}",
                      waits=("p0ld", "p0g") if (i == 0 and h == 0) else ())
                E("ve", lambda e, i=i, dst=dst: e.tensor_copy(
                    dst[:, :], PS[:, i * 1024:(i + 1) * 1024]),
                  mark=f"bcev{i}", waits=(f"bcmm{i}_0", f"bcmm{i}_1"))
            E("te", lambda e: k1mm(e, PS[:, 3072:3584], onesf_sb[0:1, :],
                                   nw1row[0:1, 0:512], True),
              mark="nw1a", waits=("p0ld",))
            E("te", lambda e: k1mm(e, PS[:, 0:512], onesf_sb[0:1, :],
                                   nw1row[0:1, 512:1024], True),
              mark="nw1b", waits=("bcev0",))
            E("ve", lambda e: e.tensor_copy(nw1_tile[:, 0:512], PS[:, 3072:3584]),
              mark="nw1ev1", waits=("nw1a",))
            E("ve", lambda e: e.tensor_copy(nw1_tile[:, 512:1024], PS[:, 0:512]),
              mark="nw1ev2", waits=("nw1b",))
            E("ve", lambda e: e.tensor_copy(nw1b_tile[:, :], nw1_tile[:, :]),
              mark="nw1bf", waits=("nw1ev2",))
            E("te", lambda e: k1mm(e, PS[:, 512:1024], onesf_sb[0:1, :],
                                   borow[0:1, 0:512], True),
              mark="boa", waits=("bcev0",))
            E("te", lambda e: k1mm(e, PS[:, 1024:1536], onesf_sb[0:1, :],
                                   borow[0:1, 512:1024], True),
              mark="bob", waits=("bcev1",))
            E("ve", lambda e: e.tensor_copy(bo_tile[:, 0:512], PS[:, 512:1024]),
              mark="boev1", waits=("boa",))
            E("ve", lambda e: e.tensor_copy(bo_tile[:, 512:1024], PS[:, 1024:1536]),
              mark="boev2", waits=("bob",))
            E("te", lambda e: k1mm(e, PS[:, 2560:2561], onesf_sb[0:1, :],
                                   scal_f[0:1, 0:1], True),
              mark="acol", waits=("bcev2", "p0ld"))
            E("te", lambda e: k1mm(e, PS[:, 2561:2562], onesf_sb[0:1, :],
                                   scal_f[0:1, 1:2], True), mark="bcol")
            E("ve", lambda e: e.tensor_copy(cols_f[:, C_A:C_B + 1],
                                            PS[:, 2560:2562]),
              mark="abev", waits=("acol", "bcol"))
            E("ve", lambda e: e.reciprocal(cols_f[:, C_BOA:C_BOA + 1],
                                           cols_f[:, C_A:C_A + 1]),
              mark="rcp0", waits=("abev",))
            E("ve", lambda e: e.tensor_tensor(cols_f[:, C_BOA:C_BOA + 1],
                                              cols_f[:, C_BOA:C_BOA + 1],
                                              cols_f[:, C_B:C_B + 1], ALU.mult),
              mark="boacol", waits=("rcp0",))
            E("ve", lambda e: e.tensor_tensor(
                cols_f[:, C_A2:C_A2 + 1], cols_f[:, C_A:C_A + 1],
                cols_f[:, C_A:C_A + 1], ALU.mult),
              mark="a2c", waits=("abev",))
            E("ve", lambda e: e.tensor_tensor(
                cols_f[:, C_AB2:C_AB2 + 1], cols_f[:, C_A:C_A + 1],
                cols_f[:, C_B:C_B + 1], ALU.mult),
              mark="ab2a")
            E("ve", lambda e: e.tensor_scalar(
                cols_f[:, C_AB2:C_AB2 + 1], cols_f[:, C_AB2:C_AB2 + 1],
                2.0, None, ALU.mult),
              mark="ab2c", waits=("ab2a",))
            E("ve", lambda e: e.tensor_tensor(
                cols_f[:, C_EB:C_EB + 1], cols_f[:, C_B:C_B + 1],
                cols_f[:, C_B:C_B + 1], ALU.mult),
              mark="eba")
            E("ve", lambda e: e.tensor_scalar(
                cols_f[:, C_EB:C_EB + 1], cols_f[:, C_EB:C_EB + 1],
                1e-8, None, ALU.add),
              mark="ebc", waits=("eba",))
            for cslot, val in [(C_EPSLN, 1e-5), (C_EPSRMS, 1e-8),
                               (C_HALF, 0.5), (C_NHALF, -0.5), (C_EPSQ, 1e-5)]:
                E("ve", lambda e, c=cslot, v=val: e.memset(cols_f[:, c:c + 1], v),
                  mark=f"ms{cslot}")
            for c in range(8):
                E("ve", lambda e, c=c: e.tensor_tensor(
                    wq_sb[:, c * 1024 + c * 128:c * 1024 + (c + 1) * 128],
                    wq_sb[:, c * 1024 + c * 128:c * 1024 + (c + 1) * 128],
                    id_sb[:, :], ALU.add),
                  waits=("wqld", "p0ld") if c == 0 else (),
                  mark="wqf" if c == 7 else None)
            for h in range(8):
                E("te", lambda e, h=h: k1mm(
                    e, PS[:, 3072:3584], onesf_sb[0:1, :],
                    nw2row[0:1, h * 512:(h + 1) * 512], True),
                  mark=f"nw2mm{h}",
                  waits=("nw1ev1",) if h == 0 else (f"nw2ev{h-1}",))
                E("ve", lambda e, h=h: e.tensor_scalar(
                    anw2_tile[:, h * 512:(h + 1) * 512], PS[:, 3072:3584],
                    cols_f[:, C_A:C_A + 1], None, ALU.mult),
                  mark=f"nw2ev{h}", waits=(f"nw2mm{h}", "abev"))

            if PHASE_LIMIT < 1:
                E("ve", fn=None, waits=("nw2ev7", "boacol"))
                return
            NG = NT_ALL // 8
            for g in range(NG):
                for j in range(8):
                    t = g * 8 + j
                    b = t % 4
                    w_ld = [f"sq{t-4}", f"cst{t-4}"] if t >= 4 else ["p0ld"]
                    E("sy" if t % 2 == 0 else "sc",
                      lambda e, t=t, b=b: e.dma_start(
                        out=xbuf[:, b * D:(b + 1) * D],
                        in_=x_d[t * 128:(t + 1) * 128, :]),
                      mark=f"xld{t}", waits=w_ld, dma=True, sem=f"syx{t%4}")
                    E("ve", lambda e, t=t, b=b: e.tensor_tensor(
                        sqscr[:, (b % 2) * D:(b % 2 + 1) * D],
                        xbuf[:, b * D:(b + 1) * D],
                        xbuf[:, b * D:(b + 1) * D], ALU.mult),
                      mark=f"sqm{t}",
                      waits=(f"xld{t}",) +
                      (de(f"sq{t-2}") if t >= 2 else ()))
                    E("ve", lambda e, t=t, b=b: e.reduce_sum(
                        stat_r[:, t:t + 1], xbuf[:, b * D:(b + 1) * D], AX.X),
                      mark=f"rd{t}", waits=(f"xld{t}",))
                    E("ve", lambda e, t=t, b=b: e.reduce_sum(
                        stat_s[:, t:t + 1],
                        sqscr[:, (b % 2) * D:(b % 2 + 1) * D], AX.X),
                      mark=f"sq{t}", waits=(f"sqm{t}",))
                    E("ve", lambda e, t=t, b=b: e.tensor_copy(
                        xbb[:, (t % 16) * D:(t % 16 + 1) * D],
                        xbuf[:, b * D:(b + 1) * D]),
                      mark=f"cst{t}",
                      waits=(f"rd{t}",) +
                      (("nw2mm7",) if t < 2 else ()) +
                      ((f"gram{t-16}", f"xnst{t-16}") if t >= 16 else ()))
                for i in range(4 * g, 4 * g + 4):
                    if i < 16:
                        wsrc = w1t_d[(i // 2) * 128:(i // 2 + 1) * 128,
                                     (i % 2) * 2048:(i % 2 + 1) * 2048]
                        wdst0 = wstage[:, (i % 2) * 2048:(i % 2 + 1) * 2048]
                    else:
                        wsrc = w2t_d.rearrange("(a p) d -> p a d", p=128)[
                            :, (i - 16) * 2:(i - 16) * 2 + 2, :]
                        wdst0 = wstage[:, (i % 2) * 2048:(i % 2 + 1) * 2048
                                       ].rearrange("p (a d) -> p a d", a=2)
                    E("gp", lambda e, w=wsrc, d0=wdst0: e.dma_start(
                        out=d0, in_=w),
                      mark=f"wld{i}",
                      waits=(f"wstat{i-2}",) if i >= 2 else (),
                      dma=True, sem=f"gpw{i%2}")
                    E("sc", lambda e, i=i: e.activation(
                        wdump[:, :],
                        wstage[:, (i % 2) * 2048:(i % 2 + 1) * 2048],
                        AF.Abs, accum_out=wstat[:, i:i + 1]),
                      mark=f"wstat{i}",
                      waits=(f"wld{i}",) +
                      (de(f"wstat{i-1}") if i >= 1 else ("nw2mm7",)))
                sl = slice(g * 8, g * 8 + 8)
                E("ve", lambda e, sl=sl: e.tensor_scalar(
                    mean_all[:, sl], stat_r[:, sl], 1.0 / D, None, ALU.mult),
                  mark=f"ga{g}", waits=(f"rd{g*8+7}",))
                E("ve", lambda e, sl=sl: e.tensor_scalar(
                    var_all[:, sl], stat_s[:, sl], 1.0 / D, None, ALU.mult),
                  mark=f"gb{g}", waits=(f"sq{g*8+7}",))
                E("ve", lambda e, sl=sl: e.tensor_tensor(
                    scr8[:, :], mean_all[:, sl], mean_all[:, sl], ALU.mult),
                  mark=f"gc{g}", waits=(f"ga{g}",))
                E("ve", lambda e, sl=sl: e.tensor_tensor(
                    var_all[:, sl], var_all[:, sl], scr8[:, :], ALU.subtract),
                  mark=f"gd{g}", waits=(f"gb{g}", f"gc{g}"))
                E("sc", lambda e, sl=sl: e.activation(
                    rstd_all[:, sl], var_all[:, sl], AF.Ln,
                    bias=cols_f[:, C_EPSLN:C_EPSLN + 1]),
                  mark=f"ge{g}", waits=(f"gd{g}",))
                E("sc", lambda e, sl=sl: e.activation(
                    rstd_all[:, sl], rstd_all[:, sl], AF.Exp, scale=-0.5),
                  mark=f"gf{g}", waits=de(f"ge{g}"))
                E("ve", lambda e, sl=sl: e.scalar_tensor_tensor(
                    negm_all[:, sl], mean_all[:, sl], -1.0, rstd_all[:, sl],
                    ALU.mult, ALU.mult),
                  mark=f"cmg{g}", waits=(f"gf{g}",))
                for j in range(8):
                    t = g * 8 + j
                    s16 = (t % 16) * D
                    E("ve", lambda e, t=t, s16=s16: e.tensor_scalar(
                        xbb[:, s16:s16 + D], xbb[:, s16:s16 + D],
                        rstd_all[:, t:t + 1], negm_all[:, t:t + 1],
                        ALU.mult, ALU.add),
                      mark=f"nx{t}", waits=(f"cst{t}", f"cmg{g}"))
                    for c in range(4):
                        for h in range(2):
                            E("te", lambda e, t=t, s16=s16, c=c, h=h: e.matmul(
                                PS[:, (2 * c + h) * 512:(2 * c + h + 1) * 512],
                                xbb[:, s16 + c * 128:s16 + (c + 1) * 128],
                                xbb[:, s16 + h * 512:s16 + (h + 1) * 512],
                                start=(t == 0), stop=(t == NT_ALL - 1)),
                              mark=f"gram{t}" if (c == 3 and h == 1) else None,
                              waits=((f"nx{t}",) +
                                     (("nw2ev7",) if t == 0 else ()))
                              if (c == 0 and h == 0) else ())
                    E("sy" if t % 2 == 0 else "sc",
                      lambda e, t=t, s16=s16: e.dma_start(
                        out=xnbf_d[t * 128:(t + 1) * 128, :],
                        in_=xbb[:, s16:s16 + D]),
                      mark=f"xnst{t}",
                      waits=(f"nx{t}",) +
                      ((f"xnst{t-2}",) if t >= 2 else ()),
                      dma=True, sem=f"gpn{t%2}")

            for c in range(4):
                E("ve", lambda e, c=c: e.tensor_copy(
                    g_bf[:, c * 1024:(c + 1) * 1024],
                    PS[:, c * 1024:(c + 1) * 1024]),
                  mark=f"gh1ev{c}", waits=(f"gram{NT_ALL-1}",) if c == 0 else ())

            E("ve", lambda e: e.reduce_sum(qcols[:, 0:1], wstat[:, 0:16], AX.X),
              waits=("wstat15",), mark="w1red")
            E("ve", lambda e: e.reduce_sum(qcols[:, 1:2], wstat[:, 16:32], AX.X),
              waits=("wstat31",), mark="w2red")
            E("te", lambda e: e.matmul(PS[0:1, 2048:2049], onesfc_sb[:, 0:1],
                                       qcols[:, 0:1], start=True, stop=True),
              waits=("w1red", "gh1ev3"), mark="w1s")
            E("te", lambda e: e.matmul(PS[0:1, 2049:2050], onesfc_sb[:, 0:1],
                                       qcols[:, 1:2], start=True, stop=True),
              waits=("w2red",), mark="w2s")
            E("ve", lambda e: e.tensor_copy(scal_f[0:1, 2:4], PS[0:1, 2048:2050]),
              waits=("w1s", "w2s"), mark="wsev")
            E("ve", lambda e: e.tensor_scalar(scal_f[0:1, 2:4], scal_f[0:1, 2:4],
                                              1.0 / (D * HID), 1e-5, ALU.mult,
                                              ALU.max), mark="wclip",
              waits=("wsev",))
            E("te", lambda e: e.matmul(PS[:, 2048:2049], onesf_sb[0:1, :],
                                       scal_f[0:1, 2:3], start=True, stop=True),
              waits=("wsev", "wclip"), mark="s1b")
            E("te", lambda e: e.matmul(PS[:, 2049:2050], onesf_sb[0:1, :],
                                       scal_f[0:1, 3:4], start=True, stop=True),
              mark="s2b")
            E("ve", lambda e: e.tensor_copy(cols_f[:, C_S1W:C_S2W + 1],
                                            PS[:, 2048:2050]),
              waits=("s1b", "s2b"), mark="swcols")
            E("ve", lambda e: e.tensor_scalar(cols_f[:, C_S2F:C_S2F + 1],
                                              cols_f[:, C_S2W:C_S2W + 1],
                                              0.5 / 127.0, None, ALU.mult),
              mark="s2fc", waits=("swcols",))
            E("ve", lambda e: e.tensor_scalar(cols_f[:, C_S1F:C_S1F + 1],
                                              cols_f[:, C_S1W:C_S1W + 1],
                                              0.5, None, ALU.mult),
              mark="s1fc")
            E("ve", lambda e: e.tensor_scalar(cols_f[:, C_TH1:C_TH1 + 1],
                                              cols_f[:, C_S1W:C_S1W + 1],
                                              0.5, None, ALU.mult),
              mark="th1c", waits=("swcols",))
            E("ve", lambda e: e.tensor_scalar(cols_f[:, C_TH2:C_TH2 + 1],
                                              cols_f[:, C_S2W:C_S2W + 1],
                                              0.5, None, ALU.mult),
              mark="th2c")
            E("ve", lambda e: e.tensor_scalar(cols_f[:, C_NTH1:C_NTH1 + 1],
                                              cols_f[:, C_TH1:C_TH1 + 1],
                                              -1.0, None, ALU.mult),
              waits=("th1c",))
            E("ve", lambda e: e.tensor_scalar(cols_f[:, C_NTH2:C_NTH2 + 1],
                                              cols_f[:, C_TH2:C_TH2 + 1],
                                              -1.0, None, ALU.mult),
              mark="thcols", waits=("th2c",))

            if PHASE_LIMIT < 2:
                E("ve", fn=None, waits=("thcols", f"gram{NT_ALL-1}",
                                        f"xnst{NT_ALL-2}", f"xnst{NT_ALL-1}",
                                        "wstat31"))
                return
            E("ve", lambda e: e.tensor_scalar(
                mean_all[:, 0:NT_OWN], var_all[:, 0:NT_OWN], 1e-5, None,
                ALU.add),
              mark="v2a", waits=("cmg1",))
            E("ve", lambda e: e.reciprocal(
                negm_all[:, 0:NT_OWN], mean_all[:, 0:NT_OWN]),
              mark="v2b", waits=("v2a", f"nx{NT_OWN-1}"))
            E("ve", lambda e: e.tensor_tensor(
                mean_all[:, 0:NT_OWN], var_all[:, 0:NT_OWN],
                negm_all[:, 0:NT_OWN], ALU.mult),
              mark="v2c", waits=("v2b",))
            E("sc", lambda e: e.activation(
                rsq16[:, :], mean_all[:, 0:NT_OWN], AF.Ln,
                bias=cols_f[:, C_EPSRMS:C_EPSRMS + 1]),
              mark="v2d", waits=("v2c",))
            E("sc", lambda e: e.activation(
                rsq16[:, :], rsq16[:, :], AF.Exp, scale=-0.5),
              mark="v2e", waits=de("v2d"))
            def tern_chunk(i):
                nth = C_NTH1 if i < 16 else C_NTH2
                pth = C_TH1 if i < 16 else C_TH2
                if i < 16:
                    wsrc = w1t_d[(i // 2) * 128:(i // 2 + 1) * 128,
                                 (i % 2) * 2048:(i % 2 + 1) * 2048]
                    wdst = w1q_d[(i // 2) * 128:(i // 2 + 1) * 128,
                                 (i % 2) * 2048:(i % 2 + 1) * 2048]
                    wsg0 = wstage[:, (i % 2) * 2048:(i % 2 + 1) * 2048]
                    wtr0 = wtrn[:, :]
                else:
                    wsrc = w2t_d.rearrange("(a p) d -> p a d", p=128)[
                        :, (i - 16) * 2:(i - 16) * 2 + 2, :]
                    wdst = w2q_d.rearrange("(a p) d -> p a d", p=128)[
                        :, (i - 16) * 2:(i - 16) * 2 + 2, :]
                    wsg0 = wstage[:, (i % 2) * 2048:(i % 2 + 1) * 2048
                                  ].rearrange("p (a d) -> p a d", a=2)
                    wtr0 = wtrn[:, :].rearrange("p (a d) -> p a d", a=2)
                E("sc" if i % 2 == 0 else "sy",
                  lambda e, w=wsrc, d0=wsg0: e.dma_start(out=d0, in_=w),
                  waits=(f"wtrn{i-2}",) if i >= 2 else ("thcols",),
                  mark=f"vwld{i}", dma=True, sem=f"syw{i%2}")
                E("sc", lambda e, i=i, nth=nth: e.sign(
                    wsA[:, :], wstage[:, (i % 2) * 2048:(i % 2 + 1) * 2048],
                    bias=cols_f[:, nth:nth + 1]),
                  mark=f"wsga{i}",
                  waits=(f"vwld{i}",) + ((f"wtrn{i-1}",) if i >= 1 else ()))
                E("sc", lambda e, i=i, pth=pth: e.sign(
                    wsB[:, :], wstage[:, (i % 2) * 2048:(i % 2 + 1) * 2048],
                    bias=cols_f[:, pth:pth + 1]),
                  mark=f"wsgn{i}")
                E("ve", lambda e: e.tensor_tensor(
                    wtrn[:, :], wsA[:, :], wsB[:, :], ALU.add),
                  waits=(f"wsga{i}", f"wsgn{i}") +
                  ((f"wtst{i-1}",) if i >= 1 else ()),
                  mark=f"wtrn{i}")
                E("gp", lambda e, w=wdst, s0=wtr0: e.dma_start(out=w, in_=s0),
                  waits=(f"wtrn{i}",), mark=f"wtst{i}", dma=True, sem="gpt")

            mseq = [0]

            def m_mms(src_ap, first, wmarks, last=False):
                for h in range(2):
                    E("te", lambda e, s=src_ap, h=h, f=first, l=last: e.matmul(
                        PS[0:1, 1024 + h * 512:1024 + (h + 1) * 512],
                        onesbc_sb[:, 0:1], s[:, h * 512:(h + 1) * 512],
                        start=f, stop=l),
                      mark=f"mmm{mseq[0]}" if h == 1 else None,
                      waits=wmarks if h == 0 else ())
                mseq[0] += 1

            for t in range(NT_OWN):
                b = t % 2
                w_ld = ([f"mmm{4*(t-2)}", f"t1m{t-2}"] if t >= 2
                        else [f"xnst{NT_ALL-2}", f"xnst{NT_ALL-1}",
                              f"gram{NT_ALL-1}"])
                E("sy", lambda e, t=t, b=b: e.dma_start(
                    out=xnb[:, b * D:(b + 1) * D],
                    in_=xnbf_d[t * 128:(t + 1) * 128, :]),
                  mark=f"jxld{t}", waits=w_ld, dma=True, sem=f"syx{t%2}")
                E("ve", lambda e, t=t, b=b: e.tensor_tensor(
                    t1b[:, :], xnb[:, b * D:(b + 1) * D], nw1b_tile[:, :],
                    ALU.mult),
                  waits=(f"jxld{t}",) +
                  (("nw1bf", f"gram{NT_ALL-1}") if t == 0
                   else (f"q0m{t-1}", f"em{t-1}")),
                  mark=f"t1m{t}")
                E("ve", lambda e: e.reduce_max(
                    qcols[:, 4:5], t1b[:, :], AX.X, apply_absolute_value=True),
                  mark=f"am1{t}", waits=(f"t1m{t}",))
                E("ve", lambda e, t=t: e.tensor_tensor(
                    qcols[:, 5:6], qcols[:, 4:5], rsq16[:, t:t + 1], ALU.mult),
                  mark=f"q5a{t}", waits=("v2e", f"am1{t}"))
                E("ve", lambda e: e.tensor_scalar(
                    qcols[:, 5:6], qcols[:, 5:6], 1e-5, None, ALU.max),
                  mark=f"q5b{t}", waits=(f"q5a{t}",))
                E("ve", lambda e, t=t: e.tensor_scalar(
                    rq1_cols[:, t:t + 1], qcols[:, 5:6], 1.0 / 127.0, None,
                    ALU.mult),
                  waits=(f"q5b{t}",))
                E("ve", lambda e: e.reciprocal(qcols[:, 6:7], qcols[:, 5:6]),
                  mark=f"rc1{t}")
                E("ve", lambda e: e.tensor_scalar(
                    qcols[:, 6:7], qcols[:, 6:7], 127.0, None, ALU.mult),
                  mark=f"q6b{t}", waits=(f"rc1{t}",))
                E("ve", lambda e, t=t: e.tensor_tensor(
                    c1_cols[:, t:t + 1], rsq16[:, t:t + 1], qcols[:, 6:7],
                    ALU.mult),
                  mark=f"c1m{t}", waits=(f"q6b{t}",))
                E("sc", lambda e, t=t: e.activation(
                    q0s[:, :], t1b[:, :], AF.Copy, scale=c1_cols[:, t:t + 1]),
                  waits=(f"c1m{t}",) +
                  ((f"xqm{t-1}",) if t >= 1 else de("wstat31")),
                  mark=f"q0m{t}")
                E("ve", lambda e, t=t: e.scalar_tensor_tensor(
                    e1s[:, :], t1b[:, :], c1_cols[:, t:t + 1], q0s[:, :],
                    ALU.mult, ALU.subtract),
                  waits=(f"q0m{t}", f"c1m{t}") +
                  ((f"sgn1{t-1}",) if t >= 1 else ()),
                  mark=f"em{t}")
                E("sc", lambda e: e.sign(sAs[:, :], e1s[:, :],
                                         bias=cols_f[:, C_NHALF:C_NHALF + 1]),
                  mark=f"sga{t}", waits=(f"em{t}",))
                E("sc", lambda e: e.sign(sBs[:, :], e1s[:, :],
                                         bias=cols_f[:, C_HALF:C_HALF + 1]),
                  mark=f"sgn1{t}")
                E("ve", lambda e: e.tensor_tensor(
                    adjs[:, :], sAs[:, :], sBs[:, :], ALU.add),
                  mark=f"adm{t}",
                  waits=(f"sga{t}", f"sgn1{t}") +
                  ((f"xqst{t-1}",) if t >= 1 else ()))
                E("ve", lambda e: e.scalar_tensor_tensor(
                    xq8[:, :], adjs[:, :], 0.5, q0s[:, :], ALU.mult, ALU.add),
                  mark=f"xqm{t}", waits=(f"adm{t}",))
                E("gp", lambda e, t=t: e.dma_start(
                    out=xq_d[t * 128:(t + 1) * 128, :], in_=xq8[:, :]),
                  waits=(f"xqm{t}",), mark=f"xqst{t}", dma=True, sem=f"gpxq{t%2}")
                for i in range(8):
                    E("te", lambda e, b=b, i=i: e.transpose(
                        PT[:, i * 128:(i + 1) * 128],
                        xnb[:, b * D + i * 128:b * D + (i + 1) * 128],
                        id_sb[:, :]),
                      waits=((f"jxld{t}",) + ((f"qTev{t-1}",) if t >= 1
                                              else ("gh1ev3",)))
                      if i == 0 else (),
                      mark=f"xnTmm{t}" if i == 7 else None)
                E("ve", lambda e: e.tensor_copy(xnT[:, :], PT[:, 0:1024]),
                  waits=(f"xnTmm{t}",), mark=f"xnTev{t}")
                for c in range(8):
                    for h in range(2):
                        E("te", lambda e, c=c, h=h: e.matmul(
                            PS[:, h * 512:(h + 1) * 512],
                            xnT[:, c * 128:(c + 1) * 128],
                            wq_sb[:, c * 1024 + h * 512:c * 1024 + (h + 1) * 512],
                            start=(c == 0), stop=(c == 7)),
                          waits=((f"xnTev{t}", "wqf") +
                                 ((f"qev{t-1}",) if t >= 1 else ()))
                          if (c == 0 and h == 0) else (),
                          mark=f"qmm{t}" if (c == 7 and h == 1) else None)
                E("ve", lambda e: e.tensor_tensor(
                    qtok[:, :], PS[:, 0:1024], bq_tile[:, :], ALU.add),
                  waits=(f"qmm{t}",), mark=f"qev{t}")
                for i in range(8):
                    E("te", lambda e, i=i: e.transpose(
                        PT[:, i * 128:(i + 1) * 128],
                        qtok[:, i * 128:(i + 1) * 128], id_sb[:, :]),
                      waits=(f"qev{t}", f"xnTev{t}") if i == 0 else (),
                      mark=f"qTmm{t}" if i == 7 else None)
                E("ve", lambda e: e.tensor_copy(qtile[:, :], PT[:, 0:1024]),
                  waits=(f"qTmm{t}",) + ((f"qtst{t-1}",) if t >= 1 else ()),
                  mark=f"qTev{t}")
                E("gp", lambda e, t=t: e.dma_start(
                    out=qt_d.rearrange("p (c u) -> p c u", c=8)[
                        :, :, t * 128:(t + 1) * 128],
                    in_=qtile[:, :].rearrange("p (c u) -> p c u", c=8)),
                  waits=(f"qTev{t}",), mark=f"qtst{t}", dma=True, sem=f"gpqt{t%2}")
                tern_chunk(t)
                m_mms(xnb[:, b * D:(b + 1) * D], first=(t == 0),
                      wmarks=(f"jxld{t}",))
                for j in range(3):
                    u = NT_OWN + t * 3 + j
                    fb = u % 2
                    E("sy", lambda e, u=u, fb=fb: e.dma_start(
                        out=fbuf[:, fb * D:(fb + 1) * D],
                        in_=xnbf_d[u * 128:(u + 1) * 128, :]),
                      waits=(f"mmm{mseq[0]-2}",) if mseq[0] >= 2 else (),
                      mark=f"fld{u}", dma=True, sem=f"syf{u%2}")
                    m_mms(fbuf[:, fb * D:(fb + 1) * D], first=False,
                          wmarks=(f"fld{u}",),
                          last=(t == NT_OWN - 1 and j == 2))

            E("ve", lambda e: e.tensor_copy(mrow_f[0:1, :], PS[0:1, 1024:2048]),
              waits=(f"mmm{mseq[0]-1}",), mark="mev")
            E("ve", lambda e: e.tensor_copy(mrow_b[0:1, :], mrow_f[0:1, :]),
              mark="mbf", waits=("mev",))
            E("gp", lambda e: e.dma_start(out=mrow_d[:, :], in_=mrow_b[0:1, :]),
              waits=("mbf",), mark="mst", dma=True, sem="gpm")
            for c in range(8):
                E("gp", lambda e, c=c: e.dma_start(
                    out=mcol_b[:, c:c + 1],
                    in_=mrow_d[0:1, c * 128:(c + 1) * 128]),
                  waits=("mst",) if c == 0 else (), mark=f"mcol{c}", dma=True,
                  sem="gpm")
            if debug:
                E("gp", lambda e: e.dma_start(out=dbg["dbg_m"][:, :],
                                              in_=mrow_f[0:1, :]),
                  waits=("mev",), mark="dbgm", dma=True, sem="gpd")

            if PHASE_LIMIT < 3:
                E("ve", fn=None, waits=("mbf", f"qtst{NT_OWN-1}",
                                        f"xqst{NT_OWN-1}", f"mcol7"))
                return
            for r in range(2):
                for i in range(8):
                    k = r * 8 + i
                    f = 4 + k // 4
                    cb = k % 4
                    E("te", lambda e, i=i, f=f, cb=cb: e.transpose(
                        PT[:, i * 128:(i + 1) * 128],
                        g_bf[:, cb * 1024 + f * 128:cb * 1024 + (f + 1) * 128],
                        id_sb[:, :]),
                      waits=(("gh1ev3", f"qTev{NT_OWN-1}") if (r == 0 and i == 0)
                             else (("symev0",) if (r == 1 and i == 0) else ())),
                      mark=f"symt{r}" if i == 7 else None)
                fo = 4 + r * 2
                E("ve", lambda e, fo=fo: e.tensor_copy(
                    g_bf[:, fo * 1024:fo * 1024 + 512], PT[:, 0:512]),
                  waits=(f"symt{r}",))
                E("ve", lambda e, fo=fo: e.tensor_copy(
                    g_bf[:, (fo + 1) * 1024:(fo + 1) * 1024 + 512],
                    PT[:, 512:1024]),
                  mark=f"symev{r}")
            for u in range(NT_ALL):
                fb = u % 4
                E("sc" if u % 2 == 0 else "sy",
                  lambda e, u=u, fb=fb: e.dma_start(
                    out=fbuf4[:, fb * 512:(fb + 1) * 512],
                    in_=xnbf_d[u * 128:(u + 1) * 128, 512:1024]),
                  waits=(f"g2{u-4}",) if u >= 4 else (f"mmm{mseq[0]-1}",),
                  mark=f"f2ld{u}", dma=True, sem=f"syf{u%4}")
                for c in range(4):
                    E("te", lambda e, u=u, fb=fb, c=c: e.matmul(
                        PS[:, c * 512:(c + 1) * 512],
                        fbuf4[:, fb * 512 + c * 128:fb * 512 + (c + 1) * 128],
                        fbuf4[:, fb * 512:(fb + 1) * 512],
                        start=(u == 0), stop=(u == NT_ALL - 1)),
                      waits=((f"f2ld{u}", "mev", f"qTev{NT_OWN-1}")
                             if (c == 0 and u == 0) else
                             ((f"f2ld{u}",) if c == 0 else ())),
                      mark=f"g2{u}" if c == 3 else None)
                if u % 4 == 0:
                    tern_chunk(16 + u // 4)
            for c in range(4, 8):
                E("ve", lambda e, c=c: e.tensor_copy(
                    g_bf[:, c * 1024 + 512:(c + 1) * 1024],
                    PS[:, (c - 4) * 512:(c - 3) * 512]),
                  mark=f"gh2ev{c}",
                  waits=(f"g2{NT_ALL-1}", "symev1") if c == 4 else ())
            if debug:
                E("gp", lambda e: e.dma_start(out=dbg["dbg_G"][:, :],
                                              in_=g_bf[:, :]),
                  waits=("gh2ev7",), mark="dbgg", dma=True, sem="gpd")

            if PHASE_LIMIT < 4:
                E("ve", fn=None, waits=("gh2ev7", "wtst31"))
                return
            for idx, (dst, src) in enumerate([
                    (bkbv_b[0:1, 0:D], bk_d[0:1, :]),
                    (bkbv_b[0:1, D:2 * D], bk_d[1:2, :]),
                    (bkbv_b[0:1, 2 * D:3 * D], bv_d[0:1, :]),
                    (bkbv_b[0:1, 3 * D:4 * D], bv_d[1:2, :])]):
                E("gp", lambda e, d=dst, s=src: e.dma_start(out=d, in_=s),
                  waits=("wtst31", f"xqst{NT_OWN-1}") if idx == 0 else (),
                  mark=f"bkbvld{idx}", dma=True, sem="gpl")
            for k in range(2):
                E("sy", lambda e, k=k: e.dma_start(out=bvrow[0:1, :],
                                                   in_=bv_d[k:k + 1, :]),
                  waits=("vwld31", f"qtst{NT_OWN-1}") if k == 0
                  else (f"wradd{k-1}",),
                  mark=f"bvrowld{k}", dma=True, sem="syp")
                wvv = wv_d.rearrange("t (c p) j -> t p c j", p=128)[k]
                wkk = wk_d.rearrange("t (c p) j -> t p c j", p=128)[k]
                E("gp", lambda e, w=wvv: e.dma_start(
                    out=wkv_sb[:, :].rearrange("p (c j) -> p c j", c=8), in_=w),
                  waits=(f"smm{k-1}_{NH-1}",) if k >= 1 else ("bkbvld3",),
                  mark=f"wvld{k}", dma=True, sem="gpl")
                for c in range(8):
                    E("ve", lambda e, c=c: e.tensor_tensor(
                        wkv_sb[:, c * 1024 + c * 128:c * 1024 + (c + 1) * 128],
                        wkv_sb[:, c * 1024 + c * 128:c * 1024 + (c + 1) * 128],
                        id_sb[:, :], ALU.add),
                      waits=(f"wvld{k}", "gh2ev7") if c == 0 else (),
                      mark=f"wvf{k}" if c == 7 else None)
                for c in range(8):
                    for h in range(2):
                        E("te", lambda e, c=c, h=h: e.matmul(
                            PS[0:1, 2560 + h * 512:2560 + (h + 1) * 512],
                            mcol_b[:, c:c + 1],
                            wkv_sb[:, c * 1024 + h * 512:c * 1024 + (h + 1) * 512],
                            start=(c == 0), stop=(c == 7)),
                          waits=(f"wvf{k}", "mcol7", "swcols")
                          if (c == 0 and h == 0) else (),
                          mark=f"wrmm{k}" if (c == 7 and h == 1) else None)
                E("ve", lambda e, k=k: e.tensor_copy(wrow_f[0:1, :],
                                                PS[0:1, 2560:3584]),
                  waits=(f"wrmm{k}", f"bvrowld{k}"), mark=f"wrev{k}")
                E("ve", lambda e, k=k: e.scalar_tensor_tensor(
                    wrow_f[0:1, :], bvrow[0:1, :],
                    float(NTOK), wrow_f[0:1, :], ALU.mult, ALU.add),
                  mark=f"wradd{k}", waits=(f"wrev{k}",))
                E("ve", lambda e: e.tensor_copy(wrow_b[0:1, :], wrow_f[0:1, :]),
                  mark=f"wrbf{k}", waits=(f"wradd{k}",))
                for c in range(8):
                    pp = (c % 2) * 1024
                    prevt = (f"tev{k}_{c-2}",) if c >= 2 else \
                        ((f"tev{k-1}_{c+6}",) if k >= 1 else ())
                    for h in range(2):
                        E("te", lambda e, c=c, h=h, pp=pp, k=k: e.matmul(
                            PS[:, pp + h * 512:pp + (h + 1) * 512],
                            mrow_b[0:1, c * 128:(c + 1) * 128],
                            bkbv_b[0:1, (2 + k) * D + h * 512:
                                   (2 + k) * D + (h + 1) * 512],
                            start=True, stop=False),
                          waits=((f"wvf{k}", "mbf") + prevt) if h == 0 else ())
                    for f in range(8):
                        for h in range(2):
                            E("te", lambda e, c=c, f=f, h=h, pp=pp: e.matmul(
                                PS[:, pp + h * 512:pp + (h + 1) * 512],
                                g_bf[:, f * 1024 + c * 128:f * 1024 + (c + 1) * 128],
                                wkv_sb[:, f * 1024 + h * 512:
                                       f * 1024 + (h + 1) * 512],
                                start=False, stop=(f == 7)),
                              mark=f"tmm{k}_{c}" if (f == 7 and h == 1) else None)
                    E("ve", lambda e, c=c, pp=pp, k=k: e.tensor_copy(
                        t_bf[:, c * 1024:(c + 1) * 1024], PS[:, pp:pp + 1024]),
                      waits=(f"tmm{k}_{c}",), mark=f"tev{k}_{c}")
                E("gp", lambda e, w=wkk: e.dma_start(
                    out=wkv_sb[:, :].rearrange("p (c j) -> p c j", c=8), in_=w),
                  waits=(f"tmm{k}_7", f"wrmm{k}"),
                  mark=f"wkld{k}", dma=True, sem="gpl")
                for c in range(8):
                    E("ve", lambda e, c=c: e.tensor_tensor(
                        wkv_sb[:, c * 1024 + c * 128:c * 1024 + (c + 1) * 128],
                        wkv_sb[:, c * 1024 + c * 128:c * 1024 + (c + 1) * 128],
                        id_sb[:, :], ALU.add),
                      waits=(f"wkld{k}", f"tev{k}_7") if c == 0 else (),
                      mark=f"wkf{k}" if c == 7 else None)
                for h in range(NH):
                    E("te", lambda e, h=h, k=k: e.matmul(
                        PS[0:64, 2048 + h * 64:2048 + (h + 1) * 64],
                        bkbv_b[0:1, k * D + h * 64:k * D + (h + 1) * 64],
                        wrow_b[0:1, h * 64:(h + 1) * 64],
                        start=True, stop=False),
                      waits=(f"wrbf{k}", f"wkf{k}") if h == 0 else ())
                    for cc in range(8):
                        E("te", lambda e, h=h, cc=cc: e.matmul(
                            PS[0:64, 2048 + h * 64:2048 + (h + 1) * 64],
                            wkv_sb[:, cc * 1024 + h * 64:cc * 1024 + h * 64 + 64],
                            t_bf[:, cc * 1024 + h * 64:cc * 1024 + h * 64 + 64],
                            start=False, stop=(cc == 7)),
                          mark=f"smm{k}_{h}" if cc == 7 else None)
                E("ve", lambda e, k=k: e.tensor_copy(
                    states_sb[0:64, k * D:(k + 1) * D], PS[0:64, 2048:3072]),
                  waits=tuple(f"smm{k}_{h}" for h in range(NH)) +
                  (("wrev0",) if k == 0 else ()),
                  mark=f"stev{k}")
            E("ve", lambda e: e.tensor_tensor(
                states_sb[0:64, 0:D], states_sb[0:64, 0:D],
                states_sb[0:64, D:2 * D], ALU.mult),
              waits=("stev0", "stev1"), mark="stp1")
            E("ve", lambda e: e.tensor_scalar(
                statet_b[0:64, :], states_sb[0:64, 0:D], ISCALE2, None, ALU.mult),
              mark="stp", waits=("stp1",))
            E("ve", lambda e: e.memset(statet2[:, :], 0.0), mark="st2z")
            E("ve", lambda e: e.tensor_copy(
                statet2[0:64, 0:512].rearrange("p (j u) -> p j u", j=8),
                statet_b[0:64, :].rearrange("p (j x u) -> p j x u",
                                            j=8, x=2)[:, :, 0, :]),
              waits=("stp", "st2z"), mark="st2e")
            for j in range(8):
                E("te", lambda e, j=j: e.matmul(
                    PS[64:128, 2048 + j * 64:2048 + (j + 1) * 64],
                    id_sb[0:64, 0:64],
                    statet_b[0:64, j * 128 + 64:j * 128 + 128],
                    start=True, stop=True),
                  waits=("stp",) if j == 0 else (),
                  mark="st2mm" if j == 7 else None)
            E("ve", lambda e: e.tensor_copy(statet2[64:128, 512:1024],
                                            PS[64:128, 2048:2560]),
              waits=("st2mm",), mark="st2o")
            if debug:
                E("gp", lambda e: e.dma_start(out=dbg["dbg_stp"][:, :],
                                              in_=states_sb[0:64, 0:D]),
                  waits=("stp1",), mark="dbgstp", dma=True, sem="gpd")

            if PHASE_LIMIT < 5:
                E("ve", fn=None, waits=("st2o", "st2e"))
                return
            for c in range(8):
                E("gp", lambda e, c=c: e.dma_start(
                    out=w1q_sb[:, c * HID:(c + 1) * HID],
                    in_=w1q_d[c * 128:(c + 1) * 128, :]),
                  waits=(("stev1",) + (("dbgstp",) if debug else ()))
                  if c == 0 else (),
                  mark=f"w1qld{c}", dma=True, sem="gpl")
            E("ve", lambda e: e.tensor_scalar(
                scol_cols[:, :], rq1_cols[:, :],
                cols_f[:, C_S1F:C_S1F + 1], None, ALU.mult),
              mark="scolc", waits=("s1fc", f"c1m{NT_OWN-1}"))

            def b1Aload(t):
                b = t % 2
                E("gp", lambda e, t=t, b=b: e.dma_start(
                    out=xqb2[:, b * D:(b + 1) * D],
                    in_=xq_d[t * 128:(t + 1) * 128, :]),
                  waits=(f"xqst{t}",) + ((f"xqTmm{t-2}",) if t >= 2
                                         else ("stp",)),
                  mark=f"xqld{t}", dma=True, sem=f"gpb{t%2}")
                for i in range(8):
                    E("te", lambda e, b=b, i=i: e.transpose(
                        PT[:, i * 128:(i + 1) * 128],
                        xqb2[:, b * D + i * 128:b * D + (i + 1) * 128],
                        id_sb[:, :]),
                      waits=((f"xqld{t}",) +
                             ((f"xqTev{t-1}",) if t >= 1 else ("st2o",)) +
                             ((f"mm1_{t-2}_2",) if t >= 2 else ()))
                      if i == 0 else (),
                      mark=f"xqTmm{t}" if i == 7 else None)
                E("ve", lambda e, b=b: e.tensor_copy(
                    xqT_sb[:, b * D:(b + 1) * D], PT[:, 0:1024]),
                  waits=(f"xqTmm{t}",), mark=f"xqTev{t}")

            def b1Amm(t):
                b = t % 2
                groups = [(0, 3, 0), (3, 3, 1536), (6, 2, 0)]
                for g, (nq0, nn, off) in enumerate(groups):
                    for c in range(8):
                        for j in range(nn):
                            nq = nq0 + j
                            wl = []
                            if c == 0 and j == 0:
                                wl = [f"xqTev{t}", "w1qld7"]
                                if g == 2:
                                    wl.append(f"rev{t}_0")
                                elif t >= 1:
                                    wl.append(f"rev{t-1}_2" if g == 0
                                              else f"rev{t-1}_{g}")
                            E("te", lambda e, c=c, nq=nq, off=off, j=j,
                              b=b: e.matmul(
                                PS[:, off + j * 512:off + (j + 1) * 512],
                                xqT_sb[:, b * D + c * 128:
                                       b * D + (c + 1) * 128],
                                w1q_sb[:, c * HID + nq * 512:
                                       c * HID + (nq + 1) * 512],
                                start=(c == 0), stop=(c == 7)),
                              waits=wl,
                              mark=f"mm1_{t}_{g}" if (c == 7 and j == nn - 1)
                              else None)

            def b1S(t):
                b = t % 2
                gws = [(0, 1536, 0), (3, 1536, 1536), (6, 1024, 0)]
                for g, (nq0, gw, off) in enumerate(gws):
                    E("sc", lambda e, g=g, off=off, gw=gw, nq0=nq0,
                      t=t: e.activation(
                        r_sb[:, nq0 * 512:nq0 * 512 + gw], PS[:, off:off + gw],
                        AF.Relu, scale=scol_cols[:, t:t + 1]),
                      waits=(f"mm1_{t}_{g}",) +
                      (("scolc",) if (t == 0 and g == 0) else ()) +
                      ((f"u0m{t-1}_1",) if (t >= 1 and g == 0) else ()),
                      mark=f"rev{t}_{g}")
                if debug:
                    E("gp", lambda e, t=t: e.dma_start(
                        out=dbg["dbg_r"][t * 128:(t + 1) * 128, :],
                        in_=r_sb[:, 0:HID]),
                      waits=(f"rev{t}_2",), mark=f"dbgr{t}", dma=True,
                      sem="gdr")
                E("sc", lambda e: e.activation(
                    u0_sb[:, :], r_sb[:, :], AF.Square,
                    accum_out=qcols[:, 0:1]),
                  waits=(f"rev{t}_2",) +
                  ((f"u0p{t-1}", f"t2m{t-1}") if t >= 1 else ()),
                  mark=f"sqa{t}_0")
                E("sc", lambda e: e.activation(
                    r_sb[:, :], u0_sb[:, :], AF.Square,
                    accum_out=qcols[:, 2:3]),
                  mark=f"u0m{t}_1",
                  waits=de(f"sqa{t}_0") +
                  ((f"dbgr{t}",) if debug else ()))
            def b1V(t):
                b = t % 2
                E("ve", lambda e: e.tensor_tensor(
                    qcols[:, 2:3], qcols[:, 2:3], cols_f[:, C_A2:C_A2 + 1],
                    ALU.mult),
                  mark=f"k2{t}", waits=(f"u0m{t}_1",) +
                  (("a2c", "ebc") if t == 0 else ()))
                E("ve", lambda e: e.tensor_scalar(
                    u0_sb[:, :], u0_sb[:, :], cols_f[:, C_BOA:C_BOA + 1],
                    None, ALU.add),
                  mark=f"u0p{t}")
                E("ve", lambda e: e.scalar_tensor_tensor(
                    qcols[:, 0:1], qcols[:, 0:1], cols_f[:, C_AB2:C_AB2 + 1],
                    qcols[:, 2:3], ALU.mult, ALU.add),
                  mark=f"msm{t}", waits=de(f"k2{t}"))
                E("sc", lambda e: e.activation(
                    qcols[:, 4:5], qcols[:, 0:1], AF.Ln, scale=1.0 / HID,
                    bias=cols_f[:, C_EB:C_EB + 1]),
                  mark=f"ln2q{t}", waits=(f"msm{t}",))
                E("sc", lambda e: e.activation(
                    qcols[:, 4:5], qcols[:, 4:5], AF.Exp, scale=-0.5),
                  mark=f"rs2{t}", waits=de(f"ln2q{t}"))
                E("ve", lambda e: e.tensor_tensor(
                    t2f_sb[:, :], u0_sb[:, :], anw2_tile[:, :], ALU.mult),
                  waits=de(f"u0p{t}") + ((f"q0bm{t-1}",) if t >= 1 else ()),
                  mark=f"t2m{t}")
                E("ve", lambda e: e.memset(scr8[:, 0:1], 0.0))
                E("ve", lambda e: e.reduce_max(
                    qcols[:, 5:6], t2f_sb[:, :], AX.X, apply_absolute_value=True),
                  waits=de(f"t2m{t}"), mark=f"am2{t}")
                E("ve", lambda e: e.memset(scr8[:, 1:2], 0.0))
                E("ve", lambda e: e.tensor_scalar(
                    qcols[:, 5:6], qcols[:, 5:6], qcols[:, 4:5], 1e-5,
                    ALU.mult, ALU.max),
                  mark=f"yb{t}", waits=(f"rs2{t}",) + de(f"am2{t}"))
                E("ve", lambda e: e.reciprocal(qcols[:, 6:7], qcols[:, 5:6]),
                  mark=f"rc2{t}", waits=(f"yb{t}",))
                E("ve", lambda e, t=t: e.tensor_tensor(
                    ysc_cols[:, t:t + 1], qcols[:, 5:6],
                    cols_f[:, C_S2F:C_S2F + 1], ALU.mult),
                  mark=f"ysc{t}", waits=de(f"yb{t}") +
                  (("s2fc",) if t == 0 else ()))
                E("ve", lambda e: e.scalar_tensor_tensor(
                    qcols[:, 6:7], qcols[:, 6:7], 127.0, qcols[:, 4:5],
                    ALU.mult, ALU.mult),
                  mark=f"c2m{t}", waits=de(f"rc2{t}"))
                E("ve", lambda e: e.tensor_scalar(
                    q0b_sb[:, :], t2f_sb[:, :], qcols[:, 6:7], None, ALU.mult),
                  waits=(f"c2m{t}",) + ((f"x2m{t-1}_3",) if t >= 1 else ()),
                  mark=f"q0bm{t}")
            def b1Q(t):
                b = t % 2
                E("ve", lambda e: e.memset(scr8[:, 4:5], 0.0))
                for hh in range(2):
                    E("ve", lambda e, hh=hh: e.scalar_tensor_tensor(
                        e2h_sb[:, :], t2f_sb[:, hh * 2048:(hh + 1) * 2048],
                        qcols[:, 6:7], q0b_sb[:, hh * 2048:(hh + 1) * 2048],
                        ALU.mult, ALU.subtract),
                      waits=de(f"q0bm{t}") +
                      ((f"sg2a{t}_1", f"sg2{t}_1") if hh >= 1 else ()),
                      mark=f"e2m{t}_{hh}")
                    for qq in range(2):
                        qi = hh * 2 + qq
                        p = qi % 2
                        sgb = sB2_sb if p == 0 else sB2b_sb
                        prevq = ((f"x2m{t}_{qi-2}",) if qi >= 2 else
                                 ((f"x2m{t-1}_{qi+2}",) if t >= 1 else ()))
                        E("sc", lambda e, qq=qq, p=p: e.sign(
                            sA2_sb[:, p * D:(p + 1) * D],
                            e2h_sb[:, qq * 1024:(qq + 1) * 1024],
                            bias=cols_f[:, C_NHALF:C_NHALF + 1]),
                          mark=f"sg2a{t}_{qi}",
                          waits=(f"e2m{t}_{hh}",) + prevq)
                        E("sc", lambda e, qq=qq, sgb=sgb: e.sign(
                            sgb[:, :], e2h_sb[:, qq * 1024:(qq + 1) * 1024],
                            bias=cols_f[:, C_HALF:C_HALF + 1]),
                          mark=f"sg2{t}_{qi}")
                        E("ve", lambda e, sgb=sgb, p=p: e.tensor_tensor(
                            sgb[:, :], sA2_sb[:, p * D:(p + 1) * D],
                            sgb[:, :], ALU.add),
                          mark=f"xma{t}_{qi}",
                          waits=(f"sg2a{t}_{qi}", f"sg2{t}_{qi}") +
                          ((f"x2st{t-1}",) if (qi == 0 and t >= 1) else ()))
                        E("ve", lambda e, qi=qi, sgb=sgb: e.scalar_tensor_tensor(
                            x2b_sb[:, qi * 1024:(qi + 1) * 1024], sgb[:, :],
                            0.5, q0b_sb[:, qi * 1024:(qi + 1) * 1024],
                            ALU.mult, ALU.add),
                          mark=f"x2m{t}_{qi}",
                          waits=(f"xma{t}_{qi}",))
                E("gp", lambda e, t=t: e.dma_start(
                    out=x2_d[t * 128:(t + 1) * 128, :], in_=x2b_sb[:, :]),
                  waits=(f"x2m{t}_3",), mark=f"x2st{t}", dma=True, sem=f"gpx2{t%2}")

            b1Aload(0)
            b1Aload(1)
            b1Amm(0)
            b1S(0)
            b1V(0)
            for t in range(NT_OWN):
                if t + 2 < NT_OWN:
                    b1Aload(t + 2)
                if t + 1 < NT_OWN:
                    b1Amm(t + 1)
                    b1S(t + 1)
                b1Q(t)
                if t + 1 < NT_OWN:
                    b1V(t + 1)

            if PHASE_LIMIT < 6:
                E("ve", fn=None, waits=(f"x2st{NT_OWN-1}",))
                return
            for c in range(4):
                E("gp", lambda e, c=c: e.dma_start(
                    out=w2q_sb[:, c * 8 * D:(c + 1) * 8 * D].rearrange(
                        "p (a j) -> p a j", a=8),
                    in_=w2q_d.rearrange("(a p) j -> p a j", p=128)[
                        :, c * 8:(c + 1) * 8, :]),
                  waits=(f"x2st{NT_OWN-1}",) if c == 0
                  else (f"w2qld{c-1}",),
                  mark=f"w2qld{c}", dma=True, sem="gpl")
            E("gp", lambda e: e.dma_start(
                out=wo_sb[:, :].rearrange("p (c j) -> p c j", c=8),
                in_=wo_d.rearrange("(c p) j -> p c j", p=128)),
              mark="wold", waits=("w2qld3",), dma=True, sem="gpl")
            def b2A(t):
                b = t % 2
                s8 = b * HID
                E("gp", lambda e, t=t, s8=s8: e.dma_start(
                    out=x2bf2[:, s8:s8 + HID],
                    in_=x2_d[t * 128:(t + 1) * 128, :]),
                  waits=(f"x2st{t}",) +
                  ((f"x2Tmm{t-2}_1",) if t >= 2 else
                   (f"e2m{NT_OWN-1}_1", f"t2m{NT_OWN-1}")),
                  mark=f"x2ld{t}", dma=True, sem=f"gpx2{t%2}")
                E("sy", lambda e, t=t, b=b: e.dma_start(
                    out=qtb[:, b * D:(b + 1) * D].rearrange(
                        "p (c u) -> p c u", c=8),
                    in_=qt_d.rearrange("p (c u) -> p c u", c=8)[
                        :, :, t * 128:(t + 1) * 128]),
                  waits=((f"omm{t-2}",) if (t >= 2 and B2SUB >= 2) else ()) +
                  ((f"t2m{NT_OWN-1}",) if t < 2 else ()),
                  mark=f"qtld{t}", dma=True, sem=f"syq{t%2}")
                E("sy", lambda e, t=t, b=b: e.dma_start(
                    out=xbuf[:, b * D:(b + 1) * D],
                    in_=x_d[t * 128:(t + 1) * 128, :]),
                  waits=(f"resm{t-2}",) if t >= 2 else (),
                  mark=f"bxld{t}", dma=True, sem=f"syx{t%2}")
                for bt in range(2):
                    for i in range(16):
                        w0 = ()
                        if i == 0:
                            w0 = (f"x2ld{t}",)
                            if bt == 1:
                                w0 += (f"x2Tev{t}_0",)
                            elif t >= 1:
                                w0 += (f"x2Tev{t-1}_1", f"mm2_{t-1}")
                            else:
                                w0 += ("st2o",)
                        E("te", lambda e, bt=bt, i=i, s8=s8: e.transpose(
                            PT2[:, i * 128:(i + 1) * 128],
                            x2bf2[:, s8 + bt * 2048 + i * 128:
                                  s8 + bt * 2048 + (i + 1) * 128],
                            id_sb[:, :]),
                          waits=w0,
                          mark=f"x2Tmm{t}_{bt}" if i == 15 else None)
                    E("ve", lambda e, bt=bt: e.tensor_copy(
                        x2T_sb[:, bt * 2048:(bt + 1) * 2048], PT2[:, 0:2048]),
                      waits=(f"x2Tmm{t}_{bt}",) +
                      ((f"x2m{NT_OWN-1}_3", f"t2m{NT_OWN-1}")
                       if (t == 0 and bt == 0) else ()),
                      mark=f"x2Tev{t}_{bt}")
            def b2Amm(t):
                b = t % 2
                for cc in range(32):
                    wl = []
                    if t == 0 and cc % 8 == 0:
                        wl.append(f"w2qld{cc//8}")
                    if cc == 0:
                        wl += [f"x2Tev{t}_1"]
                        wl.append(f"yev{t-1}" if t >= 1 else f"rev{NT_OWN-1}_2")
                    E("te", lambda e, cc=cc: e.matmul(
                        PS[:, 0:512],
                        x2T_sb[:, cc * 128:(cc + 1) * 128],
                        w2q_sb[:, cc * 1024:cc * 1024 + 512],
                        start=(cc == 0), stop=(cc == 31)),
                      waits=wl)
                    E("te", lambda e, cc=cc: e.matmul(
                        PS[:, 512:1024],
                        x2T_sb[:, cc * 128:(cc + 1) * 128],
                        w2q_sb[:, cc * 1024 + 512:(cc + 1) * 1024],
                        start=(cc == 0), stop=(cc == 31)),
                      mark=f"mm2_{t}" if cc == 31 else None)

            def b2Batt(t):
                b = t % 2
                for h in range(NH if B2SUB >= 2 else 0):
                    wl = []
                    if h == 0:
                        wl = [f"qtld{t}", "st2o"]
                        if t >= 1:
                            wl.append(f"onTev{t-1}")
                    E("te", lambda e, h=h, b=b: e.matmul(
                        PS[:, 1024 + h * 64:1024 + (h + 1) * 64],
                        qtb[0:128,
                            b * D + (h // 2) * 128:b * D + (h // 2) * 128 + 128],
                        statet2[0:128,
                                (h % 2) * 512 + (h // 2) * 64:
                                (h % 2) * 512 + (h // 2) * 64 + 64],
                        start=True, stop=True),
                      waits=wl, mark=f"omm{t}" if h == NH - 1 else None)

            def b2B(t):
                b = t % 2
                E("sc", lambda e, t=t: e.activation(
                    y_sb[:, :], PS[:, 0:1024], AF.Copy,
                    scale=ysc_cols[:, t:t + 1]),
                  waits=(f"mm2_{t}", f"ysc{t}") +
                  ((f"gm{t-1}",) if t >= 1
                   else (f"t2m{NT_OWN-1}", f"sg2{NT_OWN-1}_3")) +
                  ((f"dbgy{t-1}",) if (debug and t >= 1) else ()),
                  mark=f"yev{t}")
                E("sc", lambda e: e.activation(
                    th_sb[:, :], y_sb[:, :], AF.Tanh, scale=0.5),
                  mark=f"thm{t}", waits=de(f"yev{t}"))
                E("ve", lambda e: e.tensor_scalar(
                    gh_sb[:, :], th_sb[:, :], 0.5, 0.5, ALU.mult, ALU.add),
                  mark=f"ghm{t}", waits=(f"thm{t}",))
                E("ve", lambda e: e.memset(scr8[:, 5:6], 0.0))
                E("ve", lambda e: e.tensor_tensor(
                    g_t[:, :], gh_sb[:, :], y_sb[:, :], ALU.mult),
                  mark=f"gm{t}",
                  waits=de(f"ghm{t}") +
                  ((f"dbggt{t-1}",) if (debug and t >= 1) else ()))
                if B2SUB < 2:
                    E("ve", lambda e, t=t, b=b: e.tensor_tensor(
                        res_f[:, :], g_t[:, :], xbuf[:, b * D:(b + 1) * D],
                        ALU.add),
                      waits=(f"bxld{t}", f"gm{t}") +
                      ((f"outst{t-1}",) if t >= 1 else ()),
                      mark=f"resm{t}")
                    E("sy", lambda e, t=t: e.dma_start(
                        out=out_d[t * 128:(t + 1) * 128, :], in_=res_f[:, :]),
                      waits=(f"resm{t}",), mark=f"outst{t}", dma=True, sem="syo")
                    return
                E("ve", lambda e: e.tensor_copy(
                    outa_sb[:, :], PS[:, 1024:2048]),
                  waits=(f"omm{t}",) +
                  ((f"dbgoa{t-1}",) if (debug and t >= 1)
                   else ((f"x2m{NT_OWN-1}_3",) if t == 0 else ())),
                  mark=f"oev{t}")
                E("ve", lambda e: e.reduce_sum(
                    lncols[:, 9:10], PS[:, 1024:2048], AX.X),
                  mark=f"orow{t}")
                E("ve", lambda e: e.tensor_tensor(
                    sqscr[:, 0:D], outa_sb[:, :], outa_sb[:, :], ALU.mult),
                  mark=f"osqm{t}", waits=(f"oev{t}",))
                E("ve", lambda e: e.tensor_scalar(
                    lncols[:, 4:5], lncols[:, 9:10], 1.0 / D, None, ALU.mult),
                  mark=f"oca{t}", waits=(f"orow{t}",))
                E("ve", lambda e: e.reduce_sum(
                    lncols[:, 10:11], sqscr[:, 0:D], AX.X),
                  mark=f"osq{t}", waits=(f"osqm{t}",))
                E("ve", lambda e: e.scalar_tensor_tensor(
                    lncols[:, 6:7], lncols[:, 4:5], 1.0, lncols[:, 4:5],
                    ALU.bypass, ALU.mult),
                  mark=f"occ{t}", waits=(f"oca{t}",))
                E("ve", lambda e: e.tensor_scalar(
                    lncols[:, 5:6], lncols[:, 10:11], 1.0 / D, None, ALU.mult),
                  mark=f"ocb{t}", waits=(f"osq{t}",))
                E("ve", lambda e: e.tensor_tensor(
                    lncols[:, 6:7], lncols[:, 5:6], lncols[:, 6:7], ALU.subtract),
                  mark=f"ocm{t}", waits=(f"ocb{t}", f"occ{t}"))
                E("sc", lambda e: e.activation(
                    lncols[:, 7:8], lncols[:, 6:7], AF.Ln,
                    bias=cols_f[:, C_EPSLN:C_EPSLN + 1]),
                  mark=f"oln{t}", waits=(f"ocm{t}",))
                E("sc", lambda e: e.activation(
                    lncols[:, 7:8], lncols[:, 7:8], AF.Exp, scale=-0.5),
                  mark=f"orstd{t}", waits=de(f"oln{t}"))
                E("ve", lambda e: e.scalar_tensor_tensor(
                    lncols[:, 8:9], lncols[:, 4:5], -1.0, lncols[:, 7:8],
                    ALU.mult, ALU.mult),
                  waits=(f"orstd{t}",), mark=f"onmr{t}")
                E("ve", lambda e: e.tensor_scalar(
                    outn_sb[:, :], outa_sb[:, :],
                    lncols[:, 7:8], lncols[:, 8:9], ALU.mult, ALU.add),
                  waits=(f"onmr{t}",) + ((f"onTmm{t-1}",) if t >= 1 else ()),
                  mark=f"onorm{t}")
                E("ve", lambda e: e.tensor_tensor(
                    outn_sb[:, :], outn_sb[:, :], lng_tile[:, :], ALU.mult),
                  mark=f"og1{t}", waits=(f"onorm{t}",))
                E("ve", lambda e: e.tensor_tensor(
                    outn_sb[:, :], outn_sb[:, :], lnb_tile[:, :], ALU.add),
                  mark=f"ognb{t}", waits=(f"og1{t}",))
                for i in range(8):
                    w0 = ()
                    if i == 0:
                        w0 = (f"ognb{t}", f"oev{t}")
                        if t >= 1:
                            w0 += (f"onTev{t-1}",)
                    E("te", lambda e, i=i: e.transpose(
                        PTo[:, i * 128:(i + 1) * 128],
                        outn_sb[:, i * 128:(i + 1) * 128], id_sb[:, :]),
                      waits=w0, mark=f"onTmm{t}" if i == 7 else None)
                E("ve", lambda e: e.tensor_copy(outnT_sb[:, :], PTo[:, 0:1024]),
                  waits=(f"onTmm{t}",), mark=f"onTev{t}")
                for c in range(8):
                    wl = []
                    if c == 0:
                        wl = [f"onTev{t}", "wold"]
                        if t >= 1:
                            wl.append(f"sev{t-1}")
                    E("te", lambda e, c=c: e.matmul(
                        PS[:, 2048:2560],
                        outnT_sb[:, c * 128:(c + 1) * 128],
                        wo_sb[:, c * 1024:c * 1024 + 512],
                        start=(c == 0), stop=(c == 7)),
                      waits=wl)
                    E("te", lambda e, c=c: e.matmul(
                        PS[:, 2560:3072],
                        outnT_sb[:, c * 128:(c + 1) * 128],
                        wo_sb[:, c * 1024 + 512:(c + 1) * 1024],
                        start=(c == 0), stop=(c == 7)),
                      mark=f"wmm{t}" if c == 7 else None)
                E("ve", lambda e: e.tensor_tensor(
                    s_sb[:, :], PS[:, 2048:3072], bo_tile[:, :], ALU.add),
                  waits=(f"wmm{t}",) +
                  ((f"x2st{NT_OWN-1}",) if t == 0 else ()),
                  mark=f"sev{t}")
                E("ve", lambda e: e.memset(scr8[:, 6:7], 0.0))
                E("ve", lambda e: e.tensor_tensor(
                    sg_sb[:, :], s_sb[:, :], g_t[:, :], ALU.mult),
                  mark=f"sgm{t}", waits=(f"gm{t}",) + de(f"sev{t}"))
                E("ve", lambda e: e.memset(scr8[:, 7:8], 0.0))
                E("ve", lambda e, t=t, b=b: e.tensor_tensor(
                    res_f[:, :], sg_sb[:, :], xbuf[:, b * D:(b + 1) * D],
                    ALU.add),
                  waits=de(f"sgm{t}") + (f"bxld{t}",) +
                  ((f"outst{t-1}",) if t >= 1 else ()),
                  mark=f"resm{t}")
                E("sy", lambda e, t=t: e.dma_start(
                    out=out_d[t * 128:(t + 1) * 128, :], in_=res_f[:, :]),
                  waits=(f"resm{t}",), mark=f"outst{t}", dma=True, sem="syo")
                if debug:
                    E("gp", lambda e, t=t: e.dma_start(
                        out=dbg["dbg_y"][t * 128:(t + 1) * 128, :],
                        in_=y_sb[:, :]),
                      waits=(f"gm{t}",), mark=f"dbgy{t}", dma=True, sem="gdy")
                    E("gp", lambda e, t=t: e.dma_start(
                        out=dbg["dbg_oa"][t * 128:(t + 1) * 128, :],
                        in_=outa_sb[:, :]),
                      waits=(f"osq{t}",), mark=f"dbgoa{t}", dma=True, sem="gdo")
                    E("gp", lambda e, t=t: e.dma_start(
                        out=dbg["dbg_g"][t * 128:(t + 1) * 128, :],
                        in_=g_t[:, :]),
                      waits=(f"resm{t}",), mark=f"dbggt{t}", dma=True,
                      sem="gdg")

            b2A(0)
            b2Amm(0)
            for t in range(NT_OWN):
                if t + 1 < NT_OWN:
                    b2A(t + 1)
                b2Batt(t)
                if t + 1 < NT_OWN:
                    b2Amm(t + 1)
                b2B(t)

            if debug:
                E("gp", lambda e: e.dma_start(out=dbg["dbg_st2"][:, :],
                                              in_=statet2[:, :]),
                  waits=tuple(f"outst{t}" for t in range(NT_OWN)),
                  mark="dbgst2", dma=True, sem="gpd")
                E("ve", fn=None, waits=("dbgst2",))
            E("ve", fn=None, waits=tuple(f"outst{t}" for t in range(NT_OWN)))
            E("ve", fn=None, waits=(f"qtst{NT_OWN-1}", "wtst31",
                                    f"x2st{NT_OWN-1}"))

        em.serial = SERIAL
        em.mode = "count"
        em.auto = 0
        em.auto_s = 0
        em.prev_mark = None
        prog()
        em.mode = "emit"
        em.sems = _sems

        def mk(eng):
            def body(e):
                em.cur_eng = eng
                em.handles = e
                em.auto = 0
                em.auto_s = 0
                em.prev_mark = None
                prog()
            return body
        blk.sync(mk("sy"))
        blk.gpsimd(mk("gp"))
        blk.scalar(mk("sc"))
        blk.vector(mk("ve"))
        blk.tensor(mk("te"))

    return nc


def kernel(L0_attn_x, L0_attn_Wq, L0_attn_bq, L0_attn_Wk, L0_attn_bk,
           L0_attn_Wv, L0_attn_bv, L0_attn_Wo, L0_attn_bo, L0_attn_ln_g,
           L0_attn_ln_b, L0_ffn_norm1_w, L0_ffn_w1, L0_ffn_act_scale,
           L0_ffn_act_bias, L0_ffn_norm2_w, L0_ffn_w2, _debug=False,
           _build_only=False, _detect_races=False):
    x = np.asarray(L0_attn_x, dtype=np.float32)
    assert x.shape == (2, NTOK, D)

    def f32c(a, shape=None):
        a = np.asarray(a, dtype=np.float32)
        if shape is not None:
            a = a.reshape(shape)
        return np.ascontiguousarray(a)

    common = {
        "wq": f32c(L0_attn_Wq), "wk": f32c(L0_attn_Wk), "wv": f32c(L0_attn_Wv),
        "wo": f32c(L0_attn_Wo), "bq": f32c(L0_attn_bq, (1, D)),
        "bk": f32c(L0_attn_bk), "bv": f32c(L0_attn_bv),
        "bo": f32c(L0_attn_bo, (1, D)), "lng": f32c(L0_attn_ln_g, (1, D)),
        "lnb": f32c(L0_attn_ln_b, (1, D)), "nw1": f32c(L0_ffn_norm1_w, (1, D)),
        "nw2": f32c(L0_ffn_norm2_w, (1, HID)),
        "acts": f32c(L0_ffn_act_scale, (1, 1)),
        "actb": f32c(L0_ffn_act_bias, (1, 1)),
        "w1t": np.ascontiguousarray(np.asarray(L0_ffn_w1, dtype=np.float32).T),
        "w2t": np.ascontiguousarray(np.asarray(L0_ffn_w2, dtype=np.float32).T),
    }
    in_maps = []
    for core in range(8):
        b, slot = divmod(core, 4)
        xb = x[b]
        xrot = np.ascontiguousarray(
            np.concatenate([xb[slot * OWN:(slot + 1) * OWN],
                            xb[:slot * OWN], xb[(slot + 1) * OWN:]], axis=0))
        m = dict(common)
        m["x"] = xrot
        in_maps.append(m)

    nc = build(debug=_debug, detect_races=_detect_races)
    if _build_only:
        return nc, in_maps
    res = run_bass_kernel_spmd(nc, in_maps, core_ids=list(range(8)))
    out = np.empty((2, NTOK, D), dtype=np.float32)
    for core in range(8):
        b, slot = divmod(core, 4)
        out[b, slot * OWN:(slot + 1) * OWN] = np.asarray(res.results[core]["out"])
    return out
